# version 06
# baseline (speedup 1.0000x reference)
import contextlib
import numpy as np
import ml_dtypes
import concourse.bass as bass
import concourse.mybir as mybir
from concourse.bass_utils import run_bass_kernel_spmd
from concourse.alu_op_type import AluOpType as OP

F32 = mybir.dt.float32
BF16 = mybir.dt.bfloat16
AF = mybir.ActivationFunctionType
AX = mybir.AxisListType

B, T, D = 2, 8192, 1024
NCORE = 8
NTOK = 2048
HALO = 128
NROW = HALO + NTOK
NRC = NROW // 128
NH = 8
DFF = 4096
DHID, DINNER = 2048, 1024
NEXP, DI = 16, 512
CHUNK = 64
NCHUNK = NTOK // CHUNK
NQT = NTOK // 128
NT5 = (NROW + 511) // 512

_TAPS = ()
PHASES = 99

ENGS = ("tensor", "vector", "scalar", "gpsimd", "sync")
KB = 512


def tile_w(tt):
    return min(512, NROW - tt * 512)


class Phys:
    __slots__ = ("hh", "tot")
    def __init__(self, hh):
        self.hh, self.tot = hh, 0


class Sem:
    __slots__ = ("ph", "base", "v")
    def __init__(self, ph):
        self.ph, self.base, self.v = ph, ph.tot, 0
    @property
    def h(self):
        return self.ph.hh
    def inc(self, inst, n=1):
        inst.then_inc(self.ph.hh, n)
        self.v += n
        self.ph.tot += n
        return inst
    def wge(self, eng, val):
        eng.wait_ge(self.ph.hh, self.base + val)
    def wait(self, eng, v=None):
        eng.wait_ge(self.ph.hh, self.base + (self.v if v is None else v))


class Ctx:
    def __init__(self, nc, stack, arena):
        self.nc, self.stack, self.arena = nc, stack, arena
        self.sems = {}
        self.blk = None
        self.phase = 0
        self.taps = []
        self._avail = []
        self._scope = None
        self.bsem = {e: self.sem("bar_" + e) for e in ENGS}
        self.dsem = {e: self.sem("dma_" + e) for e in ENGS}

    def sem(self, name, pin=False):
        if name not in self.sems:
            if pin or self._scope is None or not self._avail:
                ph = Phys(self.stack.enter_context(self.nc.semaphore(name)))
            else:
                ph = self._avail.pop()
            s = Sem(ph)
            if self._scope is not None and not pin:
                self._scope.append(name)
            self.sems[name] = s
        return self.sems[name]

    def scope_begin(self):
        assert self._scope is None
        self._scope = []

    def scope_end(self):
        for name in self._scope:
            self._avail.append(self.sems.pop(name).ph)
        self._scope = None

    def at(self, kb_off, n_elem, dtype=F32):
        w = 2 if dtype == F32 else 1
        c0 = kb_off * 512
        ap = self.arena[:, c0:c0 + n_elem * w]
        return ap.bitcast(F32) if dtype == F32 else ap

    def dma(self, eng, ename, out, in_):
        return self.dsem[ename].inc(eng.dma_start(out=out, in_=in_), 16)

    def fence(self, eng, ename, sem, n=1):
        c0 = SM_KB * 512 + 1792 + 32 * ENGS.index(ename)
        d = self.arena[:, c0:c0 + 8]
        d2 = self.arena[:, c0 + 8:c0 + 16]
        if ename in ("vector", "gpsimd"):
            eng.drain()
            return sem.inc(eng.memset(d, 0.0), n)
        if ename == "scalar":
            if not getattr(self, "_sc_primed", False):
                eng.activation(out=d, in_=self.at(XF_KB, 8)[:, 0:8], func=AF.Relu)
                self._sc_primed = True
            eng.drain()
            return sem.inc(eng.activation(out=d2, in_=d, func=AF.Relu), n)
        return sem.inc(eng.nop(), n)

    def barrier(self):
        self.phase += 1
        ph = self.phase
        for e in ENGS:
            def mk(ename):
                def _(eng):
                    self.dsem[ename].wait(eng)
                    self.fence(eng, ename, self.bsem[ename])
                    for o in ENGS:
                        if o != ename:
                            eng.wait_ge(self.bsem[o].h, ph)
                return _
            getattr(self.blk, e)(mk(e))

    def tap(self, name, sbuf_ap):
        if name not in _TAPS:
            return
        t = self.nc.dram_tensor("tap_" + name, list(sbuf_ap.shape), sbuf_ap.dtype,
                                kind="ExternalOutput").ap()
        self.taps.append((t, sbuf_ap))

    def flush_taps(self):
        if not self.taps:
            return
        taps, self.taps = list(self.taps), []
        @self.blk.gpsimd
        def _(g):
            for t, s in taps:
                self.dma(g, "gpsimd", t, s)
            self.dsem["gpsimd"].wait(g)


XF_KB, SM_KB, R1_KB, R2_KB = 0, 68, 72, 107
ARENA_KB = 206


def xf_ap(ctx, dc, r0, w):
    base = XF_KB * 256
    return ctx.at(XF_KB, 8 * NROW)[:, dc * NROW + r0: dc * NROW + r0 + w]


def build_program():
    nc = bass.Bass("TRN2", target_bir_lowering=False, debug=False)
    stack = contextlib.ExitStack()
    arena = stack.enter_context(nc.sbuf_tensor("arena", [128, ARENA_KB * 512], BF16))
    ctx = Ctx(nc, stack, arena)
    for c, val in enumerate(CONSTS):
        nc.const_aps.register(F32, val, ctx.at(SM_KB, 1024)[:, 250 + c:251 + c])

    dram = {}
    def din(name, shape, dtype=F32):
        dram[name] = nc.dram_tensor(name, list(shape), dtype, kind="ExternalInput").ap()
    def dint(name, shape, dtype=F32, space="Local"):
        dram[name] = nc.dram_tensor(name, list(shape), dtype, kind="Internal", addr_space=space).ap()

    din("x_fm", [D, NROW])
    din("eqT", [D, NH * 128])
    din("evT", [D, NH * 128])
    din("mn_w1", [D, 512]); din("mn_b1", [128, 4])
    din("mn_w2", [512, D]); din("mn_b2", [128, 8])
    din("base_m", [128, 8])
    din("decT", [D, D])
    din("hebUT", [NH * 32, 128]); din("hebV", [NH * 32, 128])
    din("f0w1T", [D, DFF]); din("f0w2T", [DFF, D])
    din("f0b1", [128, 32]); din("f0b2", [128, 8])
    din("w_in", [D, DHID]); din("b_in", [128, 16])
    din("wgx", [NB8 := 8 * 256, 128]); din("bgx", [128, 8])
    din("wga", [NB8, 128]); din("bga", [128, 8])
    din("arp", [128, 8]); din("aim", [128, 8])
    din("w_out", [DHID, D]); din("b_out", [128, 8])
    din("gate_w", [D, 16]); din("gate_bt", [128, 16])
    din("sh_wg", [D, DI]); din("sh_wu", [D, DI]); din("sh_wd", [DI, D])
    din("ex_wg", [NEXP * D, DI]); din("ex_wu", [NEXP * D, DI]); din("ex_wd", [NEXP * DI, D])
    din("bmt", [128, 8])
    din("rmt", [128, 8])

    dram["y_out"] = nc.dram_tensor("y_out", [D, NTOK], F32, kind="ExternalOutput").ap()

    dint("g_hbm", [NH * (NQT + 1) * 128 * 257])
    dint("s_hbm", [NH, NROW + 192])
    dint("w2_hbm", [8 * 128 * 256], BF16)
    dint("wq1_hbm", [D * DFF], BF16)
    dint("wq2_hbm", [DFF * D], BF16)
    dint("cc_in1", [1, 2048]); dint("cc_out1", [8, 2048], space="Shared")
    dint("cc_in2", [1, 4096]); dint("cc_out2", [8, 4096], space="Shared")
    dint("misc_hbm", [16384])
    dint("spill_hbm", [4 * 32 * 128, 512], BF16)
    dint("misc2_hbm", [16384], BF16)
    dint("gp_hbm", [128, 2])
    dint("wmoe_hbm", [16, 2048], BF16)

    PS = dict(
        A=stack.enter_context(nc.psum_tensor("psA", [128, 1024], F32)),
        B=stack.enter_context(nc.psum_tensor("psB", [128, 1024], F32)),
        C=stack.enter_context(nc.psum_tensor("psC", [128, 1024], F32)),
        D=stack.enter_context(nc.psum_tensor("psD", [128, 1024], F32)),
    )

    with nc.Block() as blk:
        ctx.blk = blk
        emit(ctx, dram, PS)
    stack.close()
    return nc


NB = 8



def emit_ln(ctx, PS, st, pfx, XN):
    nc, blk = ctx.nc, ctx.blk
    onesb_col, onesb_row = st["onesb_col"], st["onesb_row"]
    XB = ctx.at(R1_KB, 8 * 512, BF16)
    SQ = ctx.at(R1_KB + 8, 8 * 512, BF16)
    SRO_ = ctx.at(R1_KB + 16, 6 * 512)[0:1, :]
    SRO = [SRO_[:, i * 512:(i + 1) * 512] for i in range(6)]
    ABH_ = ctx.at(R1_KB + 28, 4 * 512, BF16)[0:1, :]
    ABH = [ABH_[:, i * 512:(i + 1) * 512] for i in range(4)]
    TMP = ctx.at(R1_KB + 32, 512)
    NT = NTOK // 512

    s_pre = ctx.sem(pfx + "_pre")
    s_mm = ctx.sem(pfx + "_mm")
    s_ab = ctx.sem(pfx + "_ab")
    s_bc = ctx.sem(pfx + "_bc")
    s_ap = ctx.sem(pfx + "_ap")

    def bank_of(tt):
        return PS["A"] if tt % 2 == 0 else PS["B"]

    def pcd(tt):
        pa = PS["C"][:, 0:512] if tt % 2 == 0 else PS["C"][:, 512:1024]
        pb = PS["D"][:, 0:512] if tt % 2 == 0 else PS["D"][:, 512:1024]
        return pa, pb

    @blk.vector
    def _(v):
        for tt in range(NT + 1):
            if tt >= 1:
                j = tt - 1
                ctx.sems[pfx + "_mm"].wge(v, j + 1)
                bank = bank_of(j)
                v.drain()
                v.tensor_copy(SRO[0], bank[0:1, 0:512])
                v.drain()
                v.tensor_copy(SRO[1], bank[0:1, 512:1024])
                v.drain()
                v.tensor_scalar(out=SRO[0], in0=SRO[0], scalar1=1.0 / D, scalar2=None, op0=OP.mult)
                v.tensor_scalar(out=SRO[1], in0=SRO[1], scalar1=1.0 / D, scalar2=None, op0=OP.mult)
                v.drain()
                v.tensor_tensor(out=SRO[2], in0=SRO[0], in1=SRO[0], op=OP.mult)
                v.drain()
                v.tensor_tensor(out=SRO[1], in0=SRO[1], in1=SRO[2], op=OP.subtract)
                v.drain()
                v.tensor_scalar(out=SRO[1], in0=SRO[1], scalar1=1e-5, scalar2=None, op0=OP.add)
                ctx.fence(v, "vector", s_ab)
                ctx.sems[pfx + "_ab"].wge(v, 3 * j + 2)
                v.reciprocal(SRO[3], SRO[2])
                v.drain()
                v.tensor_tensor(out=SRO[4], in0=SRO[0], in1=SRO[3], op=OP.mult)
                v.drain()
                v.tensor_scalar(out=SRO[4], in0=SRO[4], scalar1=-1.0, scalar2=None, op0=OP.mult)
                v.tensor_copy(ABH[0], SRO[3])
                v.drain()
                v.tensor_copy(ABH[2], SRO[4])
                v.drain()
                v.tensor_copy(SRO[5], ABH[0])
                v.drain()
                v.tensor_tensor(out=SRO[5], in0=SRO[3], in1=SRO[5], op=OP.subtract)
                v.drain()
                v.tensor_copy(ABH[1], SRO[5])
                v.drain()
                v.tensor_copy(SRO[5], ABH[2])
                v.drain()
                v.tensor_tensor(out=SRO[5], in0=SRO[4], in1=SRO[5], op=OP.subtract)
                v.drain()
                v.tensor_copy(ABH[3], SRO[5])
                ctx.fence(v, "vector", s_ab)
                ctx.sems[pfx + "_bc"].wge(v, j + 1)
                pa, pb = pcd(j)
                o = j * 512
                for dc in range(8):
                    v.tensor_tensor(out=TMP, in0=xf_ap(ctx, dc, HALO + o, 512), in1=pa, op=OP.mult)
                    v.drain()
                    s_ap.inc(v.tensor_tensor(out=XN[:, dc * NTOK + o:dc * NTOK + o + 512],
                                             in0=TMP, in1=pb, op=OP.add))
                    v.drain()
            if tt < NT:
                o = tt * 512
                if tt >= 1:
                    ctx.sems[pfx + "_mm"].wge(v, tt)
                for dc in range(8):
                    v.tensor_copy(XB[:, dc * 512:dc * 512 + 512], xf_ap(ctx, dc, HALO + o, 512))
                ctx.fence(v, "vector", s_pre)

    @blk.scalar
    def _(s):
        act_preload(s, "sqrt_and_others")
        for tt in range(NT + 1):
            if tt >= 1:
                j = tt - 1
                ctx.sems[pfx + "_ab"].wge(s, 3 * j + 1)
                s.drain()
                s.activation(out=SRO[2], in_=SRO[1], func=AF.Sqrt)
                ctx.fence(s, "scalar", s_ab)
            if tt < NT:
                o = tt * 512
                if tt >= 1:
                    ctx.sems[pfx + "_mm"].wge(s, tt)
                for dc in range(8):
                    s.activation(out=SQ[:, dc * 512:dc * 512 + 512], in_=xf_ap(ctx, dc, HALO + o, 512), func=AF.Square)
                ctx.fence(s, "scalar", s_pre)

    @blk.tensor
    def _(te):
        for tt in range(NT + 1):
            if tt >= 1:
                j = tt - 1
                ctx.sems[pfx + "_ab"].wge(te, 3 * j + 3)
                if j >= 2:
                    ctx.sems[pfx + "_ap"].wge(te, 8 * (j - 1))
                pa, pb = pcd(j)
                nc.tensor.matmul(pa, onesb_row, ABH[0], start=True, stop=False)
                nc.tensor.matmul(pa, onesb_row, ABH[1], start=False, stop=True)
                nc.tensor.matmul(pb, onesb_row, ABH[2], start=True, stop=False)
                s_bc.inc(nc.tensor.matmul(pb, onesb_row, ABH[3], start=False, stop=True))
            if tt < NT:
                ctx.sems[pfx + "_pre"].wge(te, 2 * (tt + 1))
                if tt >= 2:
                    ctx.sems[pfx + "_ab"].wge(te, 3 * (tt - 2) + 1)
                bank = bank_of(tt)
                for dc in range(8):
                    nc.tensor.matmul(bank[0:1, 0:512], onesb_col, XB[:, dc * 512:dc * 512 + 512],
                                     start=(dc == 0), stop=(dc == 7))
                for dc in range(8):
                    last = nc.tensor.matmul(bank[0:1, 512:1024], onesb_col, SQ[:, dc * 512:dc * 512 + 512],
                                            start=(dc == 0), stop=(dc == 7))
                s_mm.inc(last)


def phaseC(ctx, dram, PS, st):
    nc, blk = ctx.nc, ctx.blk
    SM = st["SM"]
    onesb_col, onesb_row = st["onesb_col"], st["onesb_row"]

    WQ1 = ctx.at(R2_KB, 8 * 4096, BF16)
    XN2 = ctx.at(R2_KB + 64, 8 * NTOK, BF16)
    WSTR = [ctx.at(R1_KB + 8 * i, 2048) for i in range(2)]
    ABSD = [ctx.at(R1_KB + 16 + 8 * b, 2048) for b in range(2)]
    ABSF = ctx.at(R1_KB + 16, 2048)
    GP = ctx.at(SM_KB, 1024)[:, 344:392]
    CR = ctx.at(R2_KB + 96, 768)[0:1, :]
    SGNB = ctx.at(R1_KB + 24, 2048, BF16)
    MSKB = ctx.at(R1_KB + 28, 2048, BF16)
    QOUT = ctx.at(R1_KB + 32, 1024, BF16)
    HALL = ctx.at(R1_KB, 32 * 256, BF16)
    W2D = [ctx.at(R1_KB + 16 + 8 * i, 4096, BF16) for i in range(2)]
    GB = ctx.at(SM_KB + 2, 256, BF16)
    b1t = SM[:, 200:232]
    b2t = SM[:, 232:240]
    gsc = SM[:, 240:248]

    s_w = ctx.sem("F_w")
    s_gp = ctx.sem("F_gp")
    s_gam = ctx.sem("F_gam")
    s_qv = ctx.sem("F_qv")
    s_qs = ctx.sem("F_qs")
    s_qst = ctx.sem("F_qst", pin=True)
    s_h = ctx.sem("F_h")
    s_he = ctx.sem("F_he")
    s_o = ctx.sem("F_o")
    s_oe = ctx.sem("F_oe")
    s_w2 = ctx.sem("F_w2")

    w1t = dram["f0w1T"]
    w2t = dram["f0w2T"]
    wq2flat = dram["wq2_hbm"]

    chunks = []
    for dc in range(8):
        for half in range(2):
            chunks.append(("w1", dc, half, 2048,
                           bass.AP(w1t.tensor, dc * 128 * 4096 + half * 2048, [[4096, 128], [1, 2048]])))
    for i in range(32):
        chunks.append(("w2", i, 0, 1024,
                       bass.AP(w2t.tensor, i * 128 * 1024, [[1024, 128], [1, 1024]])))
    NCH = len(chunks)

    @blk.gpsimd
    def _(g):
        ctx.dma(g, "gpsimd", b1t, dram["f0b1"])
        ctx.dma(g, "gpsimd", b2t, dram["f0b2"])
        g.wait_ge(ctx.dsem["gpsimd"].h, ctx.dsem["gpsimd"].v)

    @blk.sync
    def _(e):
        for i, (kind, a, b, w, ap) in enumerate(chunks):
            if i >= 2:
                ctx.sems["F_qv"].wge(e, i - 1)
            ctx.dma(e, "sync", WSTR[i % 2][:, 0:w], ap)
            e.wait_ge(ctx.dsem["sync"].h, ctx.dsem["sync"].v)
            ctx.fence(e, "sync", s_w)

    @blk.scalar
    def _(s):
        for i, (kind, a, b, w, ap) in enumerate(chunks):
            ctx.sems["F_w"].wge(s, i + 1)
            if i >= 2:
                ctx.sems["F_gp"].wge(s, i - 1)
            s.drain()
            s_qv.inc(s.activation(out=ABSD[i % 2][:, 0:w], in_=WSTR[i % 2][:, 0:w], func=AF.Abs))

    @blk.vector
    def _(v):
        for i, (kind, a, b, w, ap) in enumerate(chunks):
            ctx.sems["F_qv"].wge(v, i + 1)
            v.drain()
            s_gp.inc(v.tensor_reduce(GP[:, i:i + 1], ABSD[i % 2][:, 0:w], axis=AX.X, op=OP.add))
        v.drain()
        v.tensor_reduce(gsc[:, 6:7], GP[:, 0:16], axis=AX.X, op=OP.add)
        v.drain()
        v.tensor_reduce(gsc[:, 7:8], GP[:, 16:NCH], axis=AX.X, op=OP.add)
        ctx.fence(v, "vector", s_gam)

    @blk.sync
    def _(e):
        ctx.sems["F_gam"].wge(e, 1)
        ctx.dma(e, "sync", dram["gp_hbm"], gsc[:, 6:8])
        e.wait_ge(ctx.dsem["sync"].h, ctx.dsem["sync"].v)
        ctx.dma(e, "sync", CR[:, 0:256], dram["gp_hbm"].rearrange("p q -> (p q)").rearrange("(a b) -> a b", a=1))
        e.wait_ge(ctx.dsem["sync"].h, ctx.dsem["sync"].v)
        ctx.fence(e, "sync", s_gam)

    @blk.vector
    def _(v):
        T, S1, S2, CC = CR[:, 0:256], CR[:, 256:384], CR[:, 384:512], CR[:, 512:640]
        ctx.sems["F_gam"].wge(v, 2)
        v.drain()
        v.tensor_copy(S1.rearrange("a (p q) -> a p q", q=1),
                      T.rearrange("a (p q) -> a p q", q=2)[:, :, 0:1])
        v.drain()
        v.tensor_copy(S2.rearrange("a (p q) -> a p q", q=1),
                      T.rearrange("a (p q) -> a p q", q=2)[:, :, 1:2])
        for t, S in ((0, S1), (1, S2)):
            v.drain()
            v.memset(CC, 0.0)
            m = 128
            while m > 1:
                h = m // 2
                a, b2 = S[:, 0:h], S[:, h:m]
                v.drain()
                v.tensor_tensor(out=T[:, 0:h], in0=a, in1=b2, op=OP.add)
                v.drain()
                v.tensor_tensor(out=T[:, h:2 * h], in0=T[:, 0:h], in1=a, op=OP.subtract)
                v.drain()
                v.tensor_tensor(out=T[:, 2 * h:3 * h], in0=T[:, 0:h], in1=T[:, h:2 * h], op=OP.subtract)
                v.drain()
                v.tensor_tensor(out=T[:, 2 * h:3 * h], in0=a, in1=T[:, 2 * h:3 * h], op=OP.subtract)
                v.drain()
                v.tensor_tensor(out=T[:, 3 * h:4 * h], in0=b2, in1=T[:, h:2 * h], op=OP.subtract)
                v.drain()
                v.tensor_tensor(out=T[:, 2 * h:3 * h], in0=T[:, 2 * h:3 * h], in1=T[:, 3 * h:4 * h], op=OP.add)
                v.drain()
                v.tensor_tensor(out=CC[:, 0:h], in0=CC[:, 0:h], in1=CC[:, h:m], op=OP.add)
                v.drain()
                v.tensor_tensor(out=CC[:, 0:h], in0=CC[:, 0:h], in1=T[:, 2 * h:3 * h], op=OP.add)
                v.drain()
                v.tensor_copy(S[:, 0:h], T[:, 0:h])
                m = h
            v.drain()
            v.tensor_tensor(out=S[:, 0:1], in0=S[:, 0:1], in1=CC[:, 0:1], op=OP.add)
            v.drain()
            v.tensor_scalar(out=S[:, 1:2], in0=S[:, 0:1], scalar1=2.0 ** -22, scalar2=1e-5,
                            op0=OP.mult, op1=OP.add)
            v.drain()
            v.tensor_scalar(out=gsc[0:1, 2 + t:3 + t], in0=S[:, 1:2], scalar1=0.5, scalar2=None, op0=OP.mult)
        v.drain()
        v.tensor_copy(GB[0:1, 144:146], gsc[0:1, 2:4])
        v.drain()
        v.tensor_copy(gsc[0:1, 6:8], GB[0:1, 144:146])
        v.drain()
        v.tensor_tensor(out=gsc[0:1, 6:8], in0=gsc[0:1, 2:4], in1=gsc[0:1, 6:8], op=OP.subtract)
        v.drain()
        v.tensor_copy(GB[0:1, 146:148], gsc[0:1, 6:8])
        v.drain()
        v.tensor_copy(CR[:, 640:642], GB[0:1, 146:148])
        v.drain()
        v.tensor_tensor(out=gsc[0:1, 6:8], in0=gsc[0:1, 6:8], in1=CR[:, 640:642], op=OP.subtract)
        v.drain()
        v.tensor_copy(GB[0:1, 148:150], gsc[0:1, 6:8])
        ctx.fence(v, "vector", s_gam)

    @blk.tensor
    def _(te):
        ctx.sems["F_gam"].wge(te, 3)
        for t in range(2):
            nc.tensor.matmul(PS["B"][:, t:t + 1], onesb_row, GB[0:1, 144 + t:145 + t], start=True, stop=False)
            nc.tensor.matmul(PS["B"][:, t:t + 1], onesb_row, GB[0:1, 146 + t:147 + t], start=False, stop=False)
            last = nc.tensor.matmul(PS["B"][:, t:t + 1], onesb_row, GB[0:1, 148 + t:149 + t], start=False, stop=True)
        s_gam.inc(last)

    @blk.vector
    def _(v):
        ctx.sems["F_gam"].wge(v, 4)
        v.tensor_copy(gsc[:, 4:6], PS["B"][:, 0:2])
        ctx.fence(v, "vector", s_gam)

    @blk.sync
    def _(e):
        for i in range(NCH):
            kind, a, b, w, ap = chunks[i]
            ctx.sems["F_qv"].wge(e, NCH + i)
            ctx.dma(e, "sync", WSTR[i % 2][:, 0:w], ap)
            e.wait_ge(ctx.dsem["sync"].h, ctx.dsem["sync"].v)
            ctx.fence(e, "sync", s_w)

    @blk.scalar
    def _(s):
        ctx.sems["F_gam"].wge(s, 5)
        for i in range(NCH):
            kind, a, b, w, ap = chunks[i]
            ctx.sems["F_w"].wge(s, NCH + i + 1)
            if i >= 1:
                ctx.sems["F_qv"].wge(s, NCH + i)
            s.drain()
            s_qs.inc(s.activation(out=SGNB[:, 0:w], in_=WSTR[i % 2][:, 0:w], func=AF.Sign))

    @blk.vector
    def _(v):
        for i in range(NCH):
            kind, a, b, w, ap = chunks[i]
            ctx.sems["F_qs"].wge(v, i + 1)
            thr = gsc[:, 4:5] if kind == "w1" else gsc[:, 5:6]
            v.drain()
            v.tensor_scalar(out=ABSF[:, 0:w], in0=WSTR[i % 2][:, 0:w], scalar1=-1.0, scalar2=None, op0=OP.mult)
            v.drain()
            v.tensor_tensor(out=ABSF[:, 0:w], in0=ABSF[:, 0:w], in1=WSTR[i % 2][:, 0:w], op=OP.max)
            v.drain()
            v.tensor_scalar(out=MSKB[:, 0:w], in0=ABSF[:, 0:w], scalar1=thr, scalar2=None, op0=OP.is_ge)
            v.drain()
            if kind == "w1":
                dst = WQ1[:, a * 4096 + b * 2048: a * 4096 + b * 2048 + 2048]
                s_qv.inc(v.tensor_tensor(out=dst, in0=SGNB[:, 0:w], in1=MSKB[:, 0:w], op=OP.mult))
            else:
                ctx.sems["F_qst"].wge(v, 16 * a)
                s_qv.inc(v.tensor_tensor(out=QOUT[:, 0:w], in0=SGNB[:, 0:w], in1=MSKB[:, 0:w], op=OP.mult))
            v.drain()

    @blk.gpsimd
    def _(g):
        for i in range(32):
            ctx.sems["F_qv"].wge(g, NCH + 16 + i + 1)
            ctx.sems["F_qst"].wge(g, 16 * i)
            dst = bass.AP(wq2flat.tensor, i * 128, [[4096, 128], [524288, 8], [1, 128]])
            s_qst.inc(g.dma_start(out=dst, in_=QOUT[:, 0:1024].rearrange("p (a b) -> p a b", a=8)), 16)
        ctx.sems["F_qst"].wge(g, 16 * 32)

    ctx.tap("wq1", WQ1)
    ctx.barrier()
    if PHASES == 35:
        return

    emit_ln(ctx, PS, st, "LN2", XN2)
    ctx.barrier()

    ctx.tap("xn2", XN2)
    NTT = NTOK // 256
    @blk.sync
    def _(e):
        for tt in range(NTT):
            for dout in range(8):
                k = tt * 8 + dout
                if k >= 2:
                    ctx.sems["F_o"].wge(e, k - 1)
                ctx.dma(e, "sync", W2D[k % 2], bass.AP(wq2flat.tensor, dout * 524288, [[4096, 128], [1, 4096]]))
                e.wait_ge(ctx.dsem["sync"].h, ctx.dsem["sync"].v)
                ctx.fence(e, "sync", s_w2)

    @blk.tensor
    def _(te):
        for tt in range(NTT):
            o = tt * 256
            for f in range(32):
                k = tt * 32 + f
                hb = PS["C"][:, 0:256] if f % 2 == 0 else PS["C"][:, 512:768]
                if k >= 2:
                    ctx.sems["F_he"].wge(te, k - 1)
                for dc in range(8):
                    last = nc.tensor.matmul(hb, WQ1[:, dc * 4096 + f * 128:dc * 4096 + f * 128 + 128],
                                            XN2[:, dc * NTOK + o:dc * NTOK + o + 256],
                                            start=(dc == 0), stop=(dc == 7))
                s_h.inc(last)
            ctx.sems["F_he"].wge(te, (tt + 1) * 32)
            for dout in range(8):
                kd = tt * 8 + dout
                ctx.sems["F_w2"].wge(te, kd + 1)
                if kd >= 2:
                    ctx.sems["F_oe"].wge(te, kd - 1)
                ob = PS["A"][:, 0:256] if dout % 2 == 0 else PS["A"][:, 512:768]
                for f in range(32):
                    last = nc.tensor.matmul(ob, W2D[kd % 2][:, f * 128:f * 128 + 128],
                                            HALL[:, f * 256:f * 256 + 256],
                                            start=(f == 0), stop=(f == 31))
                s_o.inc(last)

    @blk.scalar
    def _(s):
        for tt in range(NTT):
            if tt >= 1:
                ctx.sems["F_o"].wge(s, tt * 8)
            for f in range(32):
                k = tt * 32 + f
                ctx.sems["F_h"].wge(s, k + 1)
                hb = PS["C"][:, 0:256] if f % 2 == 0 else PS["C"][:, 512:768]
                s.drain()
                s_he.inc(s.activation(out=HALL[:, f * 256:f * 256 + 256], in_=hb,
                                      func=AF.Relu, bias=b1t[:, f:f + 1], scale=1.0))

    @blk.vector
    def _(v):
        for tt in range(NTT):
            o = tt * 256
            for dout in range(8):
                kd = tt * 8 + dout
                ctx.sems["F_o"].wge(v, kd + 1)
                ob = PS["A"][:, 0:256] if dout % 2 == 0 else PS["A"][:, 512:768]
                s_oe.inc(v.scalar_tensor_tensor(out=xf_ap(ctx, dout, HALO + o, 256), in0=ob,
                                                scalar=b2t[:, dout:dout + 1], in1=xf_ap(ctx, dout, HALO + o, 256),
                                                op0=OP.add, op1=OP.add))
                v.drain()




CONSTS = (1.5707963267948966, -44.0, 44.0)


def act_preload(s, setname):
    from concourse.hw_specs import get_activation_tables
    names = list(get_activation_tables(s.bass.m.arch).keys())
    return s.add_instruction(mybir.InstLoadActFuncSet(
        name=s.bass.get_next_instruction_name(),
        act_func_set_id=names.index(setname), ins=[], outs=[]))
EXPB = 44.0
EMB = 7.781132241133797e-20


def phaseD(ctx, dram, PS, st):
    nc, blk = ctx.nc, ctx.blk
    SM = st["SM"]

    XN3 = ctx.at(R2_KB, 8 * NTOK, BF16)
    XH = ctx.at(R2_KB + 32, 16 * 512, BF16)
    GX = ctx.at(R2_KB + 48, 8 * 512, BF16)
    GA = ctx.at(R2_KB + 56, 8 * 512)
    PL = [ctx.at(R2_KB + 72 + 2 * i, 512) for i in range(12)]
    HMB = ctx.at(R2_KB + 32, 16 * 512, BF16)
    RLB = [ctx.at(R2_KB + 48 + 4 * i, 2048, BF16) for i in range(2)]
    QALL = ctx.at(R1_KB, 1056)
    QSTr, QSTi = QALL[:, 0:264], QALL[:, 264:528]
    CLOr, CLOi = QALL[:, 528:792], QALL[:, 792:1056]
    MISC = ctx.at(R1_KB + 5, 368)
    T8 = MISC[:, 0:64]
    SUMT = MISC[:, 64:96]
    CIN = MISC[:, 96:112]
    TRT = [MISC[:, 112 + 32 * r:144 + 32 * r] for r in range(8)]
    OZ = ctx.at(R1_KB + 7, 66)
    ONES64, ZCOL = OZ[:, 0:64], OZ[:, 64:65]
    MSPT = ctx.at(R1_KB + 8, 512)
    AIMT = ctx.at(R1_KB + 10, 512)
    WGX = ctx.at(R1_KB + 12, 2048, BF16)
    WGA = ctx.at(R1_KB + 16, 2048, BF16)
    WINB = [ctx.at(R1_KB + 20 + 2 * i, 1024, BF16) for i in range(2)]
    SPB = [ctx.at(R1_KB + 24 + 4 * i, 2048, BF16) for i in range(2)]

    b_in = SM[:, 256:272]
    bgx = SM[:, 272:280]
    bga = SM[:, 280:288]
    msp = SM[:, 288:296]
    aim = SM[:, 296:304]
    b_out = SM[:, 304:312]
    smt = SM[:, 312:320]
    stmp = SM[:, 320:328]

    s_sml = ctx.sem("D_sml")
    s_dw = ctx.sem("D_dw")
    s_pre2 = ctx.sem("D_pre2")
    s_xh = ctx.sem("D_xh")
    s_xhe = ctx.sem("D_xhe")
    s_gmm = ctx.sem("D_gmm")
    s_ge = ctx.sem("D_ge")
    s_v1 = ctx.sem("D_v1")
    s_s1 = ctx.sem("D_s1")
    s_v2 = ctx.sem("D_v2")
    s_s2 = ctx.sem("D_s2")
    s_v3 = ctx.sem("D_v3")
    s_sp = ctx.sem("D_sp", pin=True)
    s_sum = ctx.sem("D_sum")
    s_cc = ctx.sem("D_cc", pin=True)
    s_cin = ctx.sem("D_cin")
    s_rl = ctx.sem("D_rl")
    s_h = ctx.sem("D_h")
    s_wo = ctx.sem("D_wo")
    s_om = ctx.sem("D_om")
    s_oe = ctx.sem("D_oe")

    @blk.gpsimd
    def _(g):
        for dst, key in ((b_in, "b_in"), (bgx, "bgx"), (bga, "bga"), (msp, "arp"),
                         (aim, "aim"), (b_out, "b_out"), (smt, "rmt")):
            ctx.dma(g, "gpsimd", dst, dram[key])
        g.wait_ge(ctx.dsem["gpsimd"].h, ctx.dsem["gpsimd"].v)
        ctx.fence(g, "gpsimd", s_sml)

    @blk.scalar
    def _(s):
        s_sml.wait(s)
        s.drain()
        act_preload(s, "natural_log_exp_and_others")
        s.activation(out=stmp, in_=msp, func=AF.Exp)
        s.drain()
        s.activation(out=msp, in_=stmp, func=AF.Ln, bias=1.0, scale=1.0)
        ctx.fence(s, "scalar", s_sml)

    @blk.vector
    def _(v):
        ctx.sems["D_sml"].wge(v, 2)
        v.drain()
        v.tensor_scalar(out=msp, in0=msp, scalar1=-8.0, scalar2=None, op0=OP.mult)
        ctx.fence(v, "vector", s_sml)

    emit_ln(ctx, PS, st, "LN3", XN3)
    ctx.barrier()

    @blk.gpsimd
    def _(g):
        ctx.dma(g, "gpsimd", WGX.rearrange("p (a b) -> p a b", a=16),
                dram["wgx"].rearrange("(a p) c -> p a c", p=128))
        ctx.dma(g, "gpsimd", WGA.rearrange("p (a b) -> p a b", a=16),
                dram["wga"].rearrange("(a p) c -> p a c", p=128))
        g.wait_ge(ctx.dsem["gpsimd"].h, ctx.dsem["gpsimd"].v)
        ctx.fence(g, "gpsimd", s_pre2)

    @blk.vector
    def _(v):
        v.memset(ONES64, 1.0)
        v.memset(ZCOL, 0.0)
        v.memset(QSTr[:, 0:8], 1.0)
        v.memset(QSTi[:, 0:8], 0.0)
        v.memset(CLOr[:, 0:8], 0.0)
        v.memset(CLOi[:, 0:8], 0.0)
        v.drain()
        for g in range(8):
            v.tensor_scalar(out=MSPT[:, g * 64:(g + 1) * 64], in0=ONES64,
                            scalar1=msp[:, g:g + 1], scalar2=None, op0=OP.mult)
            v.tensor_scalar(out=AIMT[:, g * 64:(g + 1) * 64], in0=ONES64,
                            scalar1=aim[:, g:g + 1], scalar2=None, op0=OP.mult)
        ctx.fence(v, "vector", s_pre2)
    ctx.barrier()

    w_in_r = dram["w_in"].rearrange("(a p) c -> p a c", p=128)

    @blk.gpsimd
    def _(g):
        for stt in range(4):
            for cc in range(16):
                k = stt * 16 + cc
                if k >= 2:
                    ctx.sems["D_xh"].wge(g, k - 1)
                ctx.dma(g, "gpsimd", WINB[k % 2].rearrange("p (a b) -> p a b", a=8),
                        w_in_r[:, :, cc * 128:(cc + 1) * 128])
                g.wait_ge(ctx.dsem["gpsimd"].h, ctx.dsem["gpsimd"].v)
                ctx.fence(g, "gpsimd", s_dw)
            for k in range(8):
                gk = stt * 8 + k
                ctx.sems["D_v3"].wge(g, gk + 1)
                sb = SPB[gk % 2]
                for q in range(4):
                    s_sp.inc(g.dma_start(
                        out=dram["spill_hbm"][(q * 32 + gk) * 128:(q * 32 + gk + 1) * 128, :],
                        in_=sb[:, q * 512:(q + 1) * 512]), 16)
                ctx.sems["D_sp"].wge(g, 64 * (gk + 1))
        ctx.sems["D_sum"].wge(g, 1)
        ctx.dma(g, "gpsimd", dram["cc_in2"].rearrange("a (p f) -> (a p) f", p=128), SUMT)
        g.wait_ge(ctx.dsem["gpsimd"].h, ctx.dsem["gpsimd"].v)
        s_cc.inc(g.collective_compute("AllGather", OP.bypass, replica_groups=[[0, 1, 2, 3, 4, 5, 6, 7]],
                                      ins=[dram["cc_in2"][:]], outs=[dram["cc_out2"][:]]))
        ctx.sems["D_cc"].wge(g, 1)
        for r in range(8):
            ctx.dma(g, "gpsimd", TRT[r], dram["cc_out2"][r:r + 1, :].rearrange("a (p f) -> (a p) f", p=128))
        g.wait_ge(ctx.dsem["gpsimd"].h, ctx.dsem["gpsimd"].v)
        ctx.fence(g, "gpsimd", s_cc)

    @blk.tensor
    def _(te):
        for stt in range(4):
            for cc in range(16):
                k = stt * 16 + cc
                ctx.sems["D_dw"].wge(te, k + 1)
                if k >= 2:
                    ctx.sems["D_xhe"].wge(te, k - 1)
                bank = PS["A"][:, 0:512] if k % 2 == 0 else PS["A"][:, 512:1024]
                for dc in range(8):
                    last = nc.tensor.matmul(bank, WINB[k % 2][:, dc * 128:dc * 128 + 128],
                                            XN3[:, dc * NTOK + stt * 512:dc * NTOK + stt * 512 + 512],
                                            start=(dc == 0), stop=(dc == 7))
                s_xh.inc(last)
            ctx.sems["D_xhe"].wge(te, stt * 16 + 16)
            for j in range(8):
                kj = stt * 8 + j
                if kj >= 2:
                    ctx.sems["D_ge"].wge(te, 2 * kj - 2)
                bx = PS["C"][:, 0:512] if j % 2 == 0 else PS["C"][:, 512:1024]
                ba = PS["D"][:, 0:512] if j % 2 == 0 else PS["D"][:, 512:1024]
                for bc in range(2):
                    nc.tensor.matmul(bx, WGX[:, (j * 2 + bc) * 128:(j * 2 + bc) * 128 + 128],
                                     XH[:, (j * 2 + bc) * 512:(j * 2 + bc) * 512 + 512],
                                     start=(bc == 0), stop=(bc == 1))
                for bc in range(2):
                    last = nc.tensor.matmul(ba, WGA[:, (j * 2 + bc) * 128:(j * 2 + bc) * 128 + 128],
                                            XH[:, (j * 2 + bc) * 512:(j * 2 + bc) * 512 + 512],
                                            start=(bc == 0), stop=(bc == 1))
                s_gmm.inc(last)
            if stt < 3:
                ctx.sems["D_v3"].wge(te, 8 * (stt + 1))

    @blk.scalar
    def _(s):
        for stt in range(4):
            for cc in range(16):
                k = stt * 16 + cc
                ctx.sems["D_xh"].wge(s, k + 1)
                bank = PS["A"][:, 0:512] if k % 2 == 0 else PS["A"][:, 512:1024]
                s.drain()
                s_xhe.inc(s.activation(out=XH[:, cc * 512:(cc + 1) * 512], in_=bank,
                                       func=AF.Identity, bias=b_in[:, cc:cc + 1], scale=1.0))
            act_preload(s, "sigmoid_and_others")
            for j in range(8):
                kj = stt * 8 + j
                ctx.sems["D_gmm"].wge(s, kj + 1)
                bx = PS["C"][:, 0:512] if j % 2 == 0 else PS["C"][:, 512:1024]
                ba = PS["D"][:, 0:512] if j % 2 == 0 else PS["D"][:, 512:1024]
                s.drain()
                s_ge.inc(s.activation(out=GX[:, j * 512:(j + 1) * 512], in_=bx,
                                      func=AF.Sigmoid, bias=bgx[:, j:j + 1], scale=1.0))
                s.drain()
                s_ge.inc(s.activation(out=GA[:, j * 512:(j + 1) * 512], in_=ba,
                                      func=AF.Sigmoid, bias=bga[:, j:j + 1], scale=1.0))
            for k in range(8):
                gk = stt * 8 + k
                ctx.sems["D_v1"].wge(s, gk + 1)
                s.drain()
                act_preload(s, "silu_and_others")
                s.activation(out=PL[7], in_=PL[4], func=AF.Sin)
                s.drain()
                s.activation(out=PL[8], in_=PL[5], func=AF.Sin, scale=-1.0, bias=1.5707963267948966)
                s.drain()
                act_preload(s, "natural_log_exp_and_others")
                s.activation(out=PL[6], in_=PL[0], func=AF.Exp, scale=2.0)
                s.drain()
                s.activation(out=PL[9], in_=PL[1], func=AF.Exp, scale=-1.0, bias=-EXPB)
                s.drain()
                s.activation(out=PL[10], in_=PL[1], func=AF.Exp, scale=1.0, bias=EXPB)
                s.drain()
                s.activation(out=T8[:, 48:56].rearrange("p (a b) -> p a b", b=1),
                             in_=PL[1].rearrange("p (a b) -> p a b", b=64)[:, :, 63:64],
                             func=AF.Exp)
                ctx.fence(s, "scalar", s_s1)
                ctx.sems["D_v2"].wge(s, gk + 1)
                s.drain()
                s.activation(out=PL[6], in_=PL[0], func=AF.Ln)
                s.drain()
                s.activation(out=PL[6], in_=PL[6], func=AF.Exp, scale=0.5)
                ctx.fence(s, "scalar", s_s2)

    def GAc(g, k):
        return GA[:, g * 512 + k * 64:g * 512 + k * 64 + 64]

    def GXc(g, k):
        return GX[:, g * 512 + k * 64:g * 512 + k * 64 + 64]

    def XHc(cc, k):
        return XH[:, cc * 512 + k * 64:cc * 512 + k * 64 + 64]

    def ends(p):
        return p.rearrange("p (a b) -> p a b", b=64)[:, :, 63:64]

    def t8(a, b):
        return T8[:, a:b].rearrange("p (a b) -> p a b", b=1)

    @blk.vector
    def _(v):
        for stt in range(4):
            for k in range(8):
                gk = stt * 8 + k
                ctx.sems["D_ge"].wge(v, stt * 16 + 16)
                if gk >= 2:
                    ctx.sems["D_sp"].wge(v, 64 * (gk - 1))
                v.drain()
                GA3 = GA.rearrange("p (g t) -> p g t", t=512)[:, :, k * 64:(k + 1) * 64]
                v.tensor_tensor(out=PL[0].rearrange("p (g t) -> p g t", t=64), in0=GA3,
                                in1=MSPT.rearrange("p (g t) -> p g t", t=64), op=OP.mult)
                v.tensor_tensor(out=PL[2].rearrange("p (g t) -> p g t", t=64), in0=GA3,
                                in1=AIMT.rearrange("p (g t) -> p g t", t=64), op=OP.mult)
                v.drain()
                for g in range(8):
                    v.tensor_tensor_scan(PL[1][:, g * 64:(g + 1) * 64], ONES64,
                                         PL[0][:, g * 64:(g + 1) * 64], ZCOL, OP.mult, OP.add)
                    v.tensor_tensor_scan(PL[3][:, g * 64:(g + 1) * 64], ONES64,
                                         PL[2][:, g * 64:(g + 1) * 64], ZCOL, OP.mult, OP.add)
                v.drain()
                v.tensor_scalar(out=PL[4], in0=PL[3], scalar1=0.15915494309189535, scalar2=None, op0=OP.mult)
                v.drain()
                v.tensor_scalar(out=PL[5], in0=PL[4], scalar1=0.5, scalar2=None, op0=OP.is_ge)
                v.drain()
                v.scalar_tensor_tensor(out=PL[5], in0=PL[4], scalar=1.5, in1=PL[5], op0=OP.is_ge, op1=OP.add)
                v.drain()
                v.scalar_tensor_tensor(out=PL[5], in0=PL[4], scalar=2.5, in1=PL[5], op0=OP.is_ge, op1=OP.add)
                v.drain()
                v.scalar_tensor_tensor(out=PL[5], in0=PL[4], scalar=3.5, in1=PL[5], op0=OP.is_ge, op1=OP.add)
                v.drain()
                v.scalar_tensor_tensor(out=PL[4], in0=PL[5], scalar=-6.283185307179586, in1=PL[3],
                                       op0=OP.mult, op1=OP.add)
                v.drain()
                v.tensor_scalar(out=PL[4], in0=PL[4], scalar1=3.1415925, scalar2=-3.1415925,
                                op0=OP.min, op1=OP.max)
                v.drain()
                v.tensor_scalar(out=PL[5], in0=PL[4], scalar1=-1.0, scalar2=None, op0=OP.mult)
                v.drain()
                v.tensor_tensor(out=PL[5], in0=PL[5], in1=PL[4], op=OP.max)
                ctx.fence(v, "vector", s_v1)
                ctx.sems["D_s1"].wge(v, gk + 1)
                v.drain()
                v.tensor_scalar(out=PL[0], in0=PL[6], scalar1=-1.0, scalar2=1.0, op0=OP.mult, op1=OP.add)
                v.drain()
                v.tensor_scalar(out=PL[0], in0=PL[0], scalar1=1e-6, scalar2=None, op0=OP.max)
                ctx.fence(v, "vector", s_v2)
                ctx.sems["D_s2"].wge(v, gk + 1)
                v.drain()
                GX3 = GX.rearrange("p (g t) -> p g t", t=512)[:, :, k * 64:(k + 1) * 64]
                XH3 = XH.rearrange("p (c t) -> p c t", t=512)[:, :, k * 64:(k + 1) * 64]
                v.tensor_tensor(out=PL[0].rearrange("p (g t) -> p g t", t=64), in0=GX3,
                                in1=PL[6].rearrange("p (g t) -> p g t", t=64), op=OP.mult)
                v.drain()
                v.tensor_tensor(out=PL[4].rearrange("p (g t) -> p g t", t=64),
                                in0=PL[0].rearrange("p (g t) -> p g t", t=64),
                                in1=XH3[:, 0:8, :], op=OP.mult)
                v.tensor_tensor(out=PL[5].rearrange("p (g t) -> p g t", t=64),
                                in0=PL[0].rearrange("p (g t) -> p g t", t=64),
                                in1=XH3[:, 8:16, :], op=OP.mult)
                v.drain()
                v.tensor_tensor(out=PL[11], in0=PL[9], in1=PL[8], op=OP.mult)
                v.drain()
                v.tensor_tensor(out=PL[6], in0=PL[9], in1=PL[7], op=OP.mult)
                v.drain()
                v.tensor_tensor(out=PL[0], in0=PL[11], in1=PL[4], op=OP.mult)
                v.drain()
                v.tensor_tensor(out=PL[9], in0=PL[6], in1=PL[5], op=OP.mult)
                v.drain()
                v.tensor_tensor(out=PL[0], in0=PL[0], in1=PL[9], op=OP.add)
                v.drain()
                v.tensor_tensor(out=PL[9], in0=PL[11], in1=PL[5], op=OP.mult)
                v.drain()
                v.tensor_tensor(out=PL[11], in0=PL[6], in1=PL[4], op=OP.mult)
                v.drain()
                v.tensor_tensor(out=PL[9], in0=PL[9], in1=PL[11], op=OP.subtract)
                v.drain()
                v.tensor_tensor(out=PL[4], in0=PL[10], in1=PL[8], op=OP.mult)
                v.drain()
                v.tensor_tensor(out=PL[5], in0=PL[10], in1=PL[7], op=OP.mult)
                v.drain()
                v.tensor_tensor(out=t8(16, 24), in0=t8(48, 56), in1=ends(PL[8]), op=OP.mult)
                v.drain()
                v.tensor_tensor(out=t8(24, 32), in0=t8(48, 56), in1=ends(PL[7]), op=OP.mult)
                v.drain()
                for g in range(8):
                    v.tensor_tensor_scan(PL[6][:, g * 64:(g + 1) * 64], ONES64,
                                         PL[0][:, g * 64:(g + 1) * 64], ZCOL, OP.mult, OP.add)
                    v.tensor_tensor_scan(PL[7][:, g * 64:(g + 1) * 64], ONES64,
                                         PL[9][:, g * 64:(g + 1) * 64], ZCOL, OP.mult, OP.add)
                v.drain()
                v.tensor_tensor(out=t8(0, 8), in0=ends(PL[4]), in1=ends(PL[6]), op=OP.mult)
                v.drain()
                v.tensor_tensor(out=t8(8, 16), in0=ends(PL[5]), in1=ends(PL[7]), op=OP.mult)
                v.drain()
                v.tensor_tensor(out=T8[:, 0:8], in0=T8[:, 0:8], in1=T8[:, 8:16], op=OP.subtract)
                v.drain()
                v.tensor_tensor(out=t8(8, 16), in0=ends(PL[5]), in1=ends(PL[6]), op=OP.mult)
                v.drain()
                v.tensor_tensor(out=t8(32, 40), in0=ends(PL[4]), in1=ends(PL[7]), op=OP.mult)
                v.drain()
                v.tensor_tensor(out=T8[:, 8:16], in0=T8[:, 8:16], in1=T8[:, 32:40], op=OP.add)
                v.drain()
                cr, ci = CLOr[:, gk * 8:gk * 8 + 8], CLOi[:, gk * 8:gk * 8 + 8]
                nr, ni = CLOr[:, gk * 8 + 8:gk * 8 + 16], CLOi[:, gk * 8 + 8:gk * 8 + 16]
                v.tensor_tensor(out=T8[:, 32:40], in0=T8[:, 16:24], in1=cr, op=OP.mult)
                v.drain()
                v.tensor_tensor(out=T8[:, 40:48], in0=T8[:, 24:32], in1=ci, op=OP.mult)
                v.drain()
                v.tensor_tensor(out=nr, in0=T8[:, 32:40], in1=T8[:, 40:48], op=OP.subtract)
                v.drain()
                v.tensor_tensor(out=nr, in0=nr, in1=T8[:, 0:8], op=OP.add)
                v.drain()
                v.tensor_tensor(out=T8[:, 32:40], in0=T8[:, 16:24], in1=ci, op=OP.mult)
                v.drain()
                v.tensor_tensor(out=T8[:, 40:48], in0=T8[:, 24:32], in1=cr, op=OP.mult)
                v.drain()
                v.tensor_tensor(out=ni, in0=T8[:, 32:40], in1=T8[:, 40:48], op=OP.add)
                v.drain()
                v.tensor_tensor(out=ni, in0=ni, in1=T8[:, 8:16], op=OP.add)
                v.drain()
                qr, qi = QSTr[:, gk * 8:gk * 8 + 8], QSTi[:, gk * 8:gk * 8 + 8]
                nqr, nqi = QSTr[:, gk * 8 + 8:gk * 8 + 16], QSTi[:, gk * 8 + 8:gk * 8 + 16]
                v.tensor_tensor(out=T8[:, 32:40], in0=qr, in1=T8[:, 16:24], op=OP.mult)
                v.drain()
                v.tensor_tensor(out=T8[:, 40:48], in0=qi, in1=T8[:, 24:32], op=OP.mult)
                v.drain()
                v.tensor_tensor(out=nqr, in0=T8[:, 32:40], in1=T8[:, 40:48], op=OP.subtract)
                v.drain()
                v.tensor_tensor(out=T8[:, 32:40], in0=qr, in1=T8[:, 24:32], op=OP.mult)
                v.drain()
                v.tensor_tensor(out=T8[:, 40:48], in0=qi, in1=T8[:, 16:24], op=OP.mult)
                v.drain()
                v.tensor_tensor(out=nqi, in0=T8[:, 32:40], in1=T8[:, 40:48], op=OP.add)
                v.drain()
                sb = SPB[gk % 2]
                v.tensor_copy(sb[:, 0:512], PL[4])
                v.drain()
                v.tensor_copy(sb[:, 512:1024], PL[5])
                v.drain()
                v.tensor_copy(sb[:, 1024:1536], PL[6])
                v.drain()
                v.tensor_copy(sb[:, 1536:2048], PL[7])
                ctx.fence(v, "vector", s_v3)
        v.drain()
        v.tensor_copy(SUMT[:, 0:8], QSTr[:, 256:264])
        v.drain()
        v.tensor_copy(SUMT[:, 8:16], QSTi[:, 256:264])
        v.drain()
        v.tensor_copy(SUMT[:, 16:24], CLOr[:, 256:264])
        v.drain()
        v.tensor_copy(SUMT[:, 24:32], CLOi[:, 256:264])
        ctx.fence(v, "vector", s_sum)

    @blk.vector
    def _(v):
        ctx.sems["D_cc"].wge(v, 2)
        v.memset(CIN[:, 0:16], 0.0)
        for b in range(2):
            v.drain()
            v.memset(T8[:, 48:56], 0.0)
            v.memset(T8[:, 56:64], 0.0)
            for r in range(4):
                c = b * 4 + r
                if r > 0:
                    p = b * 4 + r - 1
                    v.drain()
                    v.tensor_tensor(out=T8[:, 0:8], in0=TRT[p][:, 0:8], in1=T8[:, 48:56], op=OP.mult)
                    v.drain()
                    v.tensor_tensor(out=T8[:, 8:16], in0=TRT[p][:, 8:16], in1=T8[:, 56:64], op=OP.mult)
                    v.drain()
                    v.tensor_tensor(out=T8[:, 16:24], in0=TRT[p][:, 0:8], in1=T8[:, 56:64], op=OP.mult)
                    v.drain()
                    v.tensor_tensor(out=T8[:, 24:32], in0=TRT[p][:, 8:16], in1=T8[:, 48:56], op=OP.mult)
                    v.drain()
                    v.tensor_tensor(out=T8[:, 48:56], in0=T8[:, 0:8], in1=T8[:, 8:16], op=OP.subtract)
                    v.drain()
                    v.tensor_tensor(out=T8[:, 56:64], in0=T8[:, 16:24], in1=T8[:, 24:32], op=OP.add)
                    v.drain()
                    v.tensor_tensor(out=T8[:, 48:56], in0=T8[:, 48:56], in1=TRT[p][:, 16:24], op=OP.add)
                    v.drain()
                    v.tensor_tensor(out=T8[:, 56:64], in0=T8[:, 56:64], in1=TRT[p][:, 24:32], op=OP.add)
                v.drain()
                v.tensor_scalar(out=T8[:, 0:8], in0=T8[:, 48:56], scalar1=smt[:, c:c + 1], scalar2=None, op0=OP.mult)
                v.drain()
                v.tensor_tensor(out=CIN[:, 0:8], in0=CIN[:, 0:8], in1=T8[:, 0:8], op=OP.add)
                v.drain()
                v.tensor_scalar(out=T8[:, 0:8], in0=T8[:, 56:64], scalar1=smt[:, c:c + 1], scalar2=None, op0=OP.mult)
                v.drain()
                v.tensor_tensor(out=CIN[:, 8:16], in0=CIN[:, 8:16], in1=T8[:, 0:8], op=OP.add)
        ctx.fence(v, "vector", s_cin)

    ctx.barrier()

    def bank8(oc):
        ps = PS[["A", "B", "C", "D"][oc // 2]]
        return ps[:, 0:512] if oc % 2 == 0 else ps[:, 512:1024]

    @blk.sync
    def _(e):
        for gk in range(32):
            if gk >= 2:
                ctx.sems["D_h"].wge(e, gk - 1)
            rb = RLB[gk % 2]
            for q in range(4):
                ctx.dma(e, "sync", rb[:, q * 512:(q + 1) * 512],
                        dram["spill_hbm"][(q * 32 + gk) * 128:(q * 32 + gk + 1) * 128, :])
            e.wait_ge(ctx.dsem["sync"].h, ctx.dsem["sync"].v)
            ctx.fence(e, "sync", s_rl)

    @blk.gpsimd
    def _(g):
        for stt in range(4):
            for hc in range(16):
                kk = stt * 16 + hc
                if kk >= 2:
                    ctx.sems["D_om"].wge(g, kk - 1)
                ctx.dma(g, "gpsimd", WINB[kk % 2], dram["w_out"][hc * 128:(hc + 1) * 128, :])
                g.wait_ge(ctx.dsem["gpsimd"].h, ctx.dsem["gpsimd"].v)
                ctx.fence(g, "gpsimd", s_wo)

    @blk.tensor
    def _(te):
        for stt in range(4):
            ctx.sems["D_h"].wge(te, 8 * (stt + 1))
            if stt >= 1:
                ctx.sems["D_oe"].wge(te, 8 * stt)
            for hc in range(16):
                kk = stt * 16 + hc
                ctx.sems["D_wo"].wge(te, kk + 1)
                for oc in range(8):
                    last = nc.tensor.matmul(bank8(oc), WINB[kk % 2][:, oc * 128:oc * 128 + 128],
                                            HMB[:, hc * 512:(hc + 1) * 512],
                                            start=(hc == 0), stop=(hc == 15))
                s_om.inc(last)

    @blk.vector
    def _(v):
        def evac(stt):
            for oc in range(8):
                ctx.sems["D_om"].wge(v, 16 * (stt + 1))
                v.drain()
                s_oe.inc(v.scalar_tensor_tensor(out=xf_ap(ctx, oc, HALO + stt * 512, 512),
                                                in0=bank8(oc), scalar=b_out[:, oc:oc + 1],
                                                in1=xf_ap(ctx, oc, HALO + stt * 512, 512),
                                                op0=OP.add, op1=OP.add))
                v.drain()
        for stt in range(4):
            if stt >= 1:
                evac(stt - 1)
            for k in range(8):
                gk = stt * 8 + k
                ctx.sems["D_rl"].wge(v, gk + 1)
                if k == 0 and stt >= 1:
                    ctx.sems["D_om"].wge(v, 16 * stt)
                rb = RLB[gk % 2]
                v.drain()
                v.tensor_tensor(out=T8[:, 0:8], in0=QSTr[:, gk * 8:gk * 8 + 8], in1=CIN[:, 0:8], op=OP.mult)
                v.drain()
                v.tensor_tensor(out=T8[:, 8:16], in0=QSTi[:, gk * 8:gk * 8 + 8], in1=CIN[:, 8:16], op=OP.mult)
                v.drain()
                v.tensor_tensor(out=T8[:, 0:8], in0=T8[:, 0:8], in1=T8[:, 8:16], op=OP.subtract)
                v.drain()
                v.tensor_tensor(out=T8[:, 0:8], in0=T8[:, 0:8], in1=CLOr[:, gk * 8:gk * 8 + 8], op=OP.add)
                v.drain()
                v.tensor_tensor(out=T8[:, 8:16], in0=QSTr[:, gk * 8:gk * 8 + 8], in1=CIN[:, 8:16], op=OP.mult)
                v.drain()
                v.tensor_tensor(out=T8[:, 16:24], in0=QSTi[:, gk * 8:gk * 8 + 8], in1=CIN[:, 0:8], op=OP.mult)
                v.drain()
                v.tensor_tensor(out=T8[:, 8:16], in0=T8[:, 8:16], in1=T8[:, 16:24], op=OP.add)
                v.drain()
                v.tensor_tensor(out=T8[:, 8:16], in0=T8[:, 8:16], in1=CLOi[:, gk * 8:gk * 8 + 8], op=OP.add)
                v.drain()
                v.tensor_scalar(out=T8[:, 0:16], in0=T8[:, 0:16], scalar1=EMB, scalar2=None, op0=OP.mult)
                v.drain()
                for g in range(8):
                    v.tensor_scalar(out=PL[0][:, g * 64:(g + 1) * 64],
                                    in0=rb[:, 1024 + g * 64:1024 + g * 64 + 64],
                                    scalar1=T8[:, g:g + 1], scalar2=None, op0=OP.add)
                    v.tensor_scalar(out=PL[1][:, g * 64:(g + 1) * 64],
                                    in0=rb[:, 1536 + g * 64:1536 + g * 64 + 64],
                                    scalar1=T8[:, 8 + g:9 + g], scalar2=None, op0=OP.add)
                v.drain()
                v.tensor_tensor(out=PL[2], in0=rb[:, 0:512], in1=PL[0], op=OP.mult)
                v.drain()
                v.tensor_tensor(out=PL[3], in0=rb[:, 512:1024], in1=PL[1], op=OP.mult)
                v.drain()
                v.tensor_tensor(out=PL[2], in0=PL[2], in1=PL[3], op=OP.subtract)
                v.drain()
                v.tensor_tensor(out=PL[3], in0=rb[:, 512:1024], in1=PL[0], op=OP.mult)
                v.drain()
                v.tensor_tensor(out=PL[4], in0=rb[:, 0:512], in1=PL[1], op=OP.mult)
                v.drain()
                v.tensor_tensor(out=PL[3], in0=PL[3], in1=PL[4], op=OP.add)
                v.drain()
                H3 = HMB.rearrange("p (a b) -> p a b", b=512)
                v.tensor_copy(H3[:, 0:8, k * 64:k * 64 + 64], PL[2].rearrange("p (a b) -> p a b", a=8))
                v.drain()
                s_h.inc(v.tensor_copy(H3[:, 8:16, k * 64:k * 64 + 64], PL[3].rearrange("p (a b) -> p a b", a=8)))
                v.drain()
        evac(3)

    ctx.tap("qall", QALL)
    ctx.tap("misc", MISC)
    ctx.tap("hmb", HMB)




NE = 17


def phaseE(ctx, dram, PS, st):
    nc, blk = ctx.nc, ctx.blk
    SM = st["SM"]
    onesb_row = st["onesb_row"]

    XN4 = ctx.at(R2_KB, 8 * NTOK, BF16)
    WEX = []
    for b in range(2):
        base = R2_KB + 32 + 24 * b
        WEX.append(dict(
            WG=ctx.at(base, 4096, BF16),
            WU=ctx.at(base + 8, 4096, BF16),
            WD=ctx.at(base + 16, 4096, BF16),
        ))
    GW = ctx.at(R2_KB + 80, 16 * 512, BF16)
    RTT = ctx.at(R2_KB + 96, 256)
    SCR = ctx.at(R2_KB + 97, 256)
    WTROW = ctx.at(R1_KB, 2048, BF16)[0:1, :]
    WTONES = ctx.at(R1_KB + 4, 2048, BF16)[0:1, :]
    GRP = ctx.at(R1_KB + 8, 512, BF16)
    WB16 = GRP[:, 0:16]
    GWR = GRP[:, 128:256]
    SILB = [ctx.at(R1_KB + 9 + b, 512, BF16) for b in range(2)]
    HUW = [ctx.at(R1_KB + 12 + 2 * b, 512) for b in range(2)]
    WBTF = ctx.at(R1_KB + 16, 2048, BF16)
    GBT = SM[:, 328:344]
    WMT = ctx.stack.enter_context(
        nc.sbuf_tensor("wmtE", [128, 128], BF16, is_transpose_dest=True)).ap()

    s_rg = ctx.sem("E_rg")
    s_rmm = ctx.sem("E_rmm")
    s_ra = ctx.sem("E_ra")
    s_rs = ctx.sem("E_rs")
    s_rw = ctx.sem("E_rw")
    s_rt = ctx.sem("E_rt")
    s_wts = ctx.sem("E_wts")
    s_x4a = ctx.sem("E_x4a")
    s_wl = ctx.sem("E_wl")
    s_wtr = ctx.sem("E_wtr")
    s_wb = ctx.sem("E_wb")
    s_wbe = ctx.sem("E_wbe")
    s_p1 = ctx.sem("E_p1")
    s_sil = ctx.sem("E_sil")
    s_huw = ctx.sem("E_huw")
    s_gw = ctx.sem("E_gw")
    s_p2 = ctx.sem("E_p2")
    s_ev = ctx.sem("E_ev")

    emit_ln(ctx, PS, st, "LN4", XN4)
    ctx.barrier()

    B4 = [PS["A"][:, 0:512], PS["A"][:, 512:1024], PS["B"][:, 0:512], PS["B"][:, 512:1024]]
    B4H = [PS["C"][:, 0:512], PS["C"][:, 512:1024], PS["D"][:, 0:512], PS["D"][:, 512:1024]]
    B8 = B4 + B4H

    def wsrc(e):
        if e < 16:
            return (dram["ex_wg"][e * 1024:(e + 1) * 1024, :],
                    dram["ex_wu"][e * 1024:(e + 1) * 1024, :],
                    dram["ex_wd"][e * 512:(e + 1) * 512, :])
        return (dram["sh_wg"], dram["sh_wu"], dram["sh_wd"])

    def ldw(g, e):
        sg, su, sd = wsrc(e)
        w = WEX[e % 2]
        ctx.dma(g, "gpsimd", w["WG"].rearrange("p (a b) -> p a b", a=8),
                sg.rearrange("(a p) c -> p a c", p=128))
        ctx.dma(g, "gpsimd", w["WU"].rearrange("p (a b) -> p a b", a=8),
                su.rearrange("(a p) c -> p a c", p=128))
        ctx.dma(g, "gpsimd", w["WD"].rearrange("p (a b) -> p a b", a=4),
                sd.rearrange("(a p) c -> p a c", p=128))

    @blk.gpsimd
    def _(g):
        ctx.dma(g, "gpsimd", GWR.rearrange("p (a b) -> p a b", a=8),
                dram["gate_w"].rearrange("(a p) c -> p a c", p=128))
        ctx.dma(g, "gpsimd", GBT, dram["gate_bt"])
        g.wait_ge(ctx.dsem["gpsimd"].h, ctx.dsem["gpsimd"].v)
        ctx.fence(g, "gpsimd", s_rg)
        ldw(g, 0)
        g.wait_ge(ctx.dsem["gpsimd"].h, ctx.dsem["gpsimd"].v)
        ctx.fence(g, "gpsimd", s_wl)
        for e in range(NE):
            if e + 1 < NE:
                ctx.sems["E_p2"].wge(g, 4 * e)
                ldw(g, e + 1)
            for ic in range(4):
                for t in range(4):
                    gi = 16 * e + 4 * ic + t
                    ctx.sems["E_sil"].wge(g, gi + 1)
                    ctx.sems["E_huw"].wge(g, gi + 1)
                    if e >= 1 and ic == 0 and t == 0:
                        ctx.sems["E_p2"].wge(g, 4 * e)
                    s_gw.inc(g.tensor_tensor(out=GW[:, ic * 2048 + t * 512:ic * 2048 + t * 512 + 512],
                                             in0=SILB[t % 2], in1=HUW[t % 2], op=OP.mult))
            g.wait_ge(ctx.dsem["gpsimd"].h, ctx.dsem["gpsimd"].v)
            ctx.fence(g, "gpsimd", s_wl)

    @blk.sync
    def _(e_):
        for e in range(16):
            if e == 0:
                ctx.sems["E_rt"].wge(e_, 1)
            ctx.sems["E_wb"].wge(e_, 4 * e)
            ctx.dma(e_, "sync", WTROW, dram["wmoe_hbm"][e:e + 1, :])
            e_.wait_ge(ctx.dsem["sync"].h, ctx.dsem["sync"].v)
            ctx.fence(e_, "sync", s_wtr)

    @blk.tensor
    def _(te):
        ctx.sems["E_rg"].wge(te, 1)
        for tile in range(16):
            if tile >= 2:
                ctx.sems["E_rs"].wge(te, tile - 1)
            bank = PS["A"][:, 0:16] if tile % 2 == 0 else PS["A"][:, 512:528]
            for dc in range(8):
                last = nc.tensor.matmul(bank, XN4[:, dc * NTOK + tile * 128:dc * NTOK + tile * 128 + 128],
                                        GWR[:, dc * 16:(dc + 1) * 16], start=(dc == 0), stop=(dc == 7))
            s_rmm.inc(last)
        for e in range(NE):
            ctx.sems["E_wl"].wge(te, e + 1)
            if e < 16:
                ctx.sems["E_wtr"].wge(te, e + 1)
            else:
                ctx.sems["E_x4a"].wge(te, 1)
            if e >= 1:
                ctx.sems["E_ev"].wge(te, 4 * e)
            row = WTONES if e == 16 else WTROW
            for t in range(4):
                s_wb.inc(nc.tensor.matmul(B4[t], onesb_row,
                                          row[0:1, t * 512:(t + 1) * 512], start=True, stop=True))
            for ic in range(4):
                if ic == 0:
                    ctx.sems["E_wbe"].wge(te, 4 * e + 4)
                else:
                    ctx.sems["E_sil"].wge(te, 16 * e + 4 * ic)
                    ctx.sems["E_huw"].wge(te, 16 * e + 4 * ic)
                for dc in range(8):
                    for t in range(4):
                        last = nc.tensor.matmul(B4[t], WEX[e % 2]["WG"][:, dc * 512 + ic * 128:dc * 512 + ic * 128 + 128],
                                                XN4[:, dc * NTOK + t * 512:dc * NTOK + t * 512 + 512],
                                                start=(dc == 0), stop=(dc == 7))
                s_p1.inc(last)
                for dc in range(8):
                    for t in range(4):
                        last = nc.tensor.matmul(B4H[t], WEX[e % 2]["WU"][:, dc * 512 + ic * 128:dc * 512 + ic * 128 + 128],
                                                XN4[:, dc * NTOK + t * 512:dc * NTOK + t * 512 + 512],
                                                start=(dc == 0), stop=(dc == 7))
                s_p1.inc(last)
            ctx.sems["E_gw"].wge(te, 16 * (e + 1))
            for p4 in range(4):
                if p4 >= 1:
                    ctx.sems["E_ev"].wge(te, 4 * e + p4)
                for ic in range(4):
                    for o1 in range(2):
                        oc = 2 * p4 + o1
                        for t in range(4):
                            last = nc.tensor.matmul(B8[o1 * 4 + t],
                                                    WEX[e % 2]["WD"][:, ic * 1024 + oc * 128:ic * 1024 + oc * 128 + 128],
                                                    GW[:, ic * 2048 + t * 512:ic * 2048 + t * 512 + 512],
                                                    start=(ic == 0), stop=(ic == 3))
                s_p2.inc(last)

    @blk.scalar
    def _(s):
        act_preload(s, "sigmoid_and_others")
        for tile in range(16):
            ctx.sems["E_ra"].wge(s, tile + 1)
            s.drain()
            s.activation(out=RTT[:, tile * 16:(tile + 1) * 16], in_=SCR[:, 0:16], func=AF.Sigmoid)
            ctx.fence(s, "scalar", s_rs)
            ctx.sems["E_rw"].wge(s, tile + 1)
            ctx.dsem["scalar"].inc(s.dma_start(out=WMT[:, :], in_=GRP[:, 0:128], transpose=True), 16)
            s.wait_ge(ctx.dsem["scalar"].h, ctx.dsem["scalar"].v)
            ctx.dma(s, "scalar", dram["wmoe_hbm"][0:16, tile * 128:(tile + 1) * 128], WMT[0:16, 0:128])
            s.wait_ge(ctx.dsem["scalar"].h, ctx.dsem["scalar"].v)
            ctx.fence(s, "scalar", s_wts)
        ctx.fence(s, "scalar", s_rt)
        act_preload(s, "silu_and_others")
        for e in range(NE):
            for ic in range(4):
                ctx.sems["E_p1"].wge(s, 8 * e + 2 * ic + 1)
                for t in range(4):
                    gi = 16 * e + 4 * ic + t
                    if gi >= 2:
                        ctx.sems["E_gw"].wge(s, gi - 1)
                    s.drain()
                    s_sil.inc(s.activation(out=SILB[t % 2], in_=B4[t], func=AF.Silu))

    @blk.vector
    def _(v):
        def top2(t0):
            R = RTT[:, t0 * 16:(t0 + 1) * 16]
            m1, c1, f1, m2 = SCR[:, 16:17], SCR[:, 17:18], SCR[:, 18:19], SCR[:, 19:20]
            den, dv = SCR[:, 20:21], SCR[:, 21:22]
            msk1, inv, R2v = SCR[:, 32:48], SCR[:, 48:64], SCR[:, 64:80]
            mskB, SEL, WS, WF = SCR[:, 80:96], SCR[:, 96:112], SCR[:, 112:128], SCR[:, 128:144]
            ctx.sems["E_rs"].wge(v, t0 + 1)
            v.drain()
            v.tensor_reduce(m1, R, axis=AX.X, op=OP.max)
            v.drain()
            v.tensor_scalar(out=msk1, in0=R, scalar1=m1, scalar2=None, op0=OP.is_ge)
            v.drain()
            v.tensor_reduce(c1, msk1, axis=AX.X, op=OP.add)
            v.drain()
            v.tensor_scalar(out=f1, in0=c1, scalar1=1.5, scalar2=None, op0=OP.is_le)
            v.tensor_scalar(out=inv, in0=msk1, scalar1=-1.0, scalar2=1.0, op0=OP.mult, op1=OP.add)
            v.drain()
            v.tensor_tensor(out=R2v, in0=R, in1=inv, op=OP.mult)
            v.drain()
            v.tensor_reduce(m2, R2v, axis=AX.X, op=OP.max)
            v.drain()
            v.tensor_scalar(out=mskB, in0=R2v, scalar1=m2, scalar2=None, op0=OP.is_ge)
            v.drain()
            v.tensor_scalar(out=mskB, in0=mskB, scalar1=f1, scalar2=None, op0=OP.mult)
            v.drain()
            v.tensor_tensor(out=SEL, in0=msk1, in1=mskB, op=OP.max)
            v.drain()
            v.tensor_tensor(out=WS, in0=R, in1=SEL, op=OP.mult)
            v.drain()
            v.tensor_reduce(den, WS, axis=AX.X, op=OP.add)
            v.drain()
            v.reciprocal(dv, den)
            v.drain()
            v.tensor_scalar(out=WF, in0=WS, scalar1=dv, scalar2=None, op0=OP.mult)
            v.drain()
            if t0 >= 1:
                ctx.sems["E_wts"].wge(v, t0)
            v.tensor_copy(WB16, WF)
            ctx.fence(v, "vector", s_rw)
        for tile in range(17):
            if tile < 16:
                ctx.sems["E_rmm"].wge(v, tile + 1)
                if tile >= 1:
                    ctx.sems["E_rs"].wge(v, tile)
                bank = PS["A"][:, 0:16] if tile % 2 == 0 else PS["A"][:, 512:528]
                v.drain()
                v.tensor_tensor(out=SCR[:, 0:16], in0=bank, in1=GBT, op=OP.add)
                ctx.fence(v, "vector", s_ra)
            if tile >= 1:
                top2(tile - 1)
        v.memset(WTONES, 1.0)
        v.drain()
        for dc in range(8):
            v.tensor_tensor(out=xf_ap(ctx, dc, HALO, NTOK), in0=xf_ap(ctx, dc, HALO, NTOK),
                            in1=XN4[:, dc * NTOK:(dc + 1) * NTOK], op=OP.add)
        ctx.fence(v, "vector", s_x4a)
        for e in range(NE):
            for t in range(4):
                ctx.sems["E_wb"].wge(v, 4 * e + t + 1)
                s_wbe.inc(v.tensor_copy(WBTF[:, t * 512:(t + 1) * 512], B4[t]))
            v.drain()
            for ic in range(4):
                ctx.sems["E_p1"].wge(v, 8 * e + 2 * ic + 2)
                for t in range(4):
                    gi = 16 * e + 4 * ic + t
                    if gi >= 2:
                        ctx.sems["E_gw"].wge(v, gi - 1)
                    s_huw.inc(v.tensor_tensor(out=HUW[t % 2], in0=B4H[t],
                                              in1=WBTF[:, t * 512:(t + 1) * 512], op=OP.mult))
            for p4 in range(4):
                ctx.sems["E_p2"].wge(v, 4 * e + p4 + 1)
                for o1 in range(2):
                    oc = 2 * p4 + o1
                    for t in range(4):
                        last = v.tensor_tensor(out=xf_ap(ctx, oc, HALO + t * 512, 512),
                                               in0=B8[o1 * 4 + t],
                                               in1=xf_ap(ctx, oc, HALO + t * 512, 512), op=OP.add)
                s_ev.inc(last)


def emit(ctx, dram, PS):
    st = {}
    def run(fn):
        ctx.scope_begin()
        fn(ctx, dram, PS, st)
        ctx.barrier()
        ctx.scope_end()
    run(phaseA)
    if PHASES >= 2 and PHASES != 30:
        run(phaseB1)
    if PHASES >= 3 and PHASES != 30:
        run(phaseB2)
    if PHASES >= 4 or PHASES in (30, 35):
        run(phaseC)
    if PHASES >= 5:
        run(phaseD)
    if PHASES >= 6:
        run(phaseE)
    ctx.flush_taps()
    emit_store(ctx, dram)


def emit_store(ctx, dram):
    blk = ctx.blk
    y = dram["y_out"]
    @blk.gpsimd
    def _(g):
        for dc in range(8):
            ctx.dma(g, "gpsimd", y[dc * 128:(dc + 1) * 128, :], xf_ap(ctx, dc, HALO, NTOK))
        ctx.dsem["gpsimd"].wait(g)


def phaseA(ctx, dram, PS, st):
    nc, blk = ctx.nc, ctx.blk
    SM = ctx.at(SM_KB, 1024)
    ones_col = SM[:, 0:1]
    onesb_row = ctx.at(SM_KB + 2, 256, BF16)[0:1, 0:128]
    onesb_col = ctx.at(SM_KB + 2, 256, BF16)[:, 140:141]
    st["SM"] = SM
    st["ones_col"] = ones_col
    st["onesb_col"] = onesb_col
    st["onesb_row"] = onesb_row
    XN1 = ctx.at(R1_KB, 8 * NROW, BF16)
    st["XN1"] = XN1
    XB = ctx.at(R2_KB, 8 * 512, BF16)
    SQ = ctx.at(R2_KB + 8, 8 * 512, BF16)
    SROW_ = ctx.at(R2_KB + 16, 6 * 2304)[0:1, :]
    SROW = [SROW_[:, i * 2304:(i + 1) * 2304] for i in range(6)]
    TMP = ctx.at(R2_KB + 70, 512)
    ABH_ = ctx.at(R2_KB + 72, 4 * 2304, BF16)[0:1, :]
    ABH = [ABH_[:, i * 2304:(i + 1) * 2304] for i in range(4)]

    s_ld = ctx.sem("A_ld")
    s_pre = ctx.sem("A_pre")
    s_mm = ctx.sem("A_mm")
    s_ev = ctx.sem("A_ev")
    s_ab = ctx.sem("A_ab")
    s_bc = ctx.sem("A_bc")
    s_ap = ctx.sem("A_ap")

    @blk.sync
    def _(e):
        for dc in range(8):
            ctx.dma(e, "sync", xf_ap(ctx, dc, 0, NROW), dram["x_fm"][dc * 128:(dc + 1) * 128, :])

    def bank_of(tt):
        return PS["A"] if tt % 2 == 0 else PS["B"]

    @blk.vector
    def _(v):
        v.memset(ones_col, 1.0)
        v.memset(onesb_row, 1.0)
        v.memset(onesb_col, 1.0)
        v.memset(SM[:, 1:2], 1e-8)
        for c, val in enumerate(CONSTS):
            v.memset(SM[:, 250 + c:251 + c], val)
        ctx.dsem["sync"].wait(v)
        for tt in range(NT5 + 1):
            if tt >= 1:
                ctx.sems["A_mm"].wge(v, tt)
                o1, w1 = (tt - 1) * 512, tile_w(tt - 1)
                bank = bank_of(tt - 1)
                v.tensor_copy(SROW[0][:, o1:o1 + w1], bank[0:1, 0:w1])
                s_ev.inc(v.tensor_copy(SROW[1][:, o1:o1 + w1], bank[0:1, 512:512 + w1]))
            if tt < NT5:
                o, w = tt * 512, tile_w(tt)
                for dc in range(8):
                    v.tensor_copy(XB[:, dc * 512:dc * 512 + w], xf_ap(ctx, dc, o, w))
                ctx.fence(v, "vector", s_pre)

    @blk.scalar
    def _(s):
        act_preload(s, "sqrt_and_others")
        ctx.dsem["sync"].wait(s)
        for tt in range(NT5):
            o, w = tt * 512, tile_w(tt)
            if tt >= 1:
                ctx.sems["A_mm"].wge(s, tt)
            for dc in range(8):
                s.activation(out=SQ[:, dc * 512:dc * 512 + w], in_=xf_ap(ctx, dc, o, w), func=AF.Square)
            ctx.fence(s, "scalar", s_pre)

    @blk.tensor
    def _(te):
        for tt in range(NT5):
            o, w = tt * 512, tile_w(tt)
            ctx.sems["A_pre"].wge(te, 2 * (tt + 1))
            if tt >= 2:
                ctx.sems["A_ev"].wge(te, tt - 1)
            bank = bank_of(tt)
            for dc in range(8):
                nc.tensor.matmul(bank[0:1, 0:w], onesb_col, XB[:, dc * 512:dc * 512 + w],
                                 start=(dc == 0), stop=(dc == 7))
            for dc in range(8):
                last = nc.tensor.matmul(bank[0:1, 512:512 + w], onesb_col, SQ[:, dc * 512:dc * 512 + w],
                                        start=(dc == 0), stop=(dc == 7))
            s_mm.inc(last)

    @blk.vector
    def _(v):
        ctx.sems["A_ev"].wge(v, NT5)
        v.drain()
        v.tensor_scalar(out=SROW[0][:, :NROW], in0=SROW[0][:, :NROW], scalar1=1.0 / D, scalar2=None, op0=OP.mult)
        v.tensor_scalar(out=SROW[1][:, :NROW], in0=SROW[1][:, :NROW], scalar1=1.0 / D, scalar2=None, op0=OP.mult)
        v.drain()
        v.tensor_tensor(out=SROW[2][:, :NROW], in0=SROW[0][:, :NROW], in1=SROW[0][:, :NROW], op=OP.mult)
        v.drain()
        v.tensor_tensor(out=SROW[1][:, :NROW], in0=SROW[1][:, :NROW], in1=SROW[2][:, :NROW], op=OP.subtract)
        v.drain()
        v.tensor_scalar(out=SROW[1][:, :NROW], in0=SROW[1][:, :NROW], scalar1=1e-5, scalar2=None, op0=OP.add)
        ctx.fence(v, "vector", s_ab)

    @blk.scalar
    def _(s):
        s_ab.wait(s)
        s.activation(out=SROW[2][:, :NROW], in_=SROW[1][:, :NROW], func=AF.Sqrt)
        ctx.fence(s, "scalar", s_ab)

    @blk.vector
    def _(v):
        ctx.sems["A_ab"].wge(v, 2)
        v.reciprocal(SROW[3][:, :NROW], SROW[2][:, :NROW])
        v.drain()
        v.tensor_tensor(out=SROW[4][:, :NROW], in0=SROW[0][:, :NROW], in1=SROW[3][:, :NROW], op=OP.mult)
        v.drain()
        v.tensor_scalar(out=SROW[4][:, :NROW], in0=SROW[4][:, :NROW], scalar1=-1.0, scalar2=None, op0=OP.mult)
        v.tensor_copy(ABH[0][:, :NROW], SROW[3][:, :NROW])
        v.drain()
        v.tensor_copy(ABH[2][:, :NROW], SROW[4][:, :NROW])
        v.drain()
        v.tensor_copy(SROW[5][:, :NROW], ABH[0][:, :NROW])
        v.drain()
        v.tensor_tensor(out=SROW[5][:, :NROW], in0=SROW[3][:, :NROW], in1=SROW[5][:, :NROW], op=OP.subtract)
        v.drain()
        v.tensor_copy(ABH[1][:, :NROW], SROW[5][:, :NROW])
        v.drain()
        v.tensor_copy(SROW[5][:, :NROW], ABH[2][:, :NROW])
        v.drain()
        v.tensor_tensor(out=SROW[5][:, :NROW], in0=SROW[4][:, :NROW], in1=SROW[5][:, :NROW], op=OP.subtract)
        v.drain()
        v.tensor_copy(ABH[3][:, :NROW], SROW[5][:, :NROW])
        ctx.fence(v, "vector", s_ab)

    @blk.tensor
    def _(te):
        ctx.sems["A_ab"].wge(te, 3)
        for tt in range(NT5):
            o, w = tt * 512, tile_w(tt)
            pa = PS["C"][:, 0:w] if tt % 2 == 0 else PS["C"][:, 512:512 + w]
            pb = PS["D"][:, 0:w] if tt % 2 == 0 else PS["D"][:, 512:512 + w]
            nc.tensor.matmul(pa, onesb_row, ABH[0][:, o:o + w], start=True, stop=False)
            nc.tensor.matmul(pa, onesb_row, ABH[1][:, o:o + w], start=False, stop=True)
            nc.tensor.matmul(pb, onesb_row, ABH[2][:, o:o + w], start=True, stop=False)
            s_bc.inc(nc.tensor.matmul(pb, onesb_row, ABH[3][:, o:o + w], start=False, stop=True))
            ctx.sems["A_ap"].wge(te, tt * 8 + 8)

    @blk.vector
    def _(v):
        for tt in range(NT5):
            o, w = tt * 512, tile_w(tt)
            ctx.sems["A_bc"].wge(v, tt + 1)
            pa = PS["C"][:, 0:w] if tt % 2 == 0 else PS["C"][:, 512:512 + w]
            pb = PS["D"][:, 0:w] if tt % 2 == 0 else PS["D"][:, 512:512 + w]
            for dc in range(8):
                v.tensor_tensor(out=TMP[:, 0:w], in0=xf_ap(ctx, dc, o, w), in1=pa, op=OP.mult)
                v.drain()
                s_ap.inc(v.tensor_tensor(out=XN1[:, dc * NROW + o:dc * NROW + o + w],
                                         in0=TMP[:, 0:w], in1=pb, op=OP.add))
                v.drain()

    ctx.tap("xn1", XN1)


def phaseB1(ctx, dram, PS, st):
    nc, blk = ctx.nc, ctx.blk
    SM, ones_col = st["SM"], st["ones_col"]
    XN1 = st["XN1"]
    Q = ctx.at(R2_KB, NH * NROW, BF16)
    V = ctx.at(R2_KB + 35, NRC * NH * 128, BF16)
    EW = ctx.at(R2_KB + 70, 8 * NH * 128, BF16)
    MNW1 = ctx.at(R2_KB + 86, 4096, BF16)
    MNW2 = ctx.at(R2_KB + 90, 4096, BF16)
    GATH = ctx.at(SM_KB + 3, 128)
    st["Q"], st["V"] = Q, V
    sigb = ctx.at(SM_KB + 2, 256, BF16)[:, 128:136]
    mn_b1 = SM[:, 64:68]; mn_b2 = SM[:, 72:80]; base_m = SM[:, 80:88]
    sig = SM[:, 8:16]; metric = SM[:, 16:24]
    st["sig"], st["metric"] = sig, metric
    partials = SM[:, 40:56]
    bmt = SM[:, 96:104]
    hidb = ctx.at(SM_KB + 2, 256, BF16)[:, 136:140]

    s_w = ctx.sem("B_w")
    s_qmm = ctx.sem("B_qmm")
    s_qe = ctx.sem("B_qe")
    s_vmm = ctx.sem("B_vmm")
    s_ve = ctx.sem("B_ve")
    s_part = ctx.sem("B_part")
    s_cc = ctx.sem("B_cc", pin=True)
    s_met = ctx.sem("B_met")

    @blk.gpsimd
    def _(g):
        ctx.dma(g, "gpsimd", EW.rearrange("p (a b) -> p a b", a=8),
                dram["eqT"].rearrange("(a p) c -> p a c", p=128))
        ctx.dma(g, "gpsimd", MNW1.rearrange("p (a b) -> p a b", a=8), dram["mn_w1"].rearrange("(a p) c -> p a c", p=128))
        g.wait_ge(ctx.dsem["gpsimd"].h, ctx.dsem["gpsimd"].v)
        ctx.fence(g, "gpsimd", s_w)

    @blk.sync
    def _(e):
        ctx.dma(e, "sync", mn_b1, dram["mn_b1"])
        ctx.dma(e, "sync", mn_b2, dram["mn_b2"])
        ctx.dma(e, "sync", base_m, dram["base_m"])
        ctx.dma(e, "sync", bmt, dram["bmt"])

    banks4 = [PS["A"][:, 0:512], PS["A"][:, 512:1024], PS["B"][:, 0:512], PS["B"][:, 512:1024]]
    @blk.tensor
    def _(te):
        s_w.wait(te)
        k = 0
        for h in range(NH):
            for tt in range(NT5):
                o, w = tt * 512, tile_w(tt)
                if k >= 4:
                    ctx.sems["B_qe"].wge(te, k - 3)
                bank = banks4[k % 4]
                for dc in range(8):
                    last = nc.tensor.matmul(bank[:, 0:w], EW[:, (dc * NH + h) * 128:(dc * NH + h) * 128 + 128],
                                            XN1[:, dc * NROW + o:dc * NROW + o + w],
                                            start=(dc == 0), stop=(dc == 7))
                s_qmm.inc(last)
                k += 1

    @blk.scalar
    def _(s):
        k = 0
        for h in range(NH):
            for tt in range(NT5):
                o, w = tt * 512, tile_w(tt)
                ctx.sems["B_qmm"].wge(s, k + 1)
                s_qe.inc(s.activation(out=Q[:, h * NROW + o:h * NROW + o + w], in_=banks4[k % 4][:, 0:w], func=AF.Relu))
                k += 1

    @blk.gpsimd
    def _(g):
        ctx.sems["B_qmm"].wge(g, NH * NT5)
        ctx.dma(g, "gpsimd", EW.rearrange("p (a b) -> p a b", a=8),
                dram["evT"].rearrange("(a p) c -> p a c", p=128))
        g.wait_ge(ctx.dsem["gpsimd"].h, ctx.dsem["gpsimd"].v)
        ctx.fence(g, "gpsimd", s_w)

    @blk.tensor
    def _(te):
        ctx.sems["B_w"].wge(te, 2)
        ctx.sems["B_qe"].wge(te, NH * NT5)
        for rc in range(NRC):
            o = rc * 128
            if rc >= 2:
                ctx.sems["B_ve"].wge(te, rc - 1)
            bank = PS["C"] if rc % 2 == 0 else PS["D"]
            for half in range(2):
                for dc in range(8):
                    last = nc.tensor.matmul(bank[:, half * 512:half * 512 + 512],
                                            XN1[:, dc * NROW + o:dc * NROW + o + 128],
                                            EW[:, dc * 1024 + half * 512:dc * 1024 + half * 512 + 512],
                                            start=(dc == 0), stop=(dc == 7))
            s_vmm.inc(last)

    @blk.scalar
    def _(s):
        for rc in range(NRC):
            ctx.sems["B_vmm"].wge(s, rc + 1)
            bank = PS["C"] if rc % 2 == 0 else PS["D"]
            s_ve.inc(s.activation(out=V[:, rc * 1024:(rc + 1) * 1024], in_=bank[:, 0:1024], func=AF.Relu))

    @blk.vector
    def _(v):
        ctx.sems["B_qe"].wge(v, NH * NT5)
        for h in range(NH):
            v.tensor_reduce(partials[:, h:h + 1], Q[:, h * NROW + HALO:h * NROW + NROW], axis=AX.X, op=OP.add)
        for h in range(NH):
            qs = Q[:, h * NROW + HALO:h * NROW + NROW].rearrange("p (a b) -> p a b", b=8)[:, :, 0:1]
            v.tensor_reduce(partials[:, 8 + h:9 + h], qs, axis=AX.XY, op=OP.add)
        ctx.fence(v, "vector", s_part)

    @blk.gpsimd
    def _(g):
        s_part.wait(g)
        dst = dram["cc_in1"].rearrange("a (p f) -> (a p) f", p=128)
        ctx.dma(g, "gpsimd", dst, partials)
        g.wait_ge(ctx.dsem["gpsimd"].h, ctx.dsem["gpsimd"].v)
        s_cc.inc(g.collective_compute("AllGather", OP.bypass, replica_groups=[[0, 1, 2, 3, 4, 5, 6, 7]],
                                      ins=[dram["cc_in1"][:]], outs=[dram["cc_out1"][:]]))
        ctx.sems["B_cc"].wge(g, 1)
        for r in range(8):
            src = dram["cc_out1"][r:r + 1, :].rearrange("a (p f) -> (a p) f", p=128)
            ctx.dma(g, "gpsimd", GATH[:, r * 16:(r + 1) * 16], src)
        g.wait_ge(ctx.dsem["gpsimd"].h, ctx.dsem["gpsimd"].v)
        ctx.fence(g, "gpsimd", s_cc)

    @blk.vector
    def _(v):
        ctx.sems["B_cc"].wge(v, 2)
        ctx.dsem["sync"].wait(v)
        for r in range(8):
            v.tensor_scalar(out=GATH[:, r * 16:(r + 1) * 16], in0=GATH[:, r * 16:(r + 1) * 16],
                            scalar1=bmt[:, r:r + 1], scalar2=None, op0=OP.mult)
        for r in range(1, 8):
            v.drain()
            v.tensor_tensor(out=GATH[:, 0:16], in0=GATH[:, 0:16], in1=GATH[:, r * 16:(r + 1) * 16], op=OP.add)
        v.drain()
        v.tensor_scalar(out=GATH[:, 0:8], in0=GATH[:, 0:8], scalar1=1.0 / T, scalar2=None, op0=OP.mult)
        v.tensor_scalar(out=GATH[:, 8:16], in0=GATH[:, 8:16], scalar1=0.5 / 1024.0, scalar2=None, op0=OP.mult)
        v.drain()
        v.tensor_tensor(out=sig, in0=GATH[:, 0:8], in1=GATH[:, 8:16], op=OP.add)
        v.drain()
        v.tensor_copy(sigb, sig)
        ctx.fence(v, "vector", s_cc)

    @blk.tensor
    def _(te):
        ctx.sems["B_cc"].wge(te, 3)
        for hc in range(4):
            for dc in range(8):
                last = nc.tensor.matmul(PS["A"][:, hc:hc + 1],
                                        MNW1[:, dc * 512 + hc * 128:dc * 512 + hc * 128 + 128],
                                        sigb[:, dc:dc + 1], start=(dc == 0), stop=(dc == 7))
        s_met.inc(last)

    @blk.scalar
    def _(s):
        s_met.wait(s)
        ctx.dsem["sync"].wait(s)
        for hc in range(4):
            if hc == 0:
                act_preload(s, "gelu_and_others")
            s.activation(out=hidb[:, hc:hc + 1], in_=PS["A"][:, hc:hc + 1], func=AF.Gelu,
                         bias=mn_b1[:, hc:hc + 1], scale=1.0)
        ctx.fence(s, "scalar", s_met)

    @blk.gpsimd
    def _(g):
        s_met.wait(g)
        ctx.dma(g, "gpsimd", MNW2.rearrange("p (a b) -> p a b", a=4), dram["mn_w2"].rearrange("(a p) c -> p a c", p=128))
        g.wait_ge(ctx.dsem["gpsimd"].h, ctx.dsem["gpsimd"].v)
        ctx.fence(g, "gpsimd", s_w)

    @blk.tensor
    def _(te):
        ctx.sems["B_met"].wge(te, 2)
        ctx.sems["B_w"].wge(te, 3)
        for oc in range(8):
            for hc in range(4):
                last = nc.tensor.matmul(PS["B"][:, oc:oc + 1],
                                        MNW2[:, hc * 1024 + oc * 128:hc * 1024 + oc * 128 + 128],
                                        hidb[:, hc:hc + 1], start=(hc == 0), stop=(hc == 3))
        s_met.inc(last)

    @blk.vector
    def _(v):
        ctx.sems["B_met"].wge(v, 3)
        MW = SM[:, 160:168]
        for oc in range(8):
            v.tensor_scalar(out=MW[:, oc:oc + 1], in0=PS["B"][:, oc:oc + 1],
                            scalar1=mn_b2[:, oc:oc + 1], scalar2=0.1, op0=OP.add, op1=OP.mult)
        v.drain()
        v.tensor_tensor(out=MW, in0=MW, in1=base_m, op=OP.add)
        ctx.fence(v, "vector", s_met)

    @blk.scalar
    def _(s):
        ctx.sems["B_met"].wge(s, 4)
        act_preload(s, "natural_log_exp_and_others")
        s.activation(out=SM[:, 168:176], in_=SM[:, 160:168], func=AF.Exp)
        s.drain()
        s.activation(out=metric, in_=SM[:, 168:176], func=AF.Ln, bias=1.0, scale=1.0)
        ctx.fence(s, "scalar", s_met)

    ctx.tap("q", Q)
    ctx.tap("v", V)
    ctx.tap("sig", sig)
    ctx.tap("metric", metric)


def phaseB2(ctx, dram, PS, st):
    nc, blk = ctx.nc, ctx.blk
    SM, metric = st["SM"], st["metric"]
    onesb_row, onesb_col = st["onesb_row"], st["onesb_col"]
    Q, V = st["Q"], st["V"]
    gflat = dram["g_hbm"]
    s_hbm = dram["s_hbm"]
    w2flat = dram["w2_hbm"]
    misc = dram["misc_hbm"]
    misc2 = dram["misc2_hbm"]
    GREG = 32896
    SREG = NROW + 192

    OS = ctx.at(R2_KB + 70, NH * 512, BF16)
    DEC = ctx.at(R2_KB + 78, 8 * 512, BF16)
    ALB = ctx.at(R2_KB + 86, NTOK, BF16)
    HM = ctx.at(R2_KB + 90, NH * 128, BF16)
    ZB = ctx.at(R2_KB + 92, 256, BF16)
    HBU = ctx.at(R2_KB + 93, 8 * 128, BF16)
    HBV = ctx.at(R2_KB + 95, 8 * 128, BF16)
    GSB = [ctx.at(R2_KB + 72 + i, 256) for i in range(4)]
    SRW = ctx.at(R2_KB + 70, 512)[0:1, :]

    s_hm = ctx.sem("C_hm")
    s_hme = ctx.sem("C_hme")
    s_qmm2 = ctx.sem("C_qmm2")
    s_qn = ctx.sem("C_qn")
    s_al = ctx.sem("C_al")
    s_kt = ctx.sem("C_kt")
    s_pr = ctx.sem("C_pr")
    s_smm = ctx.sem("C_smm")
    s_srow = ctx.sem("C_srow")
    s_gmm = ctx.sem("C_gmm")
    s_gev = ctx.sem("C_gev")
    s_mw = ctx.sem("C_mw", pin=True)
    s_sw = ctx.sem("C_sw")
    s_gst = ctx.sem("C_gst")

    KT = ctx.at(R1_KB, NH * NROW, BF16)
    SQB = ctx.at(R2_KB + 72, 1024, BF16)
    QNS = ctx.at(R2_KB + 74, NTOK)
    QW = ctx.at(R2_KB + 82, 1024)
    PRD = ctx.at(R2_KB + 78, NROW, BF16)

    @blk.gpsimd
    def _(g):
        ctx.dma(g, "gpsimd", HBU[0:32, :].rearrange("p (a b) -> p a b", a=8),
                dram["hebUT"].rearrange("(a p) c -> p a c", p=32))
        ctx.dma(g, "gpsimd", HBV[0:32, :].rearrange("p (a b) -> p a b", a=8),
                dram["hebV"].rearrange("(a p) c -> p a c", p=32))
        g.wait_ge(ctx.dsem["gpsimd"].h, ctx.dsem["gpsimd"].v)
        ctx.fence(g, "gpsimd", s_hm)

    @blk.vector
    def _(v):
        v.memset(ZB, 0.0)
        for h in range(NH):
            v.tensor_scalar(out=KT[:, h * NROW:(h + 1) * NROW], in0=Q[:, h * NROW:(h + 1) * NROW],
                            scalar1=metric[:, h:h + 1], scalar2=None, op0=OP.mult)
        ctx.fence(v, "vector", s_kt)

    @blk.gpsimd
    def _(g):
        s_kt.wait(g)
        for i in range(8):
            ctx.dma(g, "gpsimd", bass.AP(w2flat.tensor, i * 32768, [[256, 128], [1, 256]]), ZB)
        ctx.fence(g, "gpsimd", s_kt)

    @blk.tensor
    def _(te):
        s_hm.wait(te)
        for h in range(NH):
            if h >= 2:
                ctx.sems["C_hme"].wge(te, h - 1)
            bank = PS["A"][:, 0:128] if h % 2 == 0 else PS["A"][:, 512:640]
            s_hm.inc(nc.tensor.matmul(bank, HBU[0:32, h * 128:(h + 1) * 128],
                                      HBV[0:32, h * 128:(h + 1) * 128], start=True, stop=True))

    @blk.vector
    def _(v):
        for h in range(NH):
            ctx.sems["C_hm"].wge(v, 1 + h + 1)
            bank = PS["A"][:, 0:128] if h % 2 == 0 else PS["A"][:, 512:640]
            s_hme.inc(v.tensor_copy(HM[:, h * 128:(h + 1) * 128], bank))

    @blk.scalar
    def _(s):
        for h in range(NH):
            for half in range(2):
                ctx.sems["C_qmm2"].wge(s, 2 * (2 * h + half))
                s.drain()
                s_qn.inc(s.activation(out=SQB, in_=Q[:, h * NROW + HALO + half * 1024:
                                                     h * NROW + HALO + half * 1024 + 1024], func=AF.Square))

    @blk.tensor
    def _(te):
        ctx.sems["C_hm"].wge(te, 1 + NH)
        for k in range(2 * NH):
            ctx.sems["C_qn"].wge(te, k + 1)
            ctx.sems["C_srow"].wge(te, 2 * k)
            for j in range(2):
                bank = PS["B"] if j == 0 else PS["C"]
                s_qmm2.inc(nc.tensor.matmul(bank[0:1, 0:512],
                                            onesb_col, SQB[:, j * 512:(j + 1) * 512],
                                            start=True, stop=True))

    @blk.vector
    def _(v):
        for k in range(2 * NH):
            ctx.sems["C_qmm2"].wge(v, 2 * (k + 1))
            for j in range(2):
                bank = PS["B"] if j == 0 else PS["C"]
                ctx.sems["C_mw"].wge(v, 16 * (2 * k + j))
                s_srow.inc(v.tensor_copy(SRW, bank[0:1, 0:512]))

    @blk.gpsimd
    def _(g):
        for k in range(2 * NH):
            for j in range(2):
                ctx.sems["C_srow"].wge(g, 2 * k + j + 1)
                ctx.sems["C_mw"].wge(g, 16 * (2 * k + j))
                s_mw.inc(g.dma_start(out=bass.AP(misc.tensor, k * 1024 + j * 512, [[512, 1], [1, 512]]),
                                     in_=SRW), 16)
        ctx.sems["C_mw"].wge(g, 16 * 32)
        ctx.dma(g, "gpsimd", QNS[0:8, :], misc.rearrange("(a b) -> a b", b=2048)[0:8, :])
        g.wait_ge(ctx.dsem["gpsimd"].h, ctx.dsem["gpsimd"].v)
        ctx.fence(g, "gpsimd", s_qn)

    @blk.scalar
    def _(s):
        for half in range(2):
            ctx.sems["C_qn"].wge(s, 2 * NH + 1 + half)
            if half == 1:
                ctx.sems["C_al"].wge(s, 1)
            s.drain()
            if half == 0:
                act_preload(s, "sqrt_and_others")
            s_qn.inc(s.activation(out=QW[0:8, :], in_=QNS[0:8, half * 1024:(half + 1) * 1024], func=AF.Sqrt))

    @blk.vector
    def _(v):
        for half in range(2):
            ctx.sems["C_qn"].wge(v, 18 + half)
            sl = slice(half * 1024, (half + 1) * 1024)
            v.drain()
            v.tensor_scalar(out=QW[0:8, :], in0=QW[0:8, :], scalar1=1e-12, scalar2=None, op0=OP.max)
            v.drain()
            v.reciprocal(QW[0:8, :], QW[0:8, :])
            v.drain()
            v.tensor_scalar(out=QW[0:8, :], in0=QW[0:8, :], scalar1=0.1, scalar2=None, op0=OP.mult)
            v.drain()
            v.tensor_scalar(out=QNS[0:8, sl], in0=QNS[0:8, sl], scalar1=0.04, scalar2=None, op0=OP.is_gt)
            v.drain()
            v.tensor_tensor(out=QNS[0:8, sl], in0=QNS[0:8, sl], in1=QW[0:8, :], op=OP.mult)
            v.drain()
            v.tensor_copy(ALB[0:8, sl], QNS[0:8, sl])
            ctx.fence(v, "vector", s_al)
    

    @blk.gpsimd
    def _(g):
        ctx.sems["C_al"].wge(g, 2)
        ctx.dma(g, "gpsimd", misc2.rearrange("(a b) -> a b", b=2048)[0:8, :], ALB[0:8, :])
        g.wait_ge(ctx.dsem["gpsimd"].h, ctx.dsem["gpsimd"].v)
        ctx.fence(g, "gpsimd", s_al)


    @blk.tensor
    def _(te):
        ctx.sems["C_qmm2"].wge(te, 2 * 2 * NH)
        ctx.sems["C_al"].wge(te, 2)
        for h in range(NH):
            ctx.sems["C_pr"].wge(te, h + 1)
            ctx.sems["C_srow"].wge(te, 32 + 5 * h)
            for j in range(5):
                w = min(512, NROW - j * 512)
                bank = [PS["B"], PS["B"], PS["C"], PS["C"], PS["D"]][j]
                col = (j % 2) * 512 if j < 4 else 0
                s_smm.inc(nc.tensor.matmul(bank[0:1, col:col + w], onesb_col,
                                           PRD[:, j * 512:j * 512 + w], start=True, stop=True))

    @blk.vector
    def _(v):
        ctx.sems["C_mw"].wge(v, 16 * 32)
        for h in range(NH + 1):
            ctx.sems["C_smm"].wge(v, 5 * h)
            if h >= 1:
                for j in range(5):
                    w = min(512, NROW - j * 512)
                    bank = [PS["B"], PS["B"], PS["C"], PS["C"], PS["D"]][j]
                    col = (j % 2) * 512 if j < 4 else 0
                    ctx.sems["C_sw"].wge(v, 16 * (5 * (h - 1) + j))
                    s_srow.inc(v.tensor_copy(SRW[:, 0:w], bank[0:1, col:col + w]))
            if h < NH:
                v.drain()
                v.tensor_tensor(out=PRD, in0=Q[:, h * NROW:(h + 1) * NROW],
                                in1=KT[:, h * NROW:(h + 1) * NROW], op=OP.mult)
                ctx.fence(v, "vector", s_pr)

    @blk.sync
    def _(e):
        for h in range(NH):
            for j in range(5):
                w = min(512, NROW - j * 512)
                ctx.sems["C_srow"].wge(e, 32 + 5 * h + j + 1)
                ctx.sems["C_sw"].wge(e, 16 * (5 * h + j))
                s_sw.inc(e.dma_start(out=s_hbm[h:h + 1, j * 512:j * 512 + w], in_=SRW[:, 0:w]), 16)
        ctx.sems["C_sw"].wge(e, 16 * 40)

    ctx.barrier()

    banks4 = [PS["A"][:, 0:256], PS["A"][:, 512:768], PS["B"][:, 0:256], PS["B"][:, 512:768]]
    @blk.tensor
    def _(te):
        k = 0
        for h in range(NH):
            for qt in range(1, NRC):
                if k >= 4:
                    ctx.sems["C_gev"].wge(te, k - 3)
                key0 = (qt - 1) * 128
                s_gmm.inc(nc.tensor.matmul(banks4[k % 4], Q[:, h * NROW + qt * 128:h * NROW + qt * 128 + 128],
                                           KT[:, h * NROW + key0:h * NROW + key0 + 256],
                                           start=True, stop=True))
                k += 1

    @blk.vector
    def _(v):
        for k in range(NH * NQT):
            ctx.sems["C_gmm"].wge(v, k + 1)
            if k % 4 == 0 and k >= 4:
                ctx.sems["C_gst"].wge(v, 64 * (k // 4))
            s_gev.inc(v.tensor_copy(GSB[k % 4], banks4[k % 4]))

    @blk.sync
    def _(e):
        for grp in range(NH * NQT // 4):
            ctx.sems["C_gev"].wge(e, 4 * grp + 4)
            for i in range(4):
                k = 4 * grp + i
                idx = k
                s_gst.inc(e.dma_start(out=bass.AP(gflat.tensor, idx * GREG, [[256, 128], [1, 256]]),
                                      in_=GSB[k % 4]), 16)
            ctx.sems["C_gst"].wge(e, 64 * (grp + 1))

    ctx.barrier()

    WC = ctx.at(R1_KB, 512)
    SU = ctx.at(R1_KB + 2, 512)
    STC = ctx.at(R1_KB + 4, 16)
    NEG = ctx.at(R1_KB + 6, 512)
    SELA = ctx.at(R1_KB + 8, 512)
    SELB = ctx.at(R1_KB + 10, 512)
    MX = ctx.at(R1_KB + 12, 8)
    MASK = ctx.at(R1_KB + 13, 512)
    NEGC = ctx.at(R1_KB + 15, 512)
    DSTT = ctx.at(R1_KB + 17, 512)
    WEXP = ctx.at(R1_KB + 19, 512)
    WW = ctx.at(R1_KB + 21, 512)
    RS = ctx.at(R1_KB + 23, 24)
    W2R = ctx.at(R1_KB + 24, 2048, BF16)
    ARW = ctx.at(R1_KB + 28, 1024, BF16)[0:1, :]
    QSC = ctx.at(R1_KB + 30, 128, BF16)

    if "W2T" not in st:
        st["W2T"] = [ctx.stack.enter_context(
            nc.sbuf_tensor(f"w2t{i}", [128, 128], BF16, is_transpose_dest=True)) for i in range(2)]
    W2T = st["W2T"]

    s_wld = ctx.sem("C_wld")
    s_sel = ctx.sem("C_sel")
    s_dst = ctx.sem("C_dst")
    s_wn = ctx.sem("C_wn")
    s_wst = ctx.sem("C_wst")
    s_w2r = ctx.sem("C_w2r")
    s_tp = ctx.sem("C_tp")
    s_abc = ctx.sem("C_abc")
    s_qsc = ctx.sem("C_qsc")
    s_ctx2 = ctx.sem("C_ctx2")
    s_os = ctx.sem("C_os")
    s_dmm = ctx.sem("C_dmm")
    s_dec = ctx.sem("C_dec")
    s_dw = ctx.sem("C_dw")

    NQ = NRC - 1

    @blk.sync
    def _(e):
        for qt in range(1, NRC):
            if qt >= 2:
                ctx.sems["C_os"].wge(e, (qt - 1) * NH)
            for h in range(NH):
                idx = h * NQT + (qt - 1)
                ctx.dma(e, "sync", WC[:, h * 64:(h + 1) * 64],
                        bass.AP(gflat.tensor, idx * GREG + 65, [[257, 128], [1, 64]]))
                ctx.dma(e, "sync", SU[:, h * 64:(h + 1) * 64],
                        bass.AP(s_hbm.tensor, h * SREG + qt * 128 - 63, [[1, 128], [1, 64]]))
                ctx.dma(e, "sync", STC[:, h:h + 1],
                        bass.AP(s_hbm.tensor, h * SREG + qt * 128, [[1, 128], [1, 1]]))
            e.wait_ge(ctx.dsem["sync"].h, ctx.dsem["sync"].v)
            ctx.fence(e, "sync", s_wld)
            ctx.sems["C_wst"].wge(e, qt)
            ctx.sems["C_tp"].wge(e, (qt - 1) * NH)
            for h in range(NH):
                ctx.dma(e, "sync", W2R[:, h * 256:(h + 1) * 256],
                        bass.AP(w2flat.tensor, h * 32768, [[256, 128], [1, 256]]))
            e.wait_ge(ctx.dsem["sync"].h, ctx.dsem["sync"].v)
            ctx.fence(e, "sync", s_w2r)

    @blk.vector
    def _(v):
        for qt in range(1, NRC):
            ctx.sems["C_wld"].wge(v, qt)
            v.scalar_tensor_tensor(out=NEG, in0=WC, scalar=2.0, in1=SU,
                                   op0=OP.mult, op1=OP.subtract)
            v.drain()
            for h in range(NH):
                v.tensor_scalar(out=NEG[:, h * 64:(h + 1) * 64], in0=NEG[:, h * 64:(h + 1) * 64],
                                scalar1=STC[:, h:h + 1], scalar2=None, op0=OP.subtract)
            v.drain()
            for h in range(NH):
                cur = NEG[:, h * 64:(h + 1) * 64]
                for r in range(4):
                    nxt = (SELA if r % 2 == 0 else SELB)[:, h * 64:(h + 1) * 64]
                    v.max(out=MX, in_=cur)
                    v.tensor_copy(nxt, cur)
                    v.drain()
                    v.match_replace(out=nxt, in_to_replace=MX, in_values=nxt, imm_value=-2.0e38)
                    v.drain()
                    cur = nxt
            v.tensor_scalar(out=MASK, in0=SELB, scalar1=-1.0e38, scalar2=None, op0=OP.is_le)
            v.tensor_scalar(out=NEGC, in0=NEG, scalar1=0.0, scalar2=None, op0=OP.min)
            ctx.fence(v, "vector", s_sel)
            ctx.sems["C_dst"].wge(v, qt)
            v.tensor_tensor(out=WW, in0=WEXP, in1=MASK, op=OP.mult)
            v.drain()
            v.tensor_reduce(RS[:, 0:8], WW.rearrange("p (a b) -> p a b", b=64), axis=AX.X, op=OP.add)
            v.drain()
            v.tensor_scalar(out=RS[:, 8:16], in0=RS[:, 0:8], scalar1=1e-8, scalar2=None, op0=OP.add)
            v.drain()
            v.reciprocal(RS[:, 16:24], RS[:, 8:16])
            v.drain()
            for h in range(NH):
                v.tensor_scalar(out=WW[:, h * 64:(h + 1) * 64], in0=WW[:, h * 64:(h + 1) * 64],
                                scalar1=RS[:, 16 + h:17 + h], scalar2=None, op0=OP.mult)
            ctx.fence(v, "vector", s_wn)
            for h in range(NH):
                ctx.sems["C_abc"].wge(v, (qt - 1) * NH + h + 1)
                v.tensor_tensor(out=QSC, in0=Q[:, h * NROW + qt * 128:h * NROW + qt * 128 + 128],
                                in1=PS["B"][:, 0:128], op=OP.mult)
                ctx.fence(v, "vector", s_qsc)
                ctx.sems["C_ctx2"].wge(v, (qt - 1) * NH + h + 1)
                s_os.inc(v.tensor_tensor(out=OS[:, h * 512 + ((qt - 1) % 4) * 128:h * 512 + ((qt - 1) % 4) * 128 + 128],
                                         in0=Q[:, h * NROW + qt * 128:h * NROW + qt * 128 + 128],
                                         in1=PS["A"][:, 0:128], op=OP.mult))
            if qt % 4 == 0:
                tg = qt // 4 - 1
                for p in range(2):
                    ctx.sems["C_dmm"].wge(v, tg * 8 + p * 4 + 4)
                    for oc in range(4):
                        bank = [PS["C"][:, 0:512], PS["C"][:, 512:1024], PS["D"][:, 0:512], PS["D"][:, 512:1024]][oc]
                        ocg = p * 4 + oc
                        s_dec.inc(v.tensor_tensor(out=xf_ap(ctx, ocg, HALO + tg * 512, 512),
                                                  in0=xf_ap(ctx, ocg, HALO + tg * 512, 512),
                                                  in1=bank, op=OP.add))
                v.drain()

    @blk.scalar
    def _(s):
        for qt in range(1, NRC):
            ctx.sems["C_sel"].wge(s, qt)
            s.drain()
            act_preload(s, "sqrt_and_others")
            s.activation(out=DSTT, in_=NEGC, func=AF.Sqrt, bias=SM[:, 1:2], scale=-1.0)
            s.drain()
            act_preload(s, "natural_log_exp_and_others")
            s.activation(out=WEXP, in_=DSTT, func=AF.Exp, scale=-1.0)
            ctx.fence(s, "scalar", s_dst)
            ctx.sems["C_w2r"].wge(s, qt)
            for h in range(NH):
                ctx.dsem["scalar"].inc(s.dma_start(out=W2T[0][:, :], in_=W2R[:, h * 256:h * 256 + 128], transpose=True), 16)
                ctx.dsem["scalar"].inc(s.dma_start(out=W2T[1][:, :], in_=W2R[:, h * 256 + 128:h * 256 + 256], transpose=True), 16)
                s.wait_ge(ctx.dsem["scalar"].h, ctx.dsem["scalar"].v)
                ctx.fence(s, "scalar", s_tp)
                ctx.sems["C_ctx2"].wge(s, (qt - 1) * NH + h + 1)

    @blk.gpsimd
    def _(g):
        for qt in range(1, NRC):
            if (qt - 1) % 4 == 0:
                tg = (qt - 1) // 4
                ctx.sems["C_dmm"].wge(g, tg * 8)
                ctx.dma(g, "gpsimd", DEC.rearrange("p (a b) -> p a b", a=8),
                        dram["decT"].rearrange("(a p) c -> p a c", p=128)[:, :, 0:512])
                g.wait_ge(ctx.dsem["gpsimd"].h, ctx.dsem["gpsimd"].v)
                ctx.fence(g, "gpsimd", s_dw)
            ctx.sems["C_wn"].wge(g, qt)
            for h in range(NH):
                ctx.dma(g, "gpsimd", bass.AP(w2flat.tensor, h * 32768 + 65, [[257, 128], [1, 64]]),
                        WW[:, h * 64:(h + 1) * 64])
            ctx.dma(g, "gpsimd", ARW.rearrange("a (b c) -> a b c", b=8),
                    bass.AP(misc2.tensor, (qt - 1) * 128, [[1, 1], [2048, 8], [1, 128]]))
            g.wait_ge(ctx.dsem["gpsimd"].h, ctx.dsem["gpsimd"].v)
            ctx.fence(g, "gpsimd", s_wst)
            if qt % 4 == 0:
                tg = qt // 4 - 1
                ctx.sems["C_dmm"].wge(g, tg * 8 + 4)
                ctx.dma(g, "gpsimd", DEC.rearrange("p (a b) -> p a b", a=8),
                        dram["decT"].rearrange("(a p) c -> p a c", p=128)[:, :, 512:1024])
                g.wait_ge(ctx.dsem["gpsimd"].h, ctx.dsem["gpsimd"].v)
                ctx.fence(g, "gpsimd", s_dw)

    @blk.tensor
    def _(te):
        for qt in range(1, NRC):
            for h in range(NH):
                ctx.sems["C_tp"].wge(te, (qt - 1) * NH + h + 1)
                if (qt - 1) * NH + h >= 1:
                    ctx.sems["C_os"].wge(te, (qt - 1) * NH + h)
                s_abc.inc(nc.tensor.matmul(PS["B"][:, 0:128], onesb_row,
                                           ARW[:, h * 128:(h + 1) * 128], start=True, stop=True))
                nc.tensor.matmul(PS["A"][:, 0:128], V[:, ((qt - 1) * NH + h) * 128:((qt - 1) * NH + h) * 128 + 128],
                                 W2T[0][:, :], start=True, stop=False)
                nc.tensor.matmul(PS["A"][:, 0:128], V[:, (qt * NH + h) * 128:(qt * NH + h) * 128 + 128],
                                 W2T[1][:, :], start=False, stop=False)
                ctx.sems["C_qsc"].wge(te, (qt - 1) * NH + h + 1)
                s_ctx2.inc(nc.tensor.matmul(PS["A"][:, 0:128], HM[:, h * 128:(h + 1) * 128],
                                            QSC, start=False, stop=True))
            if qt % 4 == 0:
                tg = qt // 4 - 1
                ctx.sems["C_os"].wge(te, qt * NH)
                for p in range(2):
                    ctx.sems["C_dw"].wge(te, tg * 2 + p + 1)
                    if tg * 8 + p * 4 >= 4:
                        ctx.sems["C_dec"].wge(te, tg * 8 + p * 4 - 4 + 4)
                    for oc in range(4):
                        bank = [PS["C"][:, 0:512], PS["C"][:, 512:1024], PS["D"][:, 0:512], PS["D"][:, 512:1024]][oc]
                        for hc in range(NH):
                            last = nc.tensor.matmul(bank, DEC[:, hc * 512 + oc * 128:hc * 512 + oc * 128 + 128],
                                                    OS[:, hc * 512:(hc + 1) * 512],
                                                    start=(hc == 0), stop=(hc == 7))
                        s_dmm.inc(last)




_BUILD_CACHE = {}

def _get_program():
    key = (PHASES, tuple(_TAPS))
    if key not in _BUILD_CACHE:
        _BUILD_CACHE[key] = build_program()
    return _BUILD_CACHE[key]


def _shard_inputs(inputs):
    f32 = lambda a: np.ascontiguousarray(np.asarray(a, np.float32))
    x = f32(inputs["L0_attn_x"])
    eq = f32(inputs["L0_attn_encoder_query"])
    ev = f32(inputs["L0_attn_encoder_value"])
    com = {
        "eqT": np.ascontiguousarray(eq.transpose(1, 0, 2).reshape(D, NH * 128)),
        "evT": np.ascontiguousarray(ev.transpose(1, 0, 2).reshape(D, NH * 128)),
        "mn_w1": f32(inputs["L0_attn_mn_w1"]),
        "mn_b1": np.ascontiguousarray(f32(inputs["L0_attn_mn_b1"]).reshape(4, 128).T),
        "mn_w2": f32(inputs["L0_attn_mn_w2"]),
        "mn_b2": np.ascontiguousarray(f32(inputs["L0_attn_mn_b2"]).reshape(8, 128).T),
        "base_m": np.ascontiguousarray(f32(inputs["L0_attn_base_metric"]).T),
        "decT": f32(inputs["L0_attn_decoder"]),
        "hebUT": np.ascontiguousarray(f32(inputs["L0_attn_hebbian_U"]).transpose(0, 2, 1).reshape(NH * 32, 128)),
        "hebV": f32(inputs["L0_attn_hebbian_V"]).reshape(NH * 32, 128),
        "f0w1T": np.ascontiguousarray(f32(inputs["L0_ffn_W1"]).T),
        "f0w2T": np.ascontiguousarray(f32(inputs["L0_ffn_W2"]).T),
        "f0b1": np.ascontiguousarray(f32(inputs["L0_ffn_b1"]).reshape(32, 128).T),
        "f0b2": np.ascontiguousarray(f32(inputs["L0_ffn_b2"]).reshape(8, 128).T),
        "w_in": f32(inputs["L1_attn_w_in"]),
        "b_in": np.ascontiguousarray(f32(inputs["L1_attn_b_in"]).reshape(16, 128).T),
        "wgx": f32(inputs["L1_attn_w_gx"]).reshape(8 * 256, 128),
        "bgx": np.ascontiguousarray(f32(inputs["L1_attn_b_gx"]).reshape(8, 128).T),
        "wga": f32(inputs["L1_attn_w_ga"]).reshape(8 * 256, 128),
        "bga": np.ascontiguousarray(f32(inputs["L1_attn_b_ga"]).reshape(8, 128).T),
        "arp": np.ascontiguousarray(f32(inputs["L1_attn_a_real_param"]).reshape(8, 128).T),
        "aim": np.ascontiguousarray(f32(inputs["L1_attn_a_imag_param"]).reshape(8, 128).T),
        "w_out": f32(inputs["L1_attn_w_out"]),
        "b_out": np.ascontiguousarray(f32(inputs["L1_attn_b_out"]).reshape(8, 128).T),
        "gate_w": f32(inputs["L1_ffn_gate_w"]),
        "gate_bt": np.ascontiguousarray(np.repeat(f32(inputs["L1_ffn_gate_b"]).reshape(1, 16), 128, axis=0)),
        "sh_wg": f32(inputs["L1_ffn_shared_wg"]),
        "sh_wu": f32(inputs["L1_ffn_shared_wu"]),
        "sh_wd": f32(inputs["L1_ffn_shared_wd"]),
        "ex_wg": f32(inputs["L1_ffn_experts_wg"]).reshape(NEXP * D, DI),
        "ex_wu": f32(inputs["L1_ffn_experts_wu"]).reshape(NEXP * D, DI),
        "ex_wd": f32(inputs["L1_ffn_experts_wd"]).reshape(NEXP * DI, D),
    }
    maps = []
    for c in range(NCORE):
        b, s = divmod(c, 4)
        start = s * NTOK
        if s == 0:
            halo = np.repeat(x[b, 0:1, :], HALO, axis=0)
        else:
            halo = x[b, start - HALO:start, :]
        xc = np.concatenate([halo, x[b, start:start + NTOK, :]], axis=0)
        bm = np.zeros((128, 8), np.float32)
        bm[:, b * 4:(b + 1) * 4] = 1.0
        rm = np.zeros((128, 8), np.float32)
        rm[:, c] = 1.0
        m = dict(com)
        m["x_fm"] = np.ascontiguousarray(xc.T)
        m["bmt"] = bm
        m["rmt"] = rm
        maps.append(m)
    return maps


def kernel(**inputs):
    nc = _get_program()
    in_maps = _shard_inputs(inputs)
    res = run_bass_kernel_spmd(nc, in_maps, core_ids=list(range(NCORE)))
    out = np.zeros((B, T, D), np.float32)
    for c in range(NCORE):
        b, s = divmod(c, 4)
        out[b, s * NTOK:(s + 1) * NTOK, :] = res.results[c]["y_out"].T
    kernel.last_results = res
    return out


# version 07
# speedup vs baseline: 1.0210x; 1.0210x over previous
import contextlib
import numpy as np
import ml_dtypes
import concourse.bass as bass
import concourse.mybir as mybir
from concourse.bass_utils import run_bass_kernel_spmd
from concourse.alu_op_type import AluOpType as OP

F32 = mybir.dt.float32
BF16 = mybir.dt.bfloat16
AF = mybir.ActivationFunctionType
AX = mybir.AxisListType

B, T, D = 2, 8192, 1024
NCORE = 8
NTOK = 2048
HALO = 128
NROW = HALO + NTOK
NRC = NROW // 128
NH = 8
DFF = 4096
DHID, DINNER = 2048, 1024
NEXP, DI = 16, 512
CHUNK = 64
NCHUNK = NTOK // CHUNK
NQT = NTOK // 128
NT5 = (NROW + 511) // 512

_TAPS = ()
PHASES = 99

ENGS = ("tensor", "vector", "scalar", "gpsimd", "sync")
KB = 512


def tile_w(tt):
    return min(512, NROW - tt * 512)


class Phys:
    __slots__ = ("hh", "tot")
    def __init__(self, hh):
        self.hh, self.tot = hh, 0


class Sem:
    __slots__ = ("ph", "base", "v")
    def __init__(self, ph):
        self.ph, self.base, self.v = ph, ph.tot, 0
    @property
    def h(self):
        return self.ph.hh
    def inc(self, inst, n=1):
        inst.then_inc(self.ph.hh, n)
        self.v += n
        self.ph.tot += n
        return inst
    def wge(self, eng, val):
        eng.wait_ge(self.ph.hh, self.base + val)
    def wait(self, eng, v=None):
        eng.wait_ge(self.ph.hh, self.base + (self.v if v is None else v))


class Ctx:
    def __init__(self, nc, stack, arena):
        self.nc, self.stack, self.arena = nc, stack, arena
        self.sems = {}
        self.blk = None
        self.phase = 0
        self.taps = []
        self._avail = []
        self._scope = None
        self.bsem = {e: self.sem("bar_" + e) for e in ENGS}
        self.dsem = {e: self.sem("dma_" + e) for e in ENGS}

    def sem(self, name, pin=False):
        if name not in self.sems:
            if pin or self._scope is None or not self._avail:
                ph = Phys(self.stack.enter_context(self.nc.semaphore(name)))
            else:
                ph = self._avail.pop()
            s = Sem(ph)
            if self._scope is not None and not pin:
                self._scope.append(name)
            self.sems[name] = s
        return self.sems[name]

    def scope_begin(self):
        assert self._scope is None
        self._scope = []

    def scope_end(self):
        for name in self._scope:
            self._avail.append(self.sems.pop(name).ph)
        self._scope = None

    def at(self, kb_off, n_elem, dtype=F32):
        w = 2 if dtype == F32 else 1
        c0 = kb_off * 512
        ap = self.arena[:, c0:c0 + n_elem * w]
        return ap.bitcast(F32) if dtype == F32 else ap

    def dma(self, eng, ename, out, in_):
        return self.dsem[ename].inc(eng.dma_start(out=out, in_=in_), 16)

    def fence(self, eng, ename, sem, n=1):
        c0 = SM_KB * 512 + 1792 + 32 * ENGS.index(ename)
        d = self.arena[:, c0:c0 + 8]
        d2 = self.arena[:, c0 + 8:c0 + 16]
        if ename in ("vector", "gpsimd"):
            eng.drain()
            return sem.inc(eng.memset(d, 0.0), n)
        if ename == "scalar":
            if not getattr(self, "_sc_primed", False):
                eng.activation(out=d, in_=self.at(XF_KB, 8)[:, 0:8], func=AF.Relu)
                self._sc_primed = True
            eng.drain()
            return sem.inc(eng.activation(out=d2, in_=d, func=AF.Relu), n)
        return sem.inc(eng.nop(), n)

    def barrier(self):
        self.phase += 1
        ph = self.phase
        for e in ENGS:
            def mk(ename):
                def _(eng):
                    self.dsem[ename].wait(eng)
                    self.fence(eng, ename, self.bsem[ename])
                    for o in ENGS:
                        if o != ename:
                            eng.wait_ge(self.bsem[o].h, ph)
                return _
            getattr(self.blk, e)(mk(e))

    def tap(self, name, sbuf_ap):
        if name not in _TAPS:
            return
        t = self.nc.dram_tensor("tap_" + name, list(sbuf_ap.shape), sbuf_ap.dtype,
                                kind="ExternalOutput").ap()
        self.taps.append((t, sbuf_ap))

    def flush_taps(self):
        if not self.taps:
            return
        taps, self.taps = list(self.taps), []
        @self.blk.gpsimd
        def _(g):
            for t, s in taps:
                self.dma(g, "gpsimd", t, s)
            self.dsem["gpsimd"].wait(g)


XF_KB, SM_KB, R1_KB, R2_KB = 0, 68, 72, 107
ARENA_KB = 206


def xf_ap(ctx, dc, r0, w):
    base = XF_KB * 256
    return ctx.at(XF_KB, 8 * NROW)[:, dc * NROW + r0: dc * NROW + r0 + w]


def build_program():
    nc = bass.Bass("TRN2", target_bir_lowering=False, debug=False)
    stack = contextlib.ExitStack()
    arena = stack.enter_context(nc.sbuf_tensor("arena", [128, ARENA_KB * 512], BF16))
    ctx = Ctx(nc, stack, arena)
    for c, val in enumerate(CONSTS):
        nc.const_aps.register(F32, val, ctx.at(SM_KB, 1024)[:, 250 + c:251 + c])

    dram = {}
    def din(name, shape, dtype=F32):
        dram[name] = nc.dram_tensor(name, list(shape), dtype, kind="ExternalInput").ap()
    def dint(name, shape, dtype=F32, space="Local"):
        dram[name] = nc.dram_tensor(name, list(shape), dtype, kind="Internal", addr_space=space).ap()

    din("x_fm", [D, NROW])
    din("eqT", [D, NH * 128])
    din("evT", [D, NH * 128])
    din("mn_w1", [D, 512]); din("mn_b1", [128, 4])
    din("mn_w2", [512, D]); din("mn_b2", [128, 8])
    din("base_m", [128, 8])
    din("decT", [D, D])
    din("hebUT", [NH * 32, 128]); din("hebV", [NH * 32, 128])
    din("f0w1T", [D, DFF]); din("f0w2T", [DFF, D])
    din("f0b1", [128, 32]); din("f0b2", [128, 8])
    din("w_in", [D, DHID]); din("b_in", [128, 16])
    din("wgx", [NB8 := 8 * 256, 128]); din("bgx", [128, 8])
    din("wga", [NB8, 128]); din("bga", [128, 8])
    din("arp", [128, 8]); din("aim", [128, 8])
    din("w_out", [DHID, D]); din("b_out", [128, 8])
    din("gate_w", [D, 16]); din("gate_bt", [128, 16])
    din("sh_wg", [D, DI]); din("sh_wu", [D, DI]); din("sh_wd", [DI, D])
    din("ex_wg", [NEXP * D, DI]); din("ex_wu", [NEXP * D, DI]); din("ex_wd", [NEXP * DI, D])
    din("bmt", [128, 8])
    din("rmt", [128, 8])

    dram["y_out"] = nc.dram_tensor("y_out", [D, NTOK], F32, kind="ExternalOutput").ap()

    dint("g_hbm", [NH * (NQT + 1) * 128 * 257])
    dint("s_hbm", [NH, NROW + 192])
    dint("w2_hbm", [8 * 128 * 256], BF16)
    dint("wq1_hbm", [D * DFF], BF16)
    dint("wq2_hbm", [DFF * D], BF16)
    dint("cc_in1", [1, 2048]); dint("cc_out1", [8, 2048], space="Shared")
    dint("cc_in2", [1, 4096]); dint("cc_out2", [8, 4096], space="Shared")
    dint("misc_hbm", [16384])
    dint("spill_hbm", [4 * 32 * 128, 512], BF16)
    dint("misc2_hbm", [16384], BF16)
    dint("gp_hbm", [128, 2])
    dint("wmoe_hbm", [16, 2048], BF16)

    PS = dict(
        A=stack.enter_context(nc.psum_tensor("psA", [128, 1024], F32)),
        B=stack.enter_context(nc.psum_tensor("psB", [128, 1024], F32)),
        C=stack.enter_context(nc.psum_tensor("psC", [128, 1024], F32)),
        D=stack.enter_context(nc.psum_tensor("psD", [128, 1024], F32)),
    )

    with nc.Block() as blk:
        ctx.blk = blk
        emit(ctx, dram, PS)
    stack.close()
    return nc


NB = 8



def emit_ln(ctx, PS, st, pfx, XN):
    nc, blk = ctx.nc, ctx.blk
    onesb_col, onesb_row = st["onesb_col"], st["onesb_row"]
    XB = ctx.at(R1_KB, 8 * 512, BF16)
    SQ = ctx.at(R1_KB + 8, 8 * 512, BF16)
    SRO_ = ctx.at(R1_KB + 16, 6 * 512)[0:1, :]
    SRO = [SRO_[:, i * 512:(i + 1) * 512] for i in range(6)]
    ABH_ = ctx.at(R1_KB + 28, 4 * 512, BF16)[0:1, :]
    ABH = [ABH_[:, i * 512:(i + 1) * 512] for i in range(4)]
    TMP = ctx.at(R1_KB + 32, 512)
    NT = NTOK // 512

    s_pre = ctx.sem(pfx + "_pre")
    s_mm = ctx.sem(pfx + "_mm")
    s_ab = ctx.sem(pfx + "_ab")
    s_bc = ctx.sem(pfx + "_bc")
    s_ap = ctx.sem(pfx + "_ap")

    def bank_of(tt):
        return PS["A"] if tt % 2 == 0 else PS["B"]

    def pcd(tt):
        pa = PS["C"][:, 0:512] if tt % 2 == 0 else PS["C"][:, 512:1024]
        pb = PS["D"][:, 0:512] if tt % 2 == 0 else PS["D"][:, 512:1024]
        return pa, pb

    @blk.vector
    def _(v):
        for tt in range(NT + 1):
            if tt >= 1:
                j = tt - 1
                ctx.sems[pfx + "_mm"].wge(v, j + 1)
                bank = bank_of(j)
                v.drain()
                v.tensor_copy(SRO[0], bank[0:1, 0:512])
                v.drain()
                v.tensor_copy(SRO[1], bank[0:1, 512:1024])
                v.drain()
                v.tensor_scalar(out=SRO[0], in0=SRO[0], scalar1=1.0 / D, scalar2=None, op0=OP.mult)
                v.tensor_scalar(out=SRO[1], in0=SRO[1], scalar1=1.0 / D, scalar2=None, op0=OP.mult)
                v.drain()
                v.tensor_tensor(out=SRO[2], in0=SRO[0], in1=SRO[0], op=OP.mult)
                v.drain()
                v.tensor_tensor(out=SRO[1], in0=SRO[1], in1=SRO[2], op=OP.subtract)
                v.drain()
                v.tensor_scalar(out=SRO[1], in0=SRO[1], scalar1=1e-5, scalar2=None, op0=OP.add)
                ctx.fence(v, "vector", s_ab)
                ctx.sems[pfx + "_ab"].wge(v, 3 * j + 2)
                v.reciprocal(SRO[3], SRO[2])
                v.drain()
                v.tensor_tensor(out=SRO[4], in0=SRO[0], in1=SRO[3], op=OP.mult)
                v.drain()
                v.tensor_scalar(out=SRO[4], in0=SRO[4], scalar1=-1.0, scalar2=None, op0=OP.mult)
                v.tensor_copy(ABH[0], SRO[3])
                v.drain()
                v.tensor_copy(ABH[2], SRO[4])
                v.drain()
                v.tensor_copy(SRO[5], ABH[0])
                v.drain()
                v.tensor_tensor(out=SRO[5], in0=SRO[3], in1=SRO[5], op=OP.subtract)
                v.drain()
                v.tensor_copy(ABH[1], SRO[5])
                v.drain()
                v.tensor_copy(SRO[5], ABH[2])
                v.drain()
                v.tensor_tensor(out=SRO[5], in0=SRO[4], in1=SRO[5], op=OP.subtract)
                v.drain()
                v.tensor_copy(ABH[3], SRO[5])
                ctx.fence(v, "vector", s_ab)
                ctx.sems[pfx + "_bc"].wge(v, j + 1)
                pa, pb = pcd(j)
                o = j * 512
                for dc in range(8):
                    v.tensor_tensor(out=TMP, in0=xf_ap(ctx, dc, HALO + o, 512), in1=pa, op=OP.mult)
                    v.drain()
                    s_ap.inc(v.tensor_tensor(out=XN[:, dc * NTOK + o:dc * NTOK + o + 512],
                                             in0=TMP, in1=pb, op=OP.add))
                    v.drain()
            if tt < NT:
                o = tt * 512
                if tt >= 1:
                    ctx.sems[pfx + "_mm"].wge(v, tt)
                for dc in range(8):
                    v.tensor_copy(XB[:, dc * 512:dc * 512 + 512], xf_ap(ctx, dc, HALO + o, 512))
                ctx.fence(v, "vector", s_pre)

    @blk.scalar
    def _(s):
        act_preload(s, "sqrt_and_others")
        for tt in range(NT + 1):
            if tt >= 1:
                j = tt - 1
                ctx.sems[pfx + "_ab"].wge(s, 3 * j + 1)
                s.drain()
                s.activation(out=SRO[2], in_=SRO[1], func=AF.Sqrt)
                ctx.fence(s, "scalar", s_ab)
            if tt < NT:
                o = tt * 512
                if tt >= 1:
                    ctx.sems[pfx + "_mm"].wge(s, tt)
                for dc in range(8):
                    s.activation(out=SQ[:, dc * 512:dc * 512 + 512], in_=xf_ap(ctx, dc, HALO + o, 512), func=AF.Square)
                ctx.fence(s, "scalar", s_pre)

    @blk.tensor
    def _(te):
        for tt in range(NT + 1):
            if tt >= 1:
                j = tt - 1
                ctx.sems[pfx + "_ab"].wge(te, 3 * j + 3)
                if j >= 2:
                    ctx.sems[pfx + "_ap"].wge(te, 8 * (j - 1))
                pa, pb = pcd(j)
                nc.tensor.matmul(pa, onesb_row, ABH[0], start=True, stop=False)
                nc.tensor.matmul(pa, onesb_row, ABH[1], start=False, stop=True)
                nc.tensor.matmul(pb, onesb_row, ABH[2], start=True, stop=False)
                s_bc.inc(nc.tensor.matmul(pb, onesb_row, ABH[3], start=False, stop=True))
            if tt < NT:
                ctx.sems[pfx + "_pre"].wge(te, 2 * (tt + 1))
                if tt >= 2:
                    ctx.sems[pfx + "_ab"].wge(te, 3 * (tt - 2) + 1)
                bank = bank_of(tt)
                for dc in range(8):
                    nc.tensor.matmul(bank[0:1, 0:512], onesb_col, XB[:, dc * 512:dc * 512 + 512],
                                     start=(dc == 0), stop=(dc == 7))
                for dc in range(8):
                    last = nc.tensor.matmul(bank[0:1, 512:1024], onesb_col, SQ[:, dc * 512:dc * 512 + 512],
                                            start=(dc == 0), stop=(dc == 7))
                s_mm.inc(last)


def phaseC(ctx, dram, PS, st):
    nc, blk = ctx.nc, ctx.blk
    SM = st["SM"]
    onesb_col, onesb_row = st["onesb_col"], st["onesb_row"]

    WQ1 = ctx.at(R2_KB, 8 * 4096, BF16)
    XN2 = ctx.at(R2_KB + 64, 8 * NTOK, BF16)
    WSTR = [ctx.at(R1_KB + 8 * i, 2048) for i in range(2)]
    ABSD = [ctx.at(R1_KB + 16 + 8 * b, 2048) for b in range(2)]
    ABSF = ctx.at(R1_KB + 16, 2048)
    GP = ctx.at(SM_KB, 1024)[:, 344:392]
    CR = ctx.at(R2_KB + 96, 768)[0:1, :]
    SGNB = ctx.at(R1_KB + 24, 2048, BF16)
    MSKB = ctx.at(R1_KB + 28, 2048, BF16)
    QOUT = ctx.at(R1_KB + 32, 1024, BF16)
    HALL = ctx.at(R1_KB, 32 * 256, BF16)
    W2D = [ctx.at(R1_KB + 16 + 8 * i, 4096, BF16) for i in range(2)]
    GB = ctx.at(SM_KB + 2, 256, BF16)
    b1t = SM[:, 200:232]
    b2t = SM[:, 232:240]
    gsc = SM[:, 240:248]

    s_w = ctx.sem("F_w")
    s_gp = ctx.sem("F_gp")
    s_gam = ctx.sem("F_gam")
    s_qv = ctx.sem("F_qv")
    s_qs = ctx.sem("F_qs")
    s_qst = ctx.sem("F_qst", pin=True)
    s_h = ctx.sem("F_h")
    s_he = ctx.sem("F_he")
    s_o = ctx.sem("F_o")
    s_oe = ctx.sem("F_oe")
    s_w2a = ctx.sem("F_w2a")
    s_w2b = ctx.sem("F_w2b")

    w1t = dram["f0w1T"]
    w2t = dram["f0w2T"]
    wq2flat = dram["wq2_hbm"]

    chunks = []
    for dc in range(8):
        for half in range(2):
            chunks.append(("w1", dc, half, 2048,
                           bass.AP(w1t.tensor, dc * 128 * 4096 + half * 2048, [[4096, 128], [1, 2048]])))
    for i in range(32):
        chunks.append(("w2", i, 0, 1024,
                       bass.AP(w2t.tensor, i * 128 * 1024, [[1024, 128], [1, 1024]])))
    NCH = len(chunks)

    @blk.gpsimd
    def _(g):
        ctx.dma(g, "gpsimd", b1t, dram["f0b1"])
        ctx.dma(g, "gpsimd", b2t, dram["f0b2"])
        g.wait_ge(ctx.dsem["gpsimd"].h, ctx.dsem["gpsimd"].v)

    @blk.sync
    def _(e):
        for i, (kind, a, b, w, ap) in enumerate(chunks):
            if i >= 2:
                ctx.sems["F_qv"].wge(e, i - 1)
            ctx.dma(e, "sync", WSTR[i % 2][:, 0:w], ap)
            e.wait_ge(ctx.dsem["sync"].h, ctx.dsem["sync"].v)
            ctx.fence(e, "sync", s_w)

    @blk.scalar
    def _(s):
        for i, (kind, a, b, w, ap) in enumerate(chunks):
            ctx.sems["F_w"].wge(s, i + 1)
            if i >= 2:
                ctx.sems["F_gp"].wge(s, i - 1)
            s.drain()
            s_qv.inc(s.activation(out=ABSD[i % 2][:, 0:w], in_=WSTR[i % 2][:, 0:w], func=AF.Abs))

    @blk.vector
    def _(v):
        for i, (kind, a, b, w, ap) in enumerate(chunks):
            ctx.sems["F_qv"].wge(v, i + 1)
            v.drain()
            s_gp.inc(v.tensor_reduce(GP[:, i:i + 1], ABSD[i % 2][:, 0:w], axis=AX.X, op=OP.add))
        v.drain()
        v.tensor_reduce(gsc[:, 6:7], GP[:, 0:16], axis=AX.X, op=OP.add)
        v.drain()
        v.tensor_reduce(gsc[:, 7:8], GP[:, 16:NCH], axis=AX.X, op=OP.add)
        ctx.fence(v, "vector", s_gam)

    @blk.sync
    def _(e):
        ctx.sems["F_gam"].wge(e, 1)
        ctx.dma(e, "sync", dram["gp_hbm"], gsc[:, 6:8])
        e.wait_ge(ctx.dsem["sync"].h, ctx.dsem["sync"].v)
        ctx.dma(e, "sync", CR[:, 0:256], dram["gp_hbm"].rearrange("p q -> (p q)").rearrange("(a b) -> a b", a=1))
        e.wait_ge(ctx.dsem["sync"].h, ctx.dsem["sync"].v)
        ctx.fence(e, "sync", s_gam)

    @blk.vector
    def _(v):
        T, S1, S2, CC = CR[:, 0:256], CR[:, 256:384], CR[:, 384:512], CR[:, 512:640]
        ctx.sems["F_gam"].wge(v, 2)
        v.drain()
        v.tensor_copy(S1.rearrange("a (p q) -> a p q", q=1),
                      T.rearrange("a (p q) -> a p q", q=2)[:, :, 0:1])
        v.drain()
        v.tensor_copy(S2.rearrange("a (p q) -> a p q", q=1),
                      T.rearrange("a (p q) -> a p q", q=2)[:, :, 1:2])
        for t, S in ((0, S1), (1, S2)):
            v.drain()
            v.memset(CC, 0.0)
            m = 128
            while m > 1:
                h = m // 2
                a, b2 = S[:, 0:h], S[:, h:m]
                v.drain()
                v.tensor_tensor(out=T[:, 0:h], in0=a, in1=b2, op=OP.add)
                v.drain()
                v.tensor_tensor(out=T[:, h:2 * h], in0=T[:, 0:h], in1=a, op=OP.subtract)
                v.drain()
                v.tensor_tensor(out=T[:, 2 * h:3 * h], in0=T[:, 0:h], in1=T[:, h:2 * h], op=OP.subtract)
                v.drain()
                v.tensor_tensor(out=T[:, 2 * h:3 * h], in0=a, in1=T[:, 2 * h:3 * h], op=OP.subtract)
                v.drain()
                v.tensor_tensor(out=T[:, 3 * h:4 * h], in0=b2, in1=T[:, h:2 * h], op=OP.subtract)
                v.drain()
                v.tensor_tensor(out=T[:, 2 * h:3 * h], in0=T[:, 2 * h:3 * h], in1=T[:, 3 * h:4 * h], op=OP.add)
                v.drain()
                v.tensor_tensor(out=CC[:, 0:h], in0=CC[:, 0:h], in1=CC[:, h:m], op=OP.add)
                v.drain()
                v.tensor_tensor(out=CC[:, 0:h], in0=CC[:, 0:h], in1=T[:, 2 * h:3 * h], op=OP.add)
                v.drain()
                v.tensor_copy(S[:, 0:h], T[:, 0:h])
                m = h
            v.drain()
            v.tensor_tensor(out=S[:, 0:1], in0=S[:, 0:1], in1=CC[:, 0:1], op=OP.add)
            v.drain()
            v.tensor_scalar(out=S[:, 1:2], in0=S[:, 0:1], scalar1=2.0 ** -22, scalar2=1e-5,
                            op0=OP.mult, op1=OP.add)
            v.drain()
            v.tensor_scalar(out=gsc[0:1, 2 + t:3 + t], in0=S[:, 1:2], scalar1=0.5, scalar2=None, op0=OP.mult)
        v.drain()
        v.tensor_copy(GB[0:1, 144:146], gsc[0:1, 2:4])
        v.drain()
        v.tensor_copy(gsc[0:1, 6:8], GB[0:1, 144:146])
        v.drain()
        v.tensor_tensor(out=gsc[0:1, 6:8], in0=gsc[0:1, 2:4], in1=gsc[0:1, 6:8], op=OP.subtract)
        v.drain()
        v.tensor_copy(GB[0:1, 146:148], gsc[0:1, 6:8])
        v.drain()
        v.tensor_copy(CR[:, 640:642], GB[0:1, 146:148])
        v.drain()
        v.tensor_tensor(out=gsc[0:1, 6:8], in0=gsc[0:1, 6:8], in1=CR[:, 640:642], op=OP.subtract)
        v.drain()
        v.tensor_copy(GB[0:1, 148:150], gsc[0:1, 6:8])
        ctx.fence(v, "vector", s_gam)

    @blk.tensor
    def _(te):
        ctx.sems["F_gam"].wge(te, 3)
        for t in range(2):
            nc.tensor.matmul(PS["B"][:, t:t + 1], onesb_row, GB[0:1, 144 + t:145 + t], start=True, stop=False)
            nc.tensor.matmul(PS["B"][:, t:t + 1], onesb_row, GB[0:1, 146 + t:147 + t], start=False, stop=False)
            last = nc.tensor.matmul(PS["B"][:, t:t + 1], onesb_row, GB[0:1, 148 + t:149 + t], start=False, stop=True)
        s_gam.inc(last)

    @blk.vector
    def _(v):
        ctx.sems["F_gam"].wge(v, 4)
        v.tensor_copy(gsc[:, 4:6], PS["B"][:, 0:2])
        ctx.fence(v, "vector", s_gam)

    @blk.sync
    def _(e):
        def store(a):
            ctx.sems["F_qv"].wge(e, NCH + 16 + a + 1)
            ctx.sems["F_qst"].wge(e, 16 * a)
            dst = bass.AP(wq2flat.tensor, a * 128, [[4096, 128], [524288, 8], [1, 128]])
            s_qst.inc(e.dma_start(out=dst, in_=QOUT[:, 0:1024].rearrange("p (a b) -> p a b", a=8)), 16)
        for i in range(NCH):
            kind, a, b, w, ap = chunks[i]
            if i >= 1:
                ctx.sems["F_qv"].wge(e, NCH + i - 1)
            ctx.dma(e, "sync", WSTR[i % 2][:, 0:w], ap)
            e.wait_ge(ctx.dsem["sync"].h, ctx.dsem["sync"].v)
            ctx.fence(e, "sync", s_w)
            if i >= 17:
                store(i - 17)
        for a2 in range(NCH - 17, 32):
            store(a2)
        ctx.sems["F_qst"].wge(e, 16 * 32)

    @blk.scalar
    def _(s):
        ctx.sems["F_gam"].wge(s, 5)
        for i in range(NCH):
            kind, a, b, w, ap = chunks[i]
            ctx.sems["F_w"].wge(s, NCH + i + 1)
            if i >= 1:
                ctx.sems["F_qv"].wge(s, NCH + i)
            s.drain()
            s_qs.inc(s.activation(out=SGNB[:, 0:w], in_=WSTR[i % 2][:, 0:w], func=AF.Sign))

    @blk.vector
    def _(v):
        for i in range(NCH):
            kind, a, b, w, ap = chunks[i]
            ctx.sems["F_qs"].wge(v, i + 1)
            thr = gsc[:, 4:5] if kind == "w1" else gsc[:, 5:6]
            v.drain()
            v.tensor_scalar(out=ABSF[:, 0:w], in0=WSTR[i % 2][:, 0:w], scalar1=-1.0, scalar2=None, op0=OP.mult)
            v.drain()
            v.tensor_tensor(out=ABSF[:, 0:w], in0=ABSF[:, 0:w], in1=WSTR[i % 2][:, 0:w], op=OP.max)
            v.drain()
            v.tensor_scalar(out=MSKB[:, 0:w], in0=ABSF[:, 0:w], scalar1=thr, scalar2=None, op0=OP.is_ge)
            v.drain()
            if kind == "w1":
                dst = WQ1[:, a * 4096 + b * 2048: a * 4096 + b * 2048 + 2048]
                s_qv.inc(v.tensor_tensor(out=dst, in0=SGNB[:, 0:w], in1=MSKB[:, 0:w], op=OP.mult))
            else:
                ctx.sems["F_qst"].wge(v, 16 * a)
                s_qv.inc(v.tensor_tensor(out=QOUT[:, 0:w], in0=SGNB[:, 0:w], in1=MSKB[:, 0:w], op=OP.mult))
            v.drain()

    ctx.tap("wq1", WQ1)
    ctx.barrier()
    if PHASES == 35:
        return

    emit_ln(ctx, PS, st, "LN2", XN2)
    ctx.barrier()

    ctx.tap("xn2", XN2)
    NTT = NTOK // 256
    @blk.sync
    def _(e):
        for tt in range(NTT):
            for dout in range(8):
                k = tt * 8 + dout
                if k >= 2:
                    ctx.sems["F_o"].wge(e, k - 1)
                ctx.dma(e, "sync", W2D[k % 2], bass.AP(wq2flat.tensor, dout * 524288, [[4096, 128], [1, 4096]]))
                e.wait_ge(ctx.dsem["sync"].h, ctx.dsem["sync"].v)
                ctx.fence(e, "sync", s_w2a)

    @blk.tensor
    def _(te):
        for tt in range(NTT):
            o = tt * 256
            for f in range(32):
                k = tt * 32 + f
                hb = PS["C"][:, 0:256] if f % 2 == 0 else PS["C"][:, 512:768]
                if k >= 2:
                    ctx.sems["F_he"].wge(te, k - 1)
                for dc in range(8):
                    last = nc.tensor.matmul(hb, WQ1[:, dc * 4096 + f * 128:dc * 4096 + f * 128 + 128],
                                            XN2[:, dc * NTOK + o:dc * NTOK + o + 256],
                                            start=(dc == 0), stop=(dc == 7))
                s_h.inc(last)
            ctx.sems["F_he"].wge(te, (tt + 1) * 32)
            for dout in range(8):
                kd = tt * 8 + dout
                ctx.sems["F_w2a"].wge(te, kd + 1)
                if kd >= 2:
                    ctx.sems["F_oe"].wge(te, kd - 1)
                ob = PS["A"][:, 0:256] if dout % 2 == 0 else PS["A"][:, 512:768]
                for f in range(32):
                    last = nc.tensor.matmul(ob, W2D[kd % 2][:, f * 128:f * 128 + 128],
                                            HALL[:, f * 256:f * 256 + 256],
                                            start=(f == 0), stop=(f == 31))
                s_o.inc(last)

    @blk.scalar
    def _(s):
        for tt in range(NTT):
            if tt >= 1:
                ctx.sems["F_o"].wge(s, tt * 8)
            for f in range(32):
                k = tt * 32 + f
                ctx.sems["F_h"].wge(s, k + 1)
                hb = PS["C"][:, 0:256] if f % 2 == 0 else PS["C"][:, 512:768]
                s.drain()
                s_he.inc(s.activation(out=HALL[:, f * 256:f * 256 + 256], in_=hb,
                                      func=AF.Relu, bias=b1t[:, f:f + 1], scale=1.0))

    @blk.vector
    def _(v):
        for tt in range(NTT):
            o = tt * 256
            for dout in range(8):
                kd = tt * 8 + dout
                ctx.sems["F_o"].wge(v, kd + 1)
                ob = PS["A"][:, 0:256] if dout % 2 == 0 else PS["A"][:, 512:768]
                s_oe.inc(v.scalar_tensor_tensor(out=xf_ap(ctx, dout, HALO + o, 256), in0=ob,
                                                scalar=b2t[:, dout:dout + 1], in1=xf_ap(ctx, dout, HALO + o, 256),
                                                op0=OP.add, op1=OP.add))
                v.drain()




CONSTS = (1.5707963267948966, -44.0, 44.0)


def act_preload(s, setname):
    from concourse.hw_specs import get_activation_tables
    names = list(get_activation_tables(s.bass.m.arch).keys())
    return s.add_instruction(mybir.InstLoadActFuncSet(
        name=s.bass.get_next_instruction_name(),
        act_func_set_id=names.index(setname), ins=[], outs=[]))
EXPB = 44.0
EMB = 7.781132241133797e-20


def phaseD(ctx, dram, PS, st):
    nc, blk = ctx.nc, ctx.blk
    SM = st["SM"]

    XN3 = ctx.at(R2_KB, 8 * NTOK, BF16)
    XH = ctx.at(R2_KB + 32, 16 * 512, BF16)
    GX = ctx.at(R2_KB + 48, 8 * 512, BF16)
    GA = ctx.at(R2_KB + 56, 8 * 512)
    PL = [ctx.at(R2_KB + 72 + 2 * i, 512) for i in range(12)]
    HMB = ctx.at(R2_KB + 32, 16 * 512, BF16)
    RLB = [ctx.at(R2_KB + 48 + 4 * i, 2048, BF16) for i in range(2)]
    QALL = ctx.at(R1_KB, 1056)
    QSTr, QSTi = QALL[:, 0:264], QALL[:, 264:528]
    CLOr, CLOi = QALL[:, 528:792], QALL[:, 792:1056]
    MISC = ctx.at(R1_KB + 5, 368)
    T8 = MISC[:, 0:64]
    SUMT = MISC[:, 64:96]
    CIN = MISC[:, 96:112]
    TRT = [MISC[:, 112 + 32 * r:144 + 32 * r] for r in range(8)]
    OZ = ctx.at(R1_KB + 7, 66)
    ONES64, ZCOL = OZ[:, 0:64], OZ[:, 64:65]
    MSPT = ctx.at(R1_KB + 8, 512)
    AIMT = ctx.at(R1_KB + 10, 512)
    WGX = ctx.at(R1_KB + 12, 2048, BF16)
    WGA = ctx.at(R1_KB + 16, 2048, BF16)
    WINB = [ctx.at(R1_KB + 20 + 2 * i, 1024, BF16) for i in range(2)]
    SPB = [ctx.at(R1_KB + 24 + 4 * i, 2048, BF16) for i in range(2)]

    b_in = SM[:, 256:272]
    bgx = SM[:, 272:280]
    bga = SM[:, 280:288]
    msp = SM[:, 288:296]
    aim = SM[:, 296:304]
    b_out = SM[:, 304:312]
    smt = SM[:, 312:320]
    stmp = SM[:, 320:328]

    s_sml = ctx.sem("D_sml")
    s_dw = ctx.sem("D_dw")
    s_pre2 = ctx.sem("D_pre2")
    s_xh = ctx.sem("D_xh")
    s_xhe = ctx.sem("D_xhe")
    s_gmm = ctx.sem("D_gmm")
    s_ge = ctx.sem("D_ge")
    s_v1 = ctx.sem("D_v1")
    s_s1 = ctx.sem("D_s1")
    s_v2 = ctx.sem("D_v2")
    s_s2 = ctx.sem("D_s2")
    s_v3 = ctx.sem("D_v3")
    s_sp = ctx.sem("D_sp", pin=True)
    s_sum = ctx.sem("D_sum")
    s_cc = ctx.sem("D_cc", pin=True)
    s_cin = ctx.sem("D_cin")
    s_rl = ctx.sem("D_rl")
    s_h = ctx.sem("D_h")
    s_wo = ctx.sem("D_wo")
    s_om = ctx.sem("D_om")
    s_oe = ctx.sem("D_oe")

    @blk.gpsimd
    def _(g):
        for dst, key in ((b_in, "b_in"), (bgx, "bgx"), (bga, "bga"), (msp, "arp"),
                         (aim, "aim"), (b_out, "b_out"), (smt, "rmt")):
            ctx.dma(g, "gpsimd", dst, dram[key])
        g.wait_ge(ctx.dsem["gpsimd"].h, ctx.dsem["gpsimd"].v)
        ctx.fence(g, "gpsimd", s_sml)

    @blk.scalar
    def _(s):
        s_sml.wait(s)
        s.drain()
        act_preload(s, "natural_log_exp_and_others")
        s.activation(out=stmp, in_=msp, func=AF.Exp)
        s.drain()
        s.activation(out=msp, in_=stmp, func=AF.Ln, bias=1.0, scale=1.0)
        ctx.fence(s, "scalar", s_sml)

    @blk.vector
    def _(v):
        ctx.sems["D_sml"].wge(v, 2)
        v.drain()
        v.tensor_scalar(out=msp, in0=msp, scalar1=-8.0, scalar2=None, op0=OP.mult)
        ctx.fence(v, "vector", s_sml)

    emit_ln(ctx, PS, st, "LN3", XN3)
    ctx.barrier()

    @blk.gpsimd
    def _(g):
        ctx.dma(g, "gpsimd", WGX.rearrange("p (a b) -> p a b", a=16),
                dram["wgx"].rearrange("(a p) c -> p a c", p=128))
        ctx.dma(g, "gpsimd", WGA.rearrange("p (a b) -> p a b", a=16),
                dram["wga"].rearrange("(a p) c -> p a c", p=128))
        g.wait_ge(ctx.dsem["gpsimd"].h, ctx.dsem["gpsimd"].v)
        ctx.fence(g, "gpsimd", s_pre2)

    @blk.vector
    def _(v):
        v.memset(ONES64, 1.0)
        v.memset(ZCOL, 0.0)
        v.memset(QSTr[:, 0:8], 1.0)
        v.memset(QSTi[:, 0:8], 0.0)
        v.memset(CLOr[:, 0:8], 0.0)
        v.memset(CLOi[:, 0:8], 0.0)
        v.drain()
        for g in range(8):
            v.tensor_scalar(out=MSPT[:, g * 64:(g + 1) * 64], in0=ONES64,
                            scalar1=msp[:, g:g + 1], scalar2=None, op0=OP.mult)
            v.tensor_scalar(out=AIMT[:, g * 64:(g + 1) * 64], in0=ONES64,
                            scalar1=aim[:, g:g + 1], scalar2=None, op0=OP.mult)
        ctx.fence(v, "vector", s_pre2)
    ctx.barrier()

    w_in_r = dram["w_in"].rearrange("(a p) c -> p a c", p=128)

    @blk.gpsimd
    def _(g):
        for stt in range(4):
            for cc in range(16):
                k = stt * 16 + cc
                if k >= 2:
                    ctx.sems["D_xh"].wge(g, k - 1)
                ctx.dma(g, "gpsimd", WINB[k % 2].rearrange("p (a b) -> p a b", a=8),
                        w_in_r[:, :, cc * 128:(cc + 1) * 128])
                g.wait_ge(ctx.dsem["gpsimd"].h, ctx.dsem["gpsimd"].v)
                ctx.fence(g, "gpsimd", s_dw)
            for k in range(8):
                gk = stt * 8 + k
                ctx.sems["D_v3"].wge(g, gk + 1)
                sb = SPB[gk % 2]
                for q in range(4):
                    s_sp.inc(g.dma_start(
                        out=dram["spill_hbm"][(q * 32 + gk) * 128:(q * 32 + gk + 1) * 128, :],
                        in_=sb[:, q * 512:(q + 1) * 512]), 16)
                ctx.sems["D_sp"].wge(g, 64 * (gk + 1))
        ctx.sems["D_sum"].wge(g, 1)
        ctx.dma(g, "gpsimd", dram["cc_in2"].rearrange("a (p f) -> (a p) f", p=128), SUMT)
        g.wait_ge(ctx.dsem["gpsimd"].h, ctx.dsem["gpsimd"].v)
        s_cc.inc(g.collective_compute("AllGather", OP.bypass, replica_groups=[[0, 1, 2, 3, 4, 5, 6, 7]],
                                      ins=[dram["cc_in2"][:]], outs=[dram["cc_out2"][:]]))
        ctx.sems["D_cc"].wge(g, 1)
        for r in range(8):
            ctx.dma(g, "gpsimd", TRT[r], dram["cc_out2"][r:r + 1, :].rearrange("a (p f) -> (a p) f", p=128))
        g.wait_ge(ctx.dsem["gpsimd"].h, ctx.dsem["gpsimd"].v)
        ctx.fence(g, "gpsimd", s_cc)

    @blk.tensor
    def _(te):
        for stt in range(4):
            for cc in range(16):
                k = stt * 16 + cc
                ctx.sems["D_dw"].wge(te, k + 1)
                if k >= 2:
                    ctx.sems["D_xhe"].wge(te, k - 1)
                bank = PS["A"][:, 0:512] if k % 2 == 0 else PS["A"][:, 512:1024]
                for dc in range(8):
                    last = nc.tensor.matmul(bank, WINB[k % 2][:, dc * 128:dc * 128 + 128],
                                            XN3[:, dc * NTOK + stt * 512:dc * NTOK + stt * 512 + 512],
                                            start=(dc == 0), stop=(dc == 7))
                s_xh.inc(last)
            ctx.sems["D_xhe"].wge(te, stt * 16 + 16)
            for j in range(8):
                kj = stt * 8 + j
                if kj >= 2:
                    ctx.sems["D_ge"].wge(te, 2 * kj - 2)
                bx = PS["C"][:, 0:512] if j % 2 == 0 else PS["C"][:, 512:1024]
                ba = PS["D"][:, 0:512] if j % 2 == 0 else PS["D"][:, 512:1024]
                for bc in range(2):
                    nc.tensor.matmul(bx, WGX[:, (j * 2 + bc) * 128:(j * 2 + bc) * 128 + 128],
                                     XH[:, (j * 2 + bc) * 512:(j * 2 + bc) * 512 + 512],
                                     start=(bc == 0), stop=(bc == 1))
                for bc in range(2):
                    last = nc.tensor.matmul(ba, WGA[:, (j * 2 + bc) * 128:(j * 2 + bc) * 128 + 128],
                                            XH[:, (j * 2 + bc) * 512:(j * 2 + bc) * 512 + 512],
                                            start=(bc == 0), stop=(bc == 1))
                s_gmm.inc(last)
            if stt < 3:
                ctx.sems["D_v3"].wge(te, 8 * (stt + 1))

    @blk.scalar
    def _(s):
        for stt in range(4):
            for cc in range(16):
                k = stt * 16 + cc
                ctx.sems["D_xh"].wge(s, k + 1)
                bank = PS["A"][:, 0:512] if k % 2 == 0 else PS["A"][:, 512:1024]
                s.drain()
                s_xhe.inc(s.activation(out=XH[:, cc * 512:(cc + 1) * 512], in_=bank,
                                       func=AF.Identity, bias=b_in[:, cc:cc + 1], scale=1.0))
            act_preload(s, "sigmoid_and_others")
            for j in range(8):
                kj = stt * 8 + j
                ctx.sems["D_gmm"].wge(s, kj + 1)
                bx = PS["C"][:, 0:512] if j % 2 == 0 else PS["C"][:, 512:1024]
                ba = PS["D"][:, 0:512] if j % 2 == 0 else PS["D"][:, 512:1024]
                s.drain()
                s_ge.inc(s.activation(out=GX[:, j * 512:(j + 1) * 512], in_=bx,
                                      func=AF.Sigmoid, bias=bgx[:, j:j + 1], scale=1.0))
                s.drain()
                s_ge.inc(s.activation(out=GA[:, j * 512:(j + 1) * 512], in_=ba,
                                      func=AF.Sigmoid, bias=bga[:, j:j + 1], scale=1.0))
            for k in range(8):
                gk = stt * 8 + k
                ctx.sems["D_v1"].wge(s, gk + 1)
                s.drain()
                act_preload(s, "silu_and_others")
                s.activation(out=PL[7], in_=PL[4], func=AF.Sin)
                s.drain()
                s.activation(out=PL[8], in_=PL[5], func=AF.Sin, scale=-1.0, bias=1.5707963267948966)
                s.drain()
                act_preload(s, "natural_log_exp_and_others")
                s.activation(out=PL[6], in_=PL[0], func=AF.Exp, scale=2.0)
                s.drain()
                s.activation(out=PL[9], in_=PL[1], func=AF.Exp, scale=-1.0, bias=-EXPB)
                s.drain()
                s.activation(out=PL[10], in_=PL[1], func=AF.Exp, scale=1.0, bias=EXPB)
                s.drain()
                s.activation(out=T8[:, 48:56].rearrange("p (a b) -> p a b", b=1),
                             in_=PL[1].rearrange("p (a b) -> p a b", b=64)[:, :, 63:64],
                             func=AF.Exp)
                ctx.fence(s, "scalar", s_s1)
                ctx.sems["D_v2"].wge(s, gk + 1)
                s.drain()
                s.activation(out=PL[6], in_=PL[0], func=AF.Ln)
                s.drain()
                s.activation(out=PL[6], in_=PL[6], func=AF.Exp, scale=0.5)
                ctx.fence(s, "scalar", s_s2)

    def GAc(g, k):
        return GA[:, g * 512 + k * 64:g * 512 + k * 64 + 64]

    def GXc(g, k):
        return GX[:, g * 512 + k * 64:g * 512 + k * 64 + 64]

    def XHc(cc, k):
        return XH[:, cc * 512 + k * 64:cc * 512 + k * 64 + 64]

    def ends(p):
        return p.rearrange("p (a b) -> p a b", b=64)[:, :, 63:64]

    def t8(a, b):
        return T8[:, a:b].rearrange("p (a b) -> p a b", b=1)

    @blk.vector
    def _(v):
        for stt in range(4):
            for k in range(8):
                gk = stt * 8 + k
                ctx.sems["D_ge"].wge(v, stt * 16 + 16)
                if gk >= 2:
                    ctx.sems["D_sp"].wge(v, 64 * (gk - 1))
                v.drain()
                GA3 = GA.rearrange("p (g t) -> p g t", t=512)[:, :, k * 64:(k + 1) * 64]
                v.tensor_tensor(out=PL[0].rearrange("p (g t) -> p g t", t=64), in0=GA3,
                                in1=MSPT.rearrange("p (g t) -> p g t", t=64), op=OP.mult)
                v.tensor_tensor(out=PL[2].rearrange("p (g t) -> p g t", t=64), in0=GA3,
                                in1=AIMT.rearrange("p (g t) -> p g t", t=64), op=OP.mult)
                v.drain()
                for g in range(8):
                    v.tensor_tensor_scan(PL[1][:, g * 64:(g + 1) * 64], ONES64,
                                         PL[0][:, g * 64:(g + 1) * 64], ZCOL, OP.mult, OP.add)
                    v.tensor_tensor_scan(PL[3][:, g * 64:(g + 1) * 64], ONES64,
                                         PL[2][:, g * 64:(g + 1) * 64], ZCOL, OP.mult, OP.add)
                v.drain()
                v.tensor_scalar(out=PL[4], in0=PL[3], scalar1=0.15915494309189535, scalar2=None, op0=OP.mult)
                v.drain()
                v.tensor_scalar(out=PL[5], in0=PL[4], scalar1=0.5, scalar2=None, op0=OP.is_ge)
                v.drain()
                v.scalar_tensor_tensor(out=PL[5], in0=PL[4], scalar=1.5, in1=PL[5], op0=OP.is_ge, op1=OP.add)
                v.drain()
                v.scalar_tensor_tensor(out=PL[5], in0=PL[4], scalar=2.5, in1=PL[5], op0=OP.is_ge, op1=OP.add)
                v.drain()
                v.scalar_tensor_tensor(out=PL[5], in0=PL[4], scalar=3.5, in1=PL[5], op0=OP.is_ge, op1=OP.add)
                v.drain()
                v.scalar_tensor_tensor(out=PL[4], in0=PL[5], scalar=-6.283185307179586, in1=PL[3],
                                       op0=OP.mult, op1=OP.add)
                v.drain()
                v.tensor_scalar(out=PL[4], in0=PL[4], scalar1=3.1415925, scalar2=-3.1415925,
                                op0=OP.min, op1=OP.max)
                v.drain()
                v.tensor_scalar(out=PL[5], in0=PL[4], scalar1=-1.0, scalar2=None, op0=OP.mult)
                v.drain()
                v.tensor_tensor(out=PL[5], in0=PL[5], in1=PL[4], op=OP.max)
                ctx.fence(v, "vector", s_v1)
                ctx.sems["D_s1"].wge(v, gk + 1)
                v.drain()
                v.tensor_scalar(out=PL[0], in0=PL[6], scalar1=-1.0, scalar2=1.0, op0=OP.mult, op1=OP.add)
                v.drain()
                v.tensor_scalar(out=PL[0], in0=PL[0], scalar1=1e-6, scalar2=None, op0=OP.max)
                ctx.fence(v, "vector", s_v2)
                ctx.sems["D_s2"].wge(v, gk + 1)
                v.drain()
                GX3 = GX.rearrange("p (g t) -> p g t", t=512)[:, :, k * 64:(k + 1) * 64]
                XH3 = XH.rearrange("p (c t) -> p c t", t=512)[:, :, k * 64:(k + 1) * 64]
                v.tensor_tensor(out=PL[0].rearrange("p (g t) -> p g t", t=64), in0=GX3,
                                in1=PL[6].rearrange("p (g t) -> p g t", t=64), op=OP.mult)
                v.drain()
                v.tensor_tensor(out=PL[4].rearrange("p (g t) -> p g t", t=64),
                                in0=PL[0].rearrange("p (g t) -> p g t", t=64),
                                in1=XH3[:, 0:8, :], op=OP.mult)
                v.tensor_tensor(out=PL[5].rearrange("p (g t) -> p g t", t=64),
                                in0=PL[0].rearrange("p (g t) -> p g t", t=64),
                                in1=XH3[:, 8:16, :], op=OP.mult)
                v.drain()
                v.tensor_tensor(out=PL[11], in0=PL[9], in1=PL[8], op=OP.mult)
                v.drain()
                v.tensor_tensor(out=PL[6], in0=PL[9], in1=PL[7], op=OP.mult)
                v.drain()
                v.tensor_tensor(out=PL[0], in0=PL[11], in1=PL[4], op=OP.mult)
                v.drain()
                v.tensor_tensor(out=PL[9], in0=PL[6], in1=PL[5], op=OP.mult)
                v.drain()
                v.tensor_tensor(out=PL[0], in0=PL[0], in1=PL[9], op=OP.add)
                v.drain()
                v.tensor_tensor(out=PL[9], in0=PL[11], in1=PL[5], op=OP.mult)
                v.drain()
                v.tensor_tensor(out=PL[11], in0=PL[6], in1=PL[4], op=OP.mult)
                v.drain()
                v.tensor_tensor(out=PL[9], in0=PL[9], in1=PL[11], op=OP.subtract)
                v.drain()
                v.tensor_tensor(out=PL[4], in0=PL[10], in1=PL[8], op=OP.mult)
                v.drain()
                v.tensor_tensor(out=PL[5], in0=PL[10], in1=PL[7], op=OP.mult)
                v.drain()
                v.tensor_tensor(out=t8(16, 24), in0=t8(48, 56), in1=ends(PL[8]), op=OP.mult)
                v.drain()
                v.tensor_tensor(out=t8(24, 32), in0=t8(48, 56), in1=ends(PL[7]), op=OP.mult)
                v.drain()
                for g in range(8):
                    v.tensor_tensor_scan(PL[6][:, g * 64:(g + 1) * 64], ONES64,
                                         PL[0][:, g * 64:(g + 1) * 64], ZCOL, OP.mult, OP.add)
                    v.tensor_tensor_scan(PL[7][:, g * 64:(g + 1) * 64], ONES64,
                                         PL[9][:, g * 64:(g + 1) * 64], ZCOL, OP.mult, OP.add)
                v.drain()
                v.tensor_tensor(out=t8(0, 8), in0=ends(PL[4]), in1=ends(PL[6]), op=OP.mult)
                v.drain()
                v.tensor_tensor(out=t8(8, 16), in0=ends(PL[5]), in1=ends(PL[7]), op=OP.mult)
                v.drain()
                v.tensor_tensor(out=T8[:, 0:8], in0=T8[:, 0:8], in1=T8[:, 8:16], op=OP.subtract)
                v.drain()
                v.tensor_tensor(out=t8(8, 16), in0=ends(PL[5]), in1=ends(PL[6]), op=OP.mult)
                v.drain()
                v.tensor_tensor(out=t8(32, 40), in0=ends(PL[4]), in1=ends(PL[7]), op=OP.mult)
                v.drain()
                v.tensor_tensor(out=T8[:, 8:16], in0=T8[:, 8:16], in1=T8[:, 32:40], op=OP.add)
                v.drain()
                cr, ci = CLOr[:, gk * 8:gk * 8 + 8], CLOi[:, gk * 8:gk * 8 + 8]
                nr, ni = CLOr[:, gk * 8 + 8:gk * 8 + 16], CLOi[:, gk * 8 + 8:gk * 8 + 16]
                v.tensor_tensor(out=T8[:, 32:40], in0=T8[:, 16:24], in1=cr, op=OP.mult)
                v.drain()
                v.tensor_tensor(out=T8[:, 40:48], in0=T8[:, 24:32], in1=ci, op=OP.mult)
                v.drain()
                v.tensor_tensor(out=nr, in0=T8[:, 32:40], in1=T8[:, 40:48], op=OP.subtract)
                v.drain()
                v.tensor_tensor(out=nr, in0=nr, in1=T8[:, 0:8], op=OP.add)
                v.drain()
                v.tensor_tensor(out=T8[:, 32:40], in0=T8[:, 16:24], in1=ci, op=OP.mult)
                v.drain()
                v.tensor_tensor(out=T8[:, 40:48], in0=T8[:, 24:32], in1=cr, op=OP.mult)
                v.drain()
                v.tensor_tensor(out=ni, in0=T8[:, 32:40], in1=T8[:, 40:48], op=OP.add)
                v.drain()
                v.tensor_tensor(out=ni, in0=ni, in1=T8[:, 8:16], op=OP.add)
                v.drain()
                qr, qi = QSTr[:, gk * 8:gk * 8 + 8], QSTi[:, gk * 8:gk * 8 + 8]
                nqr, nqi = QSTr[:, gk * 8 + 8:gk * 8 + 16], QSTi[:, gk * 8 + 8:gk * 8 + 16]
                v.tensor_tensor(out=T8[:, 32:40], in0=qr, in1=T8[:, 16:24], op=OP.mult)
                v.drain()
                v.tensor_tensor(out=T8[:, 40:48], in0=qi, in1=T8[:, 24:32], op=OP.mult)
                v.drain()
                v.tensor_tensor(out=nqr, in0=T8[:, 32:40], in1=T8[:, 40:48], op=OP.subtract)
                v.drain()
                v.tensor_tensor(out=T8[:, 32:40], in0=qr, in1=T8[:, 24:32], op=OP.mult)
                v.drain()
                v.tensor_tensor(out=T8[:, 40:48], in0=qi, in1=T8[:, 16:24], op=OP.mult)
                v.drain()
                v.tensor_tensor(out=nqi, in0=T8[:, 32:40], in1=T8[:, 40:48], op=OP.add)
                v.drain()
                sb = SPB[gk % 2]
                v.tensor_copy(sb[:, 0:512], PL[4])
                v.drain()
                v.tensor_copy(sb[:, 512:1024], PL[5])
                v.drain()
                v.tensor_copy(sb[:, 1024:1536], PL[6])
                v.drain()
                v.tensor_copy(sb[:, 1536:2048], PL[7])
                ctx.fence(v, "vector", s_v3)
        v.drain()
        v.tensor_copy(SUMT[:, 0:8], QSTr[:, 256:264])
        v.drain()
        v.tensor_copy(SUMT[:, 8:16], QSTi[:, 256:264])
        v.drain()
        v.tensor_copy(SUMT[:, 16:24], CLOr[:, 256:264])
        v.drain()
        v.tensor_copy(SUMT[:, 24:32], CLOi[:, 256:264])
        ctx.fence(v, "vector", s_sum)

    @blk.vector
    def _(v):
        ctx.sems["D_cc"].wge(v, 2)
        v.memset(CIN[:, 0:16], 0.0)
        for b in range(2):
            v.drain()
            v.memset(T8[:, 48:56], 0.0)
            v.memset(T8[:, 56:64], 0.0)
            for r in range(4):
                c = b * 4 + r
                if r > 0:
                    p = b * 4 + r - 1
                    v.drain()
                    v.tensor_tensor(out=T8[:, 0:8], in0=TRT[p][:, 0:8], in1=T8[:, 48:56], op=OP.mult)
                    v.drain()
                    v.tensor_tensor(out=T8[:, 8:16], in0=TRT[p][:, 8:16], in1=T8[:, 56:64], op=OP.mult)
                    v.drain()
                    v.tensor_tensor(out=T8[:, 16:24], in0=TRT[p][:, 0:8], in1=T8[:, 56:64], op=OP.mult)
                    v.drain()
                    v.tensor_tensor(out=T8[:, 24:32], in0=TRT[p][:, 8:16], in1=T8[:, 48:56], op=OP.mult)
                    v.drain()
                    v.tensor_tensor(out=T8[:, 48:56], in0=T8[:, 0:8], in1=T8[:, 8:16], op=OP.subtract)
                    v.drain()
                    v.tensor_tensor(out=T8[:, 56:64], in0=T8[:, 16:24], in1=T8[:, 24:32], op=OP.add)
                    v.drain()
                    v.tensor_tensor(out=T8[:, 48:56], in0=T8[:, 48:56], in1=TRT[p][:, 16:24], op=OP.add)
                    v.drain()
                    v.tensor_tensor(out=T8[:, 56:64], in0=T8[:, 56:64], in1=TRT[p][:, 24:32], op=OP.add)
                v.drain()
                v.tensor_scalar(out=T8[:, 0:8], in0=T8[:, 48:56], scalar1=smt[:, c:c + 1], scalar2=None, op0=OP.mult)
                v.drain()
                v.tensor_tensor(out=CIN[:, 0:8], in0=CIN[:, 0:8], in1=T8[:, 0:8], op=OP.add)
                v.drain()
                v.tensor_scalar(out=T8[:, 0:8], in0=T8[:, 56:64], scalar1=smt[:, c:c + 1], scalar2=None, op0=OP.mult)
                v.drain()
                v.tensor_tensor(out=CIN[:, 8:16], in0=CIN[:, 8:16], in1=T8[:, 0:8], op=OP.add)
        ctx.fence(v, "vector", s_cin)

    ctx.barrier()

    def bank8(oc):
        ps = PS[["A", "B", "C", "D"][oc // 2]]
        return ps[:, 0:512] if oc % 2 == 0 else ps[:, 512:1024]

    @blk.sync
    def _(e):
        for gk in range(32):
            if gk >= 2:
                ctx.sems["D_h"].wge(e, gk - 1)
            rb = RLB[gk % 2]
            for q in range(4):
                ctx.dma(e, "sync", rb[:, q * 512:(q + 1) * 512],
                        dram["spill_hbm"][(q * 32 + gk) * 128:(q * 32 + gk + 1) * 128, :])
            e.wait_ge(ctx.dsem["sync"].h, ctx.dsem["sync"].v)
            ctx.fence(e, "sync", s_rl)

    @blk.gpsimd
    def _(g):
        for stt in range(4):
            for hc in range(16):
                kk = stt * 16 + hc
                if kk >= 2:
                    ctx.sems["D_om"].wge(g, kk - 1)
                ctx.dma(g, "gpsimd", WINB[kk % 2], dram["w_out"][hc * 128:(hc + 1) * 128, :])
                g.wait_ge(ctx.dsem["gpsimd"].h, ctx.dsem["gpsimd"].v)
                ctx.fence(g, "gpsimd", s_wo)

    @blk.tensor
    def _(te):
        for stt in range(4):
            ctx.sems["D_h"].wge(te, 8 * (stt + 1))
            if stt >= 1:
                ctx.sems["D_oe"].wge(te, 8 * stt)
            for hc in range(16):
                kk = stt * 16 + hc
                ctx.sems["D_wo"].wge(te, kk + 1)
                for oc in range(8):
                    last = nc.tensor.matmul(bank8(oc), WINB[kk % 2][:, oc * 128:oc * 128 + 128],
                                            HMB[:, hc * 512:(hc + 1) * 512],
                                            start=(hc == 0), stop=(hc == 15))
                s_om.inc(last)

    @blk.vector
    def _(v):
        def evac(stt):
            for oc in range(8):
                ctx.sems["D_om"].wge(v, 16 * (stt + 1))
                v.drain()
                s_oe.inc(v.scalar_tensor_tensor(out=xf_ap(ctx, oc, HALO + stt * 512, 512),
                                                in0=bank8(oc), scalar=b_out[:, oc:oc + 1],
                                                in1=xf_ap(ctx, oc, HALO + stt * 512, 512),
                                                op0=OP.add, op1=OP.add))
                v.drain()
        for stt in range(4):
            if stt >= 1:
                evac(stt - 1)
            for k in range(8):
                gk = stt * 8 + k
                ctx.sems["D_rl"].wge(v, gk + 1)
                if k == 0 and stt >= 1:
                    ctx.sems["D_om"].wge(v, 16 * stt)
                rb = RLB[gk % 2]
                v.drain()
                v.tensor_tensor(out=T8[:, 0:8], in0=QSTr[:, gk * 8:gk * 8 + 8], in1=CIN[:, 0:8], op=OP.mult)
                v.drain()
                v.tensor_tensor(out=T8[:, 8:16], in0=QSTi[:, gk * 8:gk * 8 + 8], in1=CIN[:, 8:16], op=OP.mult)
                v.drain()
                v.tensor_tensor(out=T8[:, 0:8], in0=T8[:, 0:8], in1=T8[:, 8:16], op=OP.subtract)
                v.drain()
                v.tensor_tensor(out=T8[:, 0:8], in0=T8[:, 0:8], in1=CLOr[:, gk * 8:gk * 8 + 8], op=OP.add)
                v.drain()
                v.tensor_tensor(out=T8[:, 8:16], in0=QSTr[:, gk * 8:gk * 8 + 8], in1=CIN[:, 8:16], op=OP.mult)
                v.drain()
                v.tensor_tensor(out=T8[:, 16:24], in0=QSTi[:, gk * 8:gk * 8 + 8], in1=CIN[:, 0:8], op=OP.mult)
                v.drain()
                v.tensor_tensor(out=T8[:, 8:16], in0=T8[:, 8:16], in1=T8[:, 16:24], op=OP.add)
                v.drain()
                v.tensor_tensor(out=T8[:, 8:16], in0=T8[:, 8:16], in1=CLOi[:, gk * 8:gk * 8 + 8], op=OP.add)
                v.drain()
                v.tensor_scalar(out=T8[:, 0:16], in0=T8[:, 0:16], scalar1=EMB, scalar2=None, op0=OP.mult)
                v.drain()
                for g in range(8):
                    v.tensor_scalar(out=PL[0][:, g * 64:(g + 1) * 64],
                                    in0=rb[:, 1024 + g * 64:1024 + g * 64 + 64],
                                    scalar1=T8[:, g:g + 1], scalar2=None, op0=OP.add)
                    v.tensor_scalar(out=PL[1][:, g * 64:(g + 1) * 64],
                                    in0=rb[:, 1536 + g * 64:1536 + g * 64 + 64],
                                    scalar1=T8[:, 8 + g:9 + g], scalar2=None, op0=OP.add)
                v.drain()
                v.tensor_tensor(out=PL[2], in0=rb[:, 0:512], in1=PL[0], op=OP.mult)
                v.drain()
                v.tensor_tensor(out=PL[3], in0=rb[:, 512:1024], in1=PL[1], op=OP.mult)
                v.drain()
                v.tensor_tensor(out=PL[2], in0=PL[2], in1=PL[3], op=OP.subtract)
                v.drain()
                v.tensor_tensor(out=PL[3], in0=rb[:, 512:1024], in1=PL[0], op=OP.mult)
                v.drain()
                v.tensor_tensor(out=PL[4], in0=rb[:, 0:512], in1=PL[1], op=OP.mult)
                v.drain()
                v.tensor_tensor(out=PL[3], in0=PL[3], in1=PL[4], op=OP.add)
                v.drain()
                H3 = HMB.rearrange("p (a b) -> p a b", b=512)
                v.tensor_copy(H3[:, 0:8, k * 64:k * 64 + 64], PL[2].rearrange("p (a b) -> p a b", a=8))
                v.drain()
                s_h.inc(v.tensor_copy(H3[:, 8:16, k * 64:k * 64 + 64], PL[3].rearrange("p (a b) -> p a b", a=8)))
                v.drain()
        evac(3)

    ctx.tap("qall", QALL)
    ctx.tap("misc", MISC)
    ctx.tap("hmb", HMB)




NE = 17


def phaseE(ctx, dram, PS, st):
    nc, blk = ctx.nc, ctx.blk
    SM = st["SM"]
    onesb_row = st["onesb_row"]

    XN4 = ctx.at(R2_KB, 8 * NTOK, BF16)
    WEX = []
    for b in range(2):
        base = R2_KB + 32 + 24 * b
        WEX.append(dict(
            WG=ctx.at(base, 4096, BF16),
            WU=ctx.at(base + 8, 4096, BF16),
            WD=ctx.at(base + 16, 4096, BF16),
        ))
    GW = ctx.at(R2_KB + 80, 16 * 512, BF16)
    RTT = ctx.at(R2_KB + 96, 256)
    SCR = ctx.at(R2_KB + 97, 256)
    WTROW = ctx.at(R1_KB, 2048, BF16)[0:1, :]
    WTONES = ctx.at(R1_KB + 4, 2048, BF16)[0:1, :]
    GRP = ctx.at(R1_KB + 8, 512, BF16)
    WB16 = GRP[:, 0:16]
    GWR = GRP[:, 128:256]
    SILB = [ctx.at(R1_KB + 9 + b, 512, BF16) for b in range(2)]
    HUW = [ctx.at(R1_KB + 12 + 2 * b, 512) for b in range(2)]
    WBTF = ctx.at(R1_KB + 16, 2048, BF16)
    GBT = SM[:, 328:344]
    WMT = ctx.stack.enter_context(
        nc.sbuf_tensor("wmtE", [128, 128], BF16, is_transpose_dest=True)).ap()

    s_rg = ctx.sem("E_rg")
    s_rmm = ctx.sem("E_rmm")
    s_ra = ctx.sem("E_ra")
    s_rs = ctx.sem("E_rs")
    s_rw = ctx.sem("E_rw")
    s_rt = ctx.sem("E_rt")
    s_wts = ctx.sem("E_wts")
    s_x4a = ctx.sem("E_x4a")
    s_wl = ctx.sem("E_wl")
    s_wtr = ctx.sem("E_wtr")
    s_wb = ctx.sem("E_wb")
    s_wbe = ctx.sem("E_wbe")
    s_p1 = ctx.sem("E_p1")
    s_sil = ctx.sem("E_sil")
    s_huw = ctx.sem("E_huw")
    s_gw = ctx.sem("E_gw")
    s_p2 = ctx.sem("E_p2")
    s_ev = ctx.sem("E_ev")

    emit_ln(ctx, PS, st, "LN4", XN4)
    ctx.barrier()

    B4 = [PS["A"][:, 0:512], PS["A"][:, 512:1024], PS["B"][:, 0:512], PS["B"][:, 512:1024]]
    B4H = [PS["C"][:, 0:512], PS["C"][:, 512:1024], PS["D"][:, 0:512], PS["D"][:, 512:1024]]
    B8 = B4 + B4H

    def wsrc(e):
        if e < 16:
            return (dram["ex_wg"][e * 1024:(e + 1) * 1024, :],
                    dram["ex_wu"][e * 1024:(e + 1) * 1024, :],
                    dram["ex_wd"][e * 512:(e + 1) * 512, :])
        return (dram["sh_wg"], dram["sh_wu"], dram["sh_wd"])

    def ldw(g, e):
        sg, su, sd = wsrc(e)
        w = WEX[e % 2]
        ctx.dma(g, "gpsimd", w["WG"].rearrange("p (a b) -> p a b", a=8),
                sg.rearrange("(a p) c -> p a c", p=128))
        ctx.dma(g, "gpsimd", w["WU"].rearrange("p (a b) -> p a b", a=8),
                su.rearrange("(a p) c -> p a c", p=128))
        ctx.dma(g, "gpsimd", w["WD"].rearrange("p (a b) -> p a b", a=4),
                sd.rearrange("(a p) c -> p a c", p=128))

    @blk.gpsimd
    def _(g):
        ctx.dma(g, "gpsimd", GWR.rearrange("p (a b) -> p a b", a=8),
                dram["gate_w"].rearrange("(a p) c -> p a c", p=128))
        ctx.dma(g, "gpsimd", GBT, dram["gate_bt"])
        g.wait_ge(ctx.dsem["gpsimd"].h, ctx.dsem["gpsimd"].v)
        ctx.fence(g, "gpsimd", s_rg)
        ldw(g, 0)
        g.wait_ge(ctx.dsem["gpsimd"].h, ctx.dsem["gpsimd"].v)
        ctx.fence(g, "gpsimd", s_wl)
        for e in range(NE):
            if e + 1 < NE:
                ctx.sems["E_p2"].wge(g, 4 * e)
                ldw(g, e + 1)
            for ic in range(4):
                for t in range(4):
                    gi = 16 * e + 4 * ic + t
                    ctx.sems["E_sil"].wge(g, gi + 1)
                    ctx.sems["E_huw"].wge(g, gi + 1)
                    if e >= 1 and ic == 0 and t == 0:
                        ctx.sems["E_p2"].wge(g, 4 * e)
                    s_gw.inc(g.tensor_tensor(out=GW[:, ic * 2048 + t * 512:ic * 2048 + t * 512 + 512],
                                             in0=SILB[t % 2], in1=HUW[t % 2], op=OP.mult))
            g.wait_ge(ctx.dsem["gpsimd"].h, ctx.dsem["gpsimd"].v)
            ctx.fence(g, "gpsimd", s_wl)

    @blk.sync
    def _(e_):
        for e in range(16):
            if e == 0:
                ctx.sems["E_rt"].wge(e_, 1)
            ctx.sems["E_wb"].wge(e_, 4 * e)
            ctx.dma(e_, "sync", WTROW, dram["wmoe_hbm"][e:e + 1, :])
            e_.wait_ge(ctx.dsem["sync"].h, ctx.dsem["sync"].v)
            ctx.fence(e_, "sync", s_wtr)

    @blk.tensor
    def _(te):
        ctx.sems["E_rg"].wge(te, 1)
        for tile in range(16):
            if tile >= 2:
                ctx.sems["E_rs"].wge(te, tile - 1)
            bank = PS["A"][:, 0:16] if tile % 2 == 0 else PS["A"][:, 512:528]
            for dc in range(8):
                last = nc.tensor.matmul(bank, XN4[:, dc * NTOK + tile * 128:dc * NTOK + tile * 128 + 128],
                                        GWR[:, dc * 16:(dc + 1) * 16], start=(dc == 0), stop=(dc == 7))
            s_rmm.inc(last)
        for e in range(NE):
            ctx.sems["E_wl"].wge(te, e + 1)
            if e < 16:
                ctx.sems["E_wtr"].wge(te, e + 1)
            else:
                ctx.sems["E_x4a"].wge(te, 1)
            if e >= 1:
                ctx.sems["E_ev"].wge(te, 4 * e)
            row = WTONES if e == 16 else WTROW
            for t in range(4):
                s_wb.inc(nc.tensor.matmul(B4[t], onesb_row,
                                          row[0:1, t * 512:(t + 1) * 512], start=True, stop=True))
            for ic in range(4):
                if ic == 0:
                    ctx.sems["E_wbe"].wge(te, 4 * e + 4)
                else:
                    ctx.sems["E_sil"].wge(te, 16 * e + 4 * ic)
                    ctx.sems["E_huw"].wge(te, 16 * e + 4 * ic)
                for dc in range(8):
                    for t in range(4):
                        last = nc.tensor.matmul(B4[t], WEX[e % 2]["WG"][:, dc * 512 + ic * 128:dc * 512 + ic * 128 + 128],
                                                XN4[:, dc * NTOK + t * 512:dc * NTOK + t * 512 + 512],
                                                start=(dc == 0), stop=(dc == 7))
                s_p1.inc(last)
                for dc in range(8):
                    for t in range(4):
                        last = nc.tensor.matmul(B4H[t], WEX[e % 2]["WU"][:, dc * 512 + ic * 128:dc * 512 + ic * 128 + 128],
                                                XN4[:, dc * NTOK + t * 512:dc * NTOK + t * 512 + 512],
                                                start=(dc == 0), stop=(dc == 7))
                s_p1.inc(last)
            ctx.sems["E_gw"].wge(te, 16 * (e + 1))
            for p4 in range(4):
                if p4 >= 1:
                    ctx.sems["E_ev"].wge(te, 4 * e + p4)
                for ic in range(4):
                    for o1 in range(2):
                        oc = 2 * p4 + o1
                        for t in range(4):
                            last = nc.tensor.matmul(B8[o1 * 4 + t],
                                                    WEX[e % 2]["WD"][:, ic * 1024 + oc * 128:ic * 1024 + oc * 128 + 128],
                                                    GW[:, ic * 2048 + t * 512:ic * 2048 + t * 512 + 512],
                                                    start=(ic == 0), stop=(ic == 3))
                s_p2.inc(last)

    @blk.scalar
    def _(s):
        act_preload(s, "sigmoid_and_others")
        for tile in range(16):
            ctx.sems["E_ra"].wge(s, tile + 1)
            s.drain()
            s.activation(out=RTT[:, tile * 16:(tile + 1) * 16], in_=SCR[:, 0:16], func=AF.Sigmoid)
            ctx.fence(s, "scalar", s_rs)
            ctx.sems["E_rw"].wge(s, tile + 1)
            ctx.dsem["scalar"].inc(s.dma_start(out=WMT[:, :], in_=GRP[:, 0:128], transpose=True), 16)
            s.wait_ge(ctx.dsem["scalar"].h, ctx.dsem["scalar"].v)
            ctx.dma(s, "scalar", dram["wmoe_hbm"][0:16, tile * 128:(tile + 1) * 128], WMT[0:16, 0:128])
            s.wait_ge(ctx.dsem["scalar"].h, ctx.dsem["scalar"].v)
            ctx.fence(s, "scalar", s_wts)
        ctx.fence(s, "scalar", s_rt)
        act_preload(s, "silu_and_others")
        for e in range(NE):
            for ic in range(4):
                ctx.sems["E_p1"].wge(s, 8 * e + 2 * ic + 1)
                for t in range(4):
                    gi = 16 * e + 4 * ic + t
                    if gi >= 2:
                        ctx.sems["E_gw"].wge(s, gi - 1)
                    s.drain()
                    s_sil.inc(s.activation(out=SILB[t % 2], in_=B4[t], func=AF.Silu))

    @blk.vector
    def _(v):
        def top2(t0):
            R = RTT[:, t0 * 16:(t0 + 1) * 16]
            m1, c1, f1, m2 = SCR[:, 16:17], SCR[:, 17:18], SCR[:, 18:19], SCR[:, 19:20]
            den, dv = SCR[:, 20:21], SCR[:, 21:22]
            msk1, inv, R2v = SCR[:, 32:48], SCR[:, 48:64], SCR[:, 64:80]
            mskB, SEL, WS, WF = SCR[:, 80:96], SCR[:, 96:112], SCR[:, 112:128], SCR[:, 128:144]
            ctx.sems["E_rs"].wge(v, t0 + 1)
            v.drain()
            v.tensor_reduce(m1, R, axis=AX.X, op=OP.max)
            v.drain()
            v.tensor_scalar(out=msk1, in0=R, scalar1=m1, scalar2=None, op0=OP.is_ge)
            v.drain()
            v.tensor_reduce(c1, msk1, axis=AX.X, op=OP.add)
            v.drain()
            v.tensor_scalar(out=f1, in0=c1, scalar1=1.5, scalar2=None, op0=OP.is_le)
            v.tensor_scalar(out=inv, in0=msk1, scalar1=-1.0, scalar2=1.0, op0=OP.mult, op1=OP.add)
            v.drain()
            v.tensor_tensor(out=R2v, in0=R, in1=inv, op=OP.mult)
            v.drain()
            v.tensor_reduce(m2, R2v, axis=AX.X, op=OP.max)
            v.drain()
            v.tensor_scalar(out=mskB, in0=R2v, scalar1=m2, scalar2=None, op0=OP.is_ge)
            v.drain()
            v.tensor_scalar(out=mskB, in0=mskB, scalar1=f1, scalar2=None, op0=OP.mult)
            v.drain()
            v.tensor_tensor(out=SEL, in0=msk1, in1=mskB, op=OP.max)
            v.drain()
            v.tensor_tensor(out=WS, in0=R, in1=SEL, op=OP.mult)
            v.drain()
            v.tensor_reduce(den, WS, axis=AX.X, op=OP.add)
            v.drain()
            v.reciprocal(dv, den)
            v.drain()
            v.tensor_scalar(out=WF, in0=WS, scalar1=dv, scalar2=None, op0=OP.mult)
            v.drain()
            if t0 >= 1:
                ctx.sems["E_wts"].wge(v, t0)
            v.tensor_copy(WB16, WF)
            ctx.fence(v, "vector", s_rw)
        for tile in range(17):
            if tile < 16:
                ctx.sems["E_rmm"].wge(v, tile + 1)
                if tile >= 1:
                    ctx.sems["E_rs"].wge(v, tile)
                bank = PS["A"][:, 0:16] if tile % 2 == 0 else PS["A"][:, 512:528]
                v.drain()
                v.tensor_tensor(out=SCR[:, 0:16], in0=bank, in1=GBT, op=OP.add)
                ctx.fence(v, "vector", s_ra)
            if tile >= 1:
                top2(tile - 1)
        v.memset(WTONES, 1.0)
        v.drain()
        for dc in range(8):
            v.tensor_tensor(out=xf_ap(ctx, dc, HALO, NTOK), in0=xf_ap(ctx, dc, HALO, NTOK),
                            in1=XN4[:, dc * NTOK:(dc + 1) * NTOK], op=OP.add)
        ctx.fence(v, "vector", s_x4a)
        for e in range(NE):
            for t in range(4):
                ctx.sems["E_wb"].wge(v, 4 * e + t + 1)
                s_wbe.inc(v.tensor_copy(WBTF[:, t * 512:(t + 1) * 512], B4[t]))
            v.drain()
            for ic in range(4):
                ctx.sems["E_p1"].wge(v, 8 * e + 2 * ic + 2)
                for t in range(4):
                    gi = 16 * e + 4 * ic + t
                    if gi >= 2:
                        ctx.sems["E_gw"].wge(v, gi - 1)
                    s_huw.inc(v.tensor_tensor(out=HUW[t % 2], in0=B4H[t],
                                              in1=WBTF[:, t * 512:(t + 1) * 512], op=OP.mult))
            for p4 in range(4):
                ctx.sems["E_p2"].wge(v, 4 * e + p4 + 1)
                for o1 in range(2):
                    oc = 2 * p4 + o1
                    for t in range(4):
                        last = v.tensor_tensor(out=xf_ap(ctx, oc, HALO + t * 512, 512),
                                               in0=B8[o1 * 4 + t],
                                               in1=xf_ap(ctx, oc, HALO + t * 512, 512), op=OP.add)
                s_ev.inc(last)


def emit(ctx, dram, PS):
    st = {}
    def run(fn):
        ctx.scope_begin()
        fn(ctx, dram, PS, st)
        ctx.barrier()
        ctx.scope_end()
    run(phaseA)
    if PHASES >= 2 and PHASES != 30:
        run(phaseB1)
    if PHASES >= 3 and PHASES != 30:
        run(phaseB2)
    if PHASES >= 4 or PHASES in (30, 35):
        run(phaseC)
    if PHASES >= 5:
        run(phaseD)
    if PHASES >= 6:
        run(phaseE)
    ctx.flush_taps()
    emit_store(ctx, dram)


def emit_store(ctx, dram):
    blk = ctx.blk
    y = dram["y_out"]
    @blk.gpsimd
    def _(g):
        for dc in range(8):
            ctx.dma(g, "gpsimd", y[dc * 128:(dc + 1) * 128, :], xf_ap(ctx, dc, HALO, NTOK))
        ctx.dsem["gpsimd"].wait(g)


def phaseA(ctx, dram, PS, st):
    nc, blk = ctx.nc, ctx.blk
    SM = ctx.at(SM_KB, 1024)
    ones_col = SM[:, 0:1]
    onesb_row = ctx.at(SM_KB + 2, 256, BF16)[0:1, 0:128]
    onesb_col = ctx.at(SM_KB + 2, 256, BF16)[:, 140:141]
    st["SM"] = SM
    st["ones_col"] = ones_col
    st["onesb_col"] = onesb_col
    st["onesb_row"] = onesb_row
    XN1 = ctx.at(R1_KB, 8 * NROW, BF16)
    st["XN1"] = XN1
    XB = ctx.at(R2_KB, 8 * 512, BF16)
    SQ = ctx.at(R2_KB + 8, 8 * 512, BF16)
    SROW_ = ctx.at(R2_KB + 16, 6 * 2304)[0:1, :]
    SROW = [SROW_[:, i * 2304:(i + 1) * 2304] for i in range(6)]
    TMP = ctx.at(R2_KB + 70, 512)
    ABH_ = ctx.at(R2_KB + 72, 4 * 2304, BF16)[0:1, :]
    ABH = [ABH_[:, i * 2304:(i + 1) * 2304] for i in range(4)]

    s_ld = ctx.sem("A_ld")
    s_pre = ctx.sem("A_pre")
    s_mm = ctx.sem("A_mm")
    s_ev = ctx.sem("A_ev")
    s_ab = ctx.sem("A_ab")
    s_bc = ctx.sem("A_bc")
    s_ap = ctx.sem("A_ap")

    @blk.sync
    def _(e):
        for dc in range(8):
            ctx.dma(e, "sync", xf_ap(ctx, dc, 0, NROW), dram["x_fm"][dc * 128:(dc + 1) * 128, :])

    def bank_of(tt):
        return PS["A"] if tt % 2 == 0 else PS["B"]

    @blk.vector
    def _(v):
        v.memset(ones_col, 1.0)
        v.memset(onesb_row, 1.0)
        v.memset(onesb_col, 1.0)
        v.memset(SM[:, 1:2], 1e-8)
        for c, val in enumerate(CONSTS):
            v.memset(SM[:, 250 + c:251 + c], val)
        ctx.dsem["sync"].wait(v)
        for tt in range(NT5 + 1):
            if tt >= 1:
                ctx.sems["A_mm"].wge(v, tt)
                o1, w1 = (tt - 1) * 512, tile_w(tt - 1)
                bank = bank_of(tt - 1)
                v.tensor_copy(SROW[0][:, o1:o1 + w1], bank[0:1, 0:w1])
                s_ev.inc(v.tensor_copy(SROW[1][:, o1:o1 + w1], bank[0:1, 512:512 + w1]))
            if tt < NT5:
                o, w = tt * 512, tile_w(tt)
                for dc in range(8):
                    v.tensor_copy(XB[:, dc * 512:dc * 512 + w], xf_ap(ctx, dc, o, w))
                ctx.fence(v, "vector", s_pre)

    @blk.scalar
    def _(s):
        act_preload(s, "sqrt_and_others")
        ctx.dsem["sync"].wait(s)
        for tt in range(NT5):
            o, w = tt * 512, tile_w(tt)
            if tt >= 1:
                ctx.sems["A_mm"].wge(s, tt)
            for dc in range(8):
                s.activation(out=SQ[:, dc * 512:dc * 512 + w], in_=xf_ap(ctx, dc, o, w), func=AF.Square)
            ctx.fence(s, "scalar", s_pre)

    @blk.tensor
    def _(te):
        for tt in range(NT5):
            o, w = tt * 512, tile_w(tt)
            ctx.sems["A_pre"].wge(te, 2 * (tt + 1))
            if tt >= 2:
                ctx.sems["A_ev"].wge(te, tt - 1)
            bank = bank_of(tt)
            for dc in range(8):
                nc.tensor.matmul(bank[0:1, 0:w], onesb_col, XB[:, dc * 512:dc * 512 + w],
                                 start=(dc == 0), stop=(dc == 7))
            for dc in range(8):
                last = nc.tensor.matmul(bank[0:1, 512:512 + w], onesb_col, SQ[:, dc * 512:dc * 512 + w],
                                        start=(dc == 0), stop=(dc == 7))
            s_mm.inc(last)

    @blk.vector
    def _(v):
        ctx.sems["A_ev"].wge(v, NT5)
        v.drain()
        v.tensor_scalar(out=SROW[0][:, :NROW], in0=SROW[0][:, :NROW], scalar1=1.0 / D, scalar2=None, op0=OP.mult)
        v.tensor_scalar(out=SROW[1][:, :NROW], in0=SROW[1][:, :NROW], scalar1=1.0 / D, scalar2=None, op0=OP.mult)
        v.drain()
        v.tensor_tensor(out=SROW[2][:, :NROW], in0=SROW[0][:, :NROW], in1=SROW[0][:, :NROW], op=OP.mult)
        v.drain()
        v.tensor_tensor(out=SROW[1][:, :NROW], in0=SROW[1][:, :NROW], in1=SROW[2][:, :NROW], op=OP.subtract)
        v.drain()
        v.tensor_scalar(out=SROW[1][:, :NROW], in0=SROW[1][:, :NROW], scalar1=1e-5, scalar2=None, op0=OP.add)
        ctx.fence(v, "vector", s_ab)

    @blk.scalar
    def _(s):
        s_ab.wait(s)
        s.activation(out=SROW[2][:, :NROW], in_=SROW[1][:, :NROW], func=AF.Sqrt)
        ctx.fence(s, "scalar", s_ab)

    @blk.vector
    def _(v):
        ctx.sems["A_ab"].wge(v, 2)
        v.reciprocal(SROW[3][:, :NROW], SROW[2][:, :NROW])
        v.drain()
        v.tensor_tensor(out=SROW[4][:, :NROW], in0=SROW[0][:, :NROW], in1=SROW[3][:, :NROW], op=OP.mult)
        v.drain()
        v.tensor_scalar(out=SROW[4][:, :NROW], in0=SROW[4][:, :NROW], scalar1=-1.0, scalar2=None, op0=OP.mult)
        v.tensor_copy(ABH[0][:, :NROW], SROW[3][:, :NROW])
        v.drain()
        v.tensor_copy(ABH[2][:, :NROW], SROW[4][:, :NROW])
        v.drain()
        v.tensor_copy(SROW[5][:, :NROW], ABH[0][:, :NROW])
        v.drain()
        v.tensor_tensor(out=SROW[5][:, :NROW], in0=SROW[3][:, :NROW], in1=SROW[5][:, :NROW], op=OP.subtract)
        v.drain()
        v.tensor_copy(ABH[1][:, :NROW], SROW[5][:, :NROW])
        v.drain()
        v.tensor_copy(SROW[5][:, :NROW], ABH[2][:, :NROW])
        v.drain()
        v.tensor_tensor(out=SROW[5][:, :NROW], in0=SROW[4][:, :NROW], in1=SROW[5][:, :NROW], op=OP.subtract)
        v.drain()
        v.tensor_copy(ABH[3][:, :NROW], SROW[5][:, :NROW])
        ctx.fence(v, "vector", s_ab)

    @blk.tensor
    def _(te):
        ctx.sems["A_ab"].wge(te, 3)
        for tt in range(NT5):
            o, w = tt * 512, tile_w(tt)
            pa = PS["C"][:, 0:w] if tt % 2 == 0 else PS["C"][:, 512:512 + w]
            pb = PS["D"][:, 0:w] if tt % 2 == 0 else PS["D"][:, 512:512 + w]
            nc.tensor.matmul(pa, onesb_row, ABH[0][:, o:o + w], start=True, stop=False)
            nc.tensor.matmul(pa, onesb_row, ABH[1][:, o:o + w], start=False, stop=True)
            nc.tensor.matmul(pb, onesb_row, ABH[2][:, o:o + w], start=True, stop=False)
            s_bc.inc(nc.tensor.matmul(pb, onesb_row, ABH[3][:, o:o + w], start=False, stop=True))
            ctx.sems["A_ap"].wge(te, tt * 8 + 8)

    @blk.vector
    def _(v):
        for tt in range(NT5):
            o, w = tt * 512, tile_w(tt)
            ctx.sems["A_bc"].wge(v, tt + 1)
            pa = PS["C"][:, 0:w] if tt % 2 == 0 else PS["C"][:, 512:512 + w]
            pb = PS["D"][:, 0:w] if tt % 2 == 0 else PS["D"][:, 512:512 + w]
            for dc in range(8):
                v.tensor_tensor(out=TMP[:, 0:w], in0=xf_ap(ctx, dc, o, w), in1=pa, op=OP.mult)
                v.drain()
                s_ap.inc(v.tensor_tensor(out=XN1[:, dc * NROW + o:dc * NROW + o + w],
                                         in0=TMP[:, 0:w], in1=pb, op=OP.add))
                v.drain()

    ctx.tap("xn1", XN1)


def phaseB1(ctx, dram, PS, st):
    nc, blk = ctx.nc, ctx.blk
    SM, ones_col = st["SM"], st["ones_col"]
    XN1 = st["XN1"]
    Q = ctx.at(R2_KB, NH * NROW, BF16)
    V = ctx.at(R2_KB + 35, NRC * NH * 128, BF16)
    EW = ctx.at(R2_KB + 70, 8 * NH * 128, BF16)
    MNW1 = ctx.at(R2_KB + 86, 4096, BF16)
    MNW2 = ctx.at(R2_KB + 90, 4096, BF16)
    GATH = ctx.at(SM_KB + 3, 128)
    st["Q"], st["V"] = Q, V
    sigb = ctx.at(SM_KB + 2, 256, BF16)[:, 128:136]
    mn_b1 = SM[:, 64:68]; mn_b2 = SM[:, 72:80]; base_m = SM[:, 80:88]
    sig = SM[:, 8:16]; metric = SM[:, 16:24]
    st["sig"], st["metric"] = sig, metric
    partials = SM[:, 40:56]
    bmt = SM[:, 96:104]
    hidb = ctx.at(SM_KB + 2, 256, BF16)[:, 136:140]

    s_w = ctx.sem("B_w")
    s_qmm = ctx.sem("B_qmm")
    s_qe = ctx.sem("B_qe")
    s_vmm = ctx.sem("B_vmm")
    s_ve = ctx.sem("B_ve")
    s_part = ctx.sem("B_part")
    s_cc = ctx.sem("B_cc", pin=True)
    s_met = ctx.sem("B_met")

    @blk.gpsimd
    def _(g):
        ctx.dma(g, "gpsimd", EW.rearrange("p (a b) -> p a b", a=8),
                dram["eqT"].rearrange("(a p) c -> p a c", p=128))
        ctx.dma(g, "gpsimd", MNW1.rearrange("p (a b) -> p a b", a=8), dram["mn_w1"].rearrange("(a p) c -> p a c", p=128))
        g.wait_ge(ctx.dsem["gpsimd"].h, ctx.dsem["gpsimd"].v)
        ctx.fence(g, "gpsimd", s_w)

    @blk.sync
    def _(e):
        ctx.dma(e, "sync", mn_b1, dram["mn_b1"])
        ctx.dma(e, "sync", mn_b2, dram["mn_b2"])
        ctx.dma(e, "sync", base_m, dram["base_m"])
        ctx.dma(e, "sync", bmt, dram["bmt"])

    banks4 = [PS["A"][:, 0:512], PS["A"][:, 512:1024], PS["B"][:, 0:512], PS["B"][:, 512:1024]]
    @blk.tensor
    def _(te):
        s_w.wait(te)
        k = 0
        for h in range(NH):
            for tt in range(NT5):
                o, w = tt * 512, tile_w(tt)
                if k >= 4:
                    ctx.sems["B_qe"].wge(te, k - 3)
                bank = banks4[k % 4]
                for dc in range(8):
                    last = nc.tensor.matmul(bank[:, 0:w], EW[:, (dc * NH + h) * 128:(dc * NH + h) * 128 + 128],
                                            XN1[:, dc * NROW + o:dc * NROW + o + w],
                                            start=(dc == 0), stop=(dc == 7))
                s_qmm.inc(last)
                k += 1

    @blk.scalar
    def _(s):
        k = 0
        for h in range(NH):
            for tt in range(NT5):
                o, w = tt * 512, tile_w(tt)
                ctx.sems["B_qmm"].wge(s, k + 1)
                s_qe.inc(s.activation(out=Q[:, h * NROW + o:h * NROW + o + w], in_=banks4[k % 4][:, 0:w], func=AF.Relu))
                k += 1

    @blk.gpsimd
    def _(g):
        ctx.sems["B_qmm"].wge(g, NH * NT5)
        ctx.dma(g, "gpsimd", EW.rearrange("p (a b) -> p a b", a=8),
                dram["evT"].rearrange("(a p) c -> p a c", p=128))
        g.wait_ge(ctx.dsem["gpsimd"].h, ctx.dsem["gpsimd"].v)
        ctx.fence(g, "gpsimd", s_w)

    @blk.tensor
    def _(te):
        ctx.sems["B_w"].wge(te, 2)
        ctx.sems["B_qe"].wge(te, NH * NT5)
        for rc in range(NRC):
            o = rc * 128
            if rc >= 2:
                ctx.sems["B_ve"].wge(te, rc - 1)
            bank = PS["C"] if rc % 2 == 0 else PS["D"]
            for half in range(2):
                for dc in range(8):
                    last = nc.tensor.matmul(bank[:, half * 512:half * 512 + 512],
                                            XN1[:, dc * NROW + o:dc * NROW + o + 128],
                                            EW[:, dc * 1024 + half * 512:dc * 1024 + half * 512 + 512],
                                            start=(dc == 0), stop=(dc == 7))
            s_vmm.inc(last)

    @blk.scalar
    def _(s):
        for rc in range(NRC):
            ctx.sems["B_vmm"].wge(s, rc + 1)
            bank = PS["C"] if rc % 2 == 0 else PS["D"]
            s_ve.inc(s.activation(out=V[:, rc * 1024:(rc + 1) * 1024], in_=bank[:, 0:1024], func=AF.Relu))

    @blk.vector
    def _(v):
        ctx.sems["B_qe"].wge(v, NH * NT5)
        for h in range(NH):
            v.tensor_reduce(partials[:, h:h + 1], Q[:, h * NROW + HALO:h * NROW + NROW], axis=AX.X, op=OP.add)
        for h in range(NH):
            qs = Q[:, h * NROW + HALO:h * NROW + NROW].rearrange("p (a b) -> p a b", b=8)[:, :, 0:1]
            v.tensor_reduce(partials[:, 8 + h:9 + h], qs, axis=AX.XY, op=OP.add)
        ctx.fence(v, "vector", s_part)

    @blk.gpsimd
    def _(g):
        s_part.wait(g)
        dst = dram["cc_in1"].rearrange("a (p f) -> (a p) f", p=128)
        ctx.dma(g, "gpsimd", dst, partials)
        g.wait_ge(ctx.dsem["gpsimd"].h, ctx.dsem["gpsimd"].v)
        s_cc.inc(g.collective_compute("AllGather", OP.bypass, replica_groups=[[0, 1, 2, 3, 4, 5, 6, 7]],
                                      ins=[dram["cc_in1"][:]], outs=[dram["cc_out1"][:]]))
        ctx.sems["B_cc"].wge(g, 1)
        for r in range(8):
            src = dram["cc_out1"][r:r + 1, :].rearrange("a (p f) -> (a p) f", p=128)
            ctx.dma(g, "gpsimd", GATH[:, r * 16:(r + 1) * 16], src)
        g.wait_ge(ctx.dsem["gpsimd"].h, ctx.dsem["gpsimd"].v)
        ctx.fence(g, "gpsimd", s_cc)

    @blk.vector
    def _(v):
        ctx.sems["B_cc"].wge(v, 2)
        ctx.dsem["sync"].wait(v)
        for r in range(8):
            v.tensor_scalar(out=GATH[:, r * 16:(r + 1) * 16], in0=GATH[:, r * 16:(r + 1) * 16],
                            scalar1=bmt[:, r:r + 1], scalar2=None, op0=OP.mult)
        for r in range(1, 8):
            v.drain()
            v.tensor_tensor(out=GATH[:, 0:16], in0=GATH[:, 0:16], in1=GATH[:, r * 16:(r + 1) * 16], op=OP.add)
        v.drain()
        v.tensor_scalar(out=GATH[:, 0:8], in0=GATH[:, 0:8], scalar1=1.0 / T, scalar2=None, op0=OP.mult)
        v.tensor_scalar(out=GATH[:, 8:16], in0=GATH[:, 8:16], scalar1=0.5 / 1024.0, scalar2=None, op0=OP.mult)
        v.drain()
        v.tensor_tensor(out=sig, in0=GATH[:, 0:8], in1=GATH[:, 8:16], op=OP.add)
        v.drain()
        v.tensor_copy(sigb, sig)
        ctx.fence(v, "vector", s_cc)

    @blk.tensor
    def _(te):
        ctx.sems["B_cc"].wge(te, 3)
        for hc in range(4):
            for dc in range(8):
                last = nc.tensor.matmul(PS["A"][:, hc:hc + 1],
                                        MNW1[:, dc * 512 + hc * 128:dc * 512 + hc * 128 + 128],
                                        sigb[:, dc:dc + 1], start=(dc == 0), stop=(dc == 7))
        s_met.inc(last)

    @blk.scalar
    def _(s):
        s_met.wait(s)
        ctx.dsem["sync"].wait(s)
        for hc in range(4):
            if hc == 0:
                act_preload(s, "gelu_and_others")
            s.activation(out=hidb[:, hc:hc + 1], in_=PS["A"][:, hc:hc + 1], func=AF.Gelu,
                         bias=mn_b1[:, hc:hc + 1], scale=1.0)
        ctx.fence(s, "scalar", s_met)

    @blk.gpsimd
    def _(g):
        s_met.wait(g)
        ctx.dma(g, "gpsimd", MNW2.rearrange("p (a b) -> p a b", a=4), dram["mn_w2"].rearrange("(a p) c -> p a c", p=128))
        g.wait_ge(ctx.dsem["gpsimd"].h, ctx.dsem["gpsimd"].v)
        ctx.fence(g, "gpsimd", s_w)

    @blk.tensor
    def _(te):
        ctx.sems["B_met"].wge(te, 2)
        ctx.sems["B_w"].wge(te, 3)
        for oc in range(8):
            for hc in range(4):
                last = nc.tensor.matmul(PS["B"][:, oc:oc + 1],
                                        MNW2[:, hc * 1024 + oc * 128:hc * 1024 + oc * 128 + 128],
                                        hidb[:, hc:hc + 1], start=(hc == 0), stop=(hc == 3))
        s_met.inc(last)

    @blk.vector
    def _(v):
        ctx.sems["B_met"].wge(v, 3)
        MW = SM[:, 160:168]
        for oc in range(8):
            v.tensor_scalar(out=MW[:, oc:oc + 1], in0=PS["B"][:, oc:oc + 1],
                            scalar1=mn_b2[:, oc:oc + 1], scalar2=0.1, op0=OP.add, op1=OP.mult)
        v.drain()
        v.tensor_tensor(out=MW, in0=MW, in1=base_m, op=OP.add)
        ctx.fence(v, "vector", s_met)

    @blk.scalar
    def _(s):
        ctx.sems["B_met"].wge(s, 4)
        act_preload(s, "natural_log_exp_and_others")
        s.activation(out=SM[:, 168:176], in_=SM[:, 160:168], func=AF.Exp)
        s.drain()
        s.activation(out=metric, in_=SM[:, 168:176], func=AF.Ln, bias=1.0, scale=1.0)
        ctx.fence(s, "scalar", s_met)

    ctx.tap("q", Q)
    ctx.tap("v", V)
    ctx.tap("sig", sig)
    ctx.tap("metric", metric)


def phaseB2(ctx, dram, PS, st):
    nc, blk = ctx.nc, ctx.blk
    SM, metric = st["SM"], st["metric"]
    onesb_row, onesb_col = st["onesb_row"], st["onesb_col"]
    Q, V = st["Q"], st["V"]
    gflat = dram["g_hbm"]
    s_hbm = dram["s_hbm"]
    w2flat = dram["w2_hbm"]
    misc = dram["misc_hbm"]
    misc2 = dram["misc2_hbm"]
    GREG = 32896
    SREG = NROW + 192

    OS = ctx.at(R2_KB + 70, NH * 512, BF16)
    DEC = ctx.at(R2_KB + 78, 8 * 512, BF16)
    ALB = ctx.at(R2_KB + 86, NTOK, BF16)
    HM = ctx.at(R2_KB + 90, NH * 128, BF16)
    ZB = ctx.at(R2_KB + 92, 256, BF16)
    HBU = ctx.at(R2_KB + 93, 8 * 128, BF16)
    HBV = ctx.at(R2_KB + 95, 8 * 128, BF16)
    GSB = [ctx.at(R2_KB + 72 + i, 256) for i in range(4)]
    SRW = ctx.at(R2_KB + 70, 512)[0:1, :]

    s_hm = ctx.sem("C_hm")
    s_hme = ctx.sem("C_hme")
    s_qmm2 = ctx.sem("C_qmm2")
    s_qn = ctx.sem("C_qn")
    s_al = ctx.sem("C_al")
    s_kt = ctx.sem("C_kt")
    s_pr = ctx.sem("C_pr")
    s_smm = ctx.sem("C_smm")
    s_srow = ctx.sem("C_srow")
    s_gmm = ctx.sem("C_gmm")
    s_gev = ctx.sem("C_gev")
    s_mw = ctx.sem("C_mw", pin=True)
    s_sw = ctx.sem("C_sw")
    s_gst = ctx.sem("C_gst")

    KT = ctx.at(R1_KB, NH * NROW, BF16)
    SQB = ctx.at(R2_KB + 72, 1024, BF16)
    QNS = ctx.at(R2_KB + 74, NTOK)
    QW = ctx.at(R2_KB + 82, 1024)
    PRD = ctx.at(R2_KB + 78, NROW, BF16)

    @blk.gpsimd
    def _(g):
        ctx.dma(g, "gpsimd", HBU[0:32, :].rearrange("p (a b) -> p a b", a=8),
                dram["hebUT"].rearrange("(a p) c -> p a c", p=32))
        ctx.dma(g, "gpsimd", HBV[0:32, :].rearrange("p (a b) -> p a b", a=8),
                dram["hebV"].rearrange("(a p) c -> p a c", p=32))
        g.wait_ge(ctx.dsem["gpsimd"].h, ctx.dsem["gpsimd"].v)
        ctx.fence(g, "gpsimd", s_hm)

    @blk.vector
    def _(v):
        v.memset(ZB, 0.0)
        for h in range(NH):
            v.tensor_scalar(out=KT[:, h * NROW:(h + 1) * NROW], in0=Q[:, h * NROW:(h + 1) * NROW],
                            scalar1=metric[:, h:h + 1], scalar2=None, op0=OP.mult)
        ctx.fence(v, "vector", s_kt)

    @blk.gpsimd
    def _(g):
        s_kt.wait(g)
        for i in range(8):
            ctx.dma(g, "gpsimd", bass.AP(w2flat.tensor, i * 32768, [[256, 128], [1, 256]]), ZB)
        ctx.fence(g, "gpsimd", s_kt)

    @blk.tensor
    def _(te):
        s_hm.wait(te)
        for h in range(NH):
            if h >= 2:
                ctx.sems["C_hme"].wge(te, h - 1)
            bank = PS["A"][:, 0:128] if h % 2 == 0 else PS["A"][:, 512:640]
            s_hm.inc(nc.tensor.matmul(bank, HBU[0:32, h * 128:(h + 1) * 128],
                                      HBV[0:32, h * 128:(h + 1) * 128], start=True, stop=True))

    @blk.vector
    def _(v):
        for h in range(NH):
            ctx.sems["C_hm"].wge(v, 1 + h + 1)
            bank = PS["A"][:, 0:128] if h % 2 == 0 else PS["A"][:, 512:640]
            s_hme.inc(v.tensor_copy(HM[:, h * 128:(h + 1) * 128], bank))

    @blk.scalar
    def _(s):
        for h in range(NH):
            for half in range(2):
                ctx.sems["C_qmm2"].wge(s, 2 * (2 * h + half))
                s.drain()
                s_qn.inc(s.activation(out=SQB, in_=Q[:, h * NROW + HALO + half * 1024:
                                                     h * NROW + HALO + half * 1024 + 1024], func=AF.Square))

    @blk.tensor
    def _(te):
        ctx.sems["C_hm"].wge(te, 1 + NH)
        for k in range(2 * NH):
            ctx.sems["C_qn"].wge(te, k + 1)
            ctx.sems["C_srow"].wge(te, 2 * k)
            for j in range(2):
                bank = PS["B"] if j == 0 else PS["C"]
                s_qmm2.inc(nc.tensor.matmul(bank[0:1, 0:512],
                                            onesb_col, SQB[:, j * 512:(j + 1) * 512],
                                            start=True, stop=True))

    @blk.vector
    def _(v):
        for k in range(2 * NH):
            ctx.sems["C_qmm2"].wge(v, 2 * (k + 1))
            for j in range(2):
                bank = PS["B"] if j == 0 else PS["C"]
                ctx.sems["C_mw"].wge(v, 16 * (2 * k + j))
                s_srow.inc(v.tensor_copy(SRW, bank[0:1, 0:512]))

    @blk.gpsimd
    def _(g):
        for k in range(2 * NH):
            for j in range(2):
                ctx.sems["C_srow"].wge(g, 2 * k + j + 1)
                ctx.sems["C_mw"].wge(g, 16 * (2 * k + j))
                s_mw.inc(g.dma_start(out=bass.AP(misc.tensor, k * 1024 + j * 512, [[512, 1], [1, 512]]),
                                     in_=SRW), 16)
        ctx.sems["C_mw"].wge(g, 16 * 32)
        ctx.dma(g, "gpsimd", QNS[0:8, :], misc.rearrange("(a b) -> a b", b=2048)[0:8, :])
        g.wait_ge(ctx.dsem["gpsimd"].h, ctx.dsem["gpsimd"].v)
        ctx.fence(g, "gpsimd", s_qn)

    @blk.scalar
    def _(s):
        for half in range(2):
            ctx.sems["C_qn"].wge(s, 2 * NH + 1 + half)
            if half == 1:
                ctx.sems["C_al"].wge(s, 1)
            s.drain()
            if half == 0:
                act_preload(s, "sqrt_and_others")
            s_qn.inc(s.activation(out=QW[0:8, :], in_=QNS[0:8, half * 1024:(half + 1) * 1024], func=AF.Sqrt))

    @blk.vector
    def _(v):
        for half in range(2):
            ctx.sems["C_qn"].wge(v, 18 + half)
            sl = slice(half * 1024, (half + 1) * 1024)
            v.drain()
            v.tensor_scalar(out=QW[0:8, :], in0=QW[0:8, :], scalar1=1e-12, scalar2=None, op0=OP.max)
            v.drain()
            v.reciprocal(QW[0:8, :], QW[0:8, :])
            v.drain()
            v.tensor_scalar(out=QW[0:8, :], in0=QW[0:8, :], scalar1=0.1, scalar2=None, op0=OP.mult)
            v.drain()
            v.tensor_scalar(out=QNS[0:8, sl], in0=QNS[0:8, sl], scalar1=0.04, scalar2=None, op0=OP.is_gt)
            v.drain()
            v.tensor_tensor(out=QNS[0:8, sl], in0=QNS[0:8, sl], in1=QW[0:8, :], op=OP.mult)
            v.drain()
            v.tensor_copy(ALB[0:8, sl], QNS[0:8, sl])
            ctx.fence(v, "vector", s_al)
    

    @blk.gpsimd
    def _(g):
        ctx.sems["C_al"].wge(g, 2)
        ctx.dma(g, "gpsimd", misc2.rearrange("(a b) -> a b", b=2048)[0:8, :], ALB[0:8, :])
        g.wait_ge(ctx.dsem["gpsimd"].h, ctx.dsem["gpsimd"].v)
        ctx.fence(g, "gpsimd", s_al)


    @blk.tensor
    def _(te):
        ctx.sems["C_qmm2"].wge(te, 2 * 2 * NH)
        ctx.sems["C_al"].wge(te, 2)
        for h in range(NH):
            ctx.sems["C_pr"].wge(te, h + 1)
            ctx.sems["C_srow"].wge(te, 32 + 5 * h)
            for j in range(5):
                w = min(512, NROW - j * 512)
                bank = [PS["B"], PS["B"], PS["C"], PS["C"], PS["D"]][j]
                col = (j % 2) * 512 if j < 4 else 0
                s_smm.inc(nc.tensor.matmul(bank[0:1, col:col + w], onesb_col,
                                           PRD[:, j * 512:j * 512 + w], start=True, stop=True))

    @blk.vector
    def _(v):
        ctx.sems["C_mw"].wge(v, 16 * 32)
        for h in range(NH + 1):
            ctx.sems["C_smm"].wge(v, 5 * h)
            if h >= 1:
                for j in range(5):
                    w = min(512, NROW - j * 512)
                    bank = [PS["B"], PS["B"], PS["C"], PS["C"], PS["D"]][j]
                    col = (j % 2) * 512 if j < 4 else 0
                    ctx.sems["C_sw"].wge(v, 16 * (5 * (h - 1) + j))
                    s_srow.inc(v.tensor_copy(SRW[:, 0:w], bank[0:1, col:col + w]))
            if h < NH:
                v.drain()
                v.tensor_tensor(out=PRD, in0=Q[:, h * NROW:(h + 1) * NROW],
                                in1=KT[:, h * NROW:(h + 1) * NROW], op=OP.mult)
                ctx.fence(v, "vector", s_pr)

    @blk.sync
    def _(e):
        for h in range(NH):
            for j in range(5):
                w = min(512, NROW - j * 512)
                ctx.sems["C_srow"].wge(e, 32 + 5 * h + j + 1)
                ctx.sems["C_sw"].wge(e, 16 * (5 * h + j))
                s_sw.inc(e.dma_start(out=s_hbm[h:h + 1, j * 512:j * 512 + w], in_=SRW[:, 0:w]), 16)
        ctx.sems["C_sw"].wge(e, 16 * 40)

    ctx.barrier()

    banks4 = [PS["A"][:, 0:256], PS["A"][:, 512:768], PS["B"][:, 0:256], PS["B"][:, 512:768]]
    @blk.tensor
    def _(te):
        k = 0
        for h in range(NH):
            for qt in range(1, NRC):
                if k >= 4:
                    ctx.sems["C_gev"].wge(te, k - 3)
                key0 = (qt - 1) * 128
                s_gmm.inc(nc.tensor.matmul(banks4[k % 4], Q[:, h * NROW + qt * 128:h * NROW + qt * 128 + 128],
                                           KT[:, h * NROW + key0:h * NROW + key0 + 256],
                                           start=True, stop=True))
                k += 1

    @blk.vector
    def _(v):
        for k in range(NH * NQT):
            ctx.sems["C_gmm"].wge(v, k + 1)
            if k % 4 == 0 and k >= 4:
                ctx.sems["C_gst"].wge(v, 64 * (k // 4))
            s_gev.inc(v.tensor_copy(GSB[k % 4], banks4[k % 4]))

    @blk.sync
    def _(e):
        for grp in range(NH * NQT // 4):
            ctx.sems["C_gev"].wge(e, 4 * grp + 4)
            for i in range(4):
                k = 4 * grp + i
                idx = k
                s_gst.inc(e.dma_start(out=bass.AP(gflat.tensor, idx * GREG, [[256, 128], [1, 256]]),
                                      in_=GSB[k % 4]), 16)
            ctx.sems["C_gst"].wge(e, 64 * (grp + 1))

    ctx.barrier()

    WC = ctx.at(R1_KB, 512)
    SU = ctx.at(R1_KB + 2, 512)
    STC = ctx.at(R1_KB + 4, 16)
    NEG = ctx.at(R1_KB + 6, 512)
    SELA = ctx.at(R1_KB + 8, 512)
    SELB = ctx.at(R1_KB + 10, 512)
    MX = ctx.at(R1_KB + 12, 8)
    MASK = ctx.at(R1_KB + 13, 512)
    NEGC = ctx.at(R1_KB + 15, 512)
    DSTT = ctx.at(R1_KB + 17, 512)
    WEXP = ctx.at(R1_KB + 19, 512)
    WW = ctx.at(R1_KB + 21, 512)
    RS = ctx.at(R1_KB + 23, 24)
    W2R = ctx.at(R1_KB + 24, 2048, BF16)
    ARW = ctx.at(R1_KB + 28, 1024, BF16)[0:1, :]
    QSC = ctx.at(R1_KB + 30, 128, BF16)
    WWB = ctx.at(R1_KB + 31, 512, BF16)

    if "W2T" not in st:
        st["W2T"] = [ctx.stack.enter_context(
            nc.sbuf_tensor(f"w2t{i}", [128, 128], BF16, is_transpose_dest=True)) for i in range(2)]
    W2T = st["W2T"]

    s_wld = ctx.sem("C_wld")
    s_sel = ctx.sem("C_sel")
    s_dst = ctx.sem("C_dst")
    s_wn = ctx.sem("C_wn")
    s_wst = ctx.sem("C_wst")
    s_w2r = ctx.sem("C_w2r")
    s_tp = ctx.sem("C_tp")
    s_abc = ctx.sem("C_abc")
    s_qsc = ctx.sem("C_qsc")
    s_ctx2 = ctx.sem("C_ctx2")
    s_os = ctx.sem("C_os")
    s_dmm = ctx.sem("C_dmm")
    s_dec = ctx.sem("C_dec")
    s_dw = ctx.sem("C_dw")

    NQ = NRC - 1

    @blk.sync
    def _(e):
        for qt in range(1, NRC):
            if qt >= 2:
                ctx.sems["C_os"].wge(e, (qt - 1) * NH)
            for h in range(NH):
                idx = h * NQT + (qt - 1)
                ctx.dma(e, "sync", WC[:, h * 64:(h + 1) * 64],
                        bass.AP(gflat.tensor, idx * GREG + 65, [[257, 128], [1, 64]]))
                ctx.dma(e, "sync", SU[:, h * 64:(h + 1) * 64],
                        bass.AP(s_hbm.tensor, h * SREG + qt * 128 - 63, [[1, 128], [1, 64]]))
                ctx.dma(e, "sync", STC[:, h:h + 1],
                        bass.AP(s_hbm.tensor, h * SREG + qt * 128, [[1, 128], [1, 1]]))
            e.wait_ge(ctx.dsem["sync"].h, ctx.dsem["sync"].v)
            ctx.fence(e, "sync", s_wld)
            ctx.sems["C_wn"].wge(e, qt)
            for h in range(NH):
                ctx.dma(e, "sync", bass.AP(w2flat.tensor, h * 32768 + 65, [[257, 128], [1, 64]]),
                        WWB[:, h * 64:(h + 1) * 64])
            e.wait_ge(ctx.dsem["sync"].h, ctx.dsem["sync"].v)
            ctx.sems["C_wst"].wge(e, qt)
            ctx.sems["C_tp"].wge(e, (qt - 1) * NH)
            for h in range(NH):
                ctx.dma(e, "sync", W2R[:, h * 256:(h + 1) * 256],
                        bass.AP(w2flat.tensor, h * 32768, [[256, 128], [1, 256]]))
            e.wait_ge(ctx.dsem["sync"].h, ctx.dsem["sync"].v)
            ctx.fence(e, "sync", s_w2r)

    @blk.vector
    def _(v):
        for qt in range(1, NRC):
            ctx.sems["C_wld"].wge(v, qt)
            v.scalar_tensor_tensor(out=NEG, in0=WC, scalar=2.0, in1=SU,
                                   op0=OP.mult, op1=OP.subtract)
            v.drain()
            for h in range(NH):
                v.tensor_scalar(out=NEG[:, h * 64:(h + 1) * 64], in0=NEG[:, h * 64:(h + 1) * 64],
                                scalar1=STC[:, h:h + 1], scalar2=None, op0=OP.subtract)
            v.drain()
            for h in range(NH):
                cur = NEG[:, h * 64:(h + 1) * 64]
                for r in range(4):
                    nxt = (SELA if r % 2 == 0 else SELB)[:, h * 64:(h + 1) * 64]
                    v.max(out=MX, in_=cur)
                    v.tensor_copy(nxt, cur)
                    v.drain()
                    v.match_replace(out=nxt, in_to_replace=MX, in_values=nxt, imm_value=-2.0e38)
                    v.drain()
                    cur = nxt
            v.tensor_scalar(out=MASK, in0=SELB, scalar1=-1.0e38, scalar2=None, op0=OP.is_le)
            v.tensor_scalar(out=NEGC, in0=NEG, scalar1=0.0, scalar2=None, op0=OP.min)
            ctx.fence(v, "vector", s_sel)
            ctx.sems["C_dst"].wge(v, qt)
            v.tensor_tensor(out=WW, in0=WEXP, in1=MASK, op=OP.mult)
            v.drain()
            v.tensor_reduce(RS[:, 0:8], WW.rearrange("p (a b) -> p a b", b=64), axis=AX.X, op=OP.add)
            v.drain()
            v.tensor_scalar(out=RS[:, 8:16], in0=RS[:, 0:8], scalar1=1e-8, scalar2=None, op0=OP.add)
            v.drain()
            v.reciprocal(RS[:, 16:24], RS[:, 8:16])
            v.drain()
            for h in range(NH):
                v.tensor_scalar(out=WW[:, h * 64:(h + 1) * 64], in0=WW[:, h * 64:(h + 1) * 64],
                                scalar1=RS[:, 16 + h:17 + h], scalar2=None, op0=OP.mult)
            v.drain()
            v.tensor_copy(WWB, WW)
            ctx.fence(v, "vector", s_wn)
            for h in range(NH):
                ctx.sems["C_abc"].wge(v, (qt - 1) * NH + h + 1)
                v.tensor_tensor(out=QSC, in0=Q[:, h * NROW + qt * 128:h * NROW + qt * 128 + 128],
                                in1=PS["B"][:, 0:128], op=OP.mult)
                ctx.fence(v, "vector", s_qsc)
                ctx.sems["C_ctx2"].wge(v, (qt - 1) * NH + h + 1)
                s_os.inc(v.tensor_tensor(out=OS[:, h * 512 + ((qt - 1) % 4) * 128:h * 512 + ((qt - 1) % 4) * 128 + 128],
                                         in0=Q[:, h * NROW + qt * 128:h * NROW + qt * 128 + 128],
                                         in1=PS["A"][:, 0:128], op=OP.mult))
            if qt % 4 == 0:
                tg = qt // 4 - 1
                for p in range(2):
                    ctx.sems["C_dmm"].wge(v, tg * 8 + p * 4 + 4)
                    for oc in range(4):
                        bank = [PS["C"][:, 0:512], PS["C"][:, 512:1024], PS["D"][:, 0:512], PS["D"][:, 512:1024]][oc]
                        ocg = p * 4 + oc
                        s_dec.inc(v.tensor_tensor(out=xf_ap(ctx, ocg, HALO + tg * 512, 512),
                                                  in0=xf_ap(ctx, ocg, HALO + tg * 512, 512),
                                                  in1=bank, op=OP.add))
                v.drain()

    @blk.scalar
    def _(s):
        for qt in range(1, NRC):
            ctx.sems["C_sel"].wge(s, qt)
            s.drain()
            act_preload(s, "sqrt_and_others")
            s.activation(out=DSTT, in_=NEGC, func=AF.Sqrt, bias=SM[:, 1:2], scale=-1.0)
            s.drain()
            act_preload(s, "natural_log_exp_and_others")
            s.activation(out=WEXP, in_=DSTT, func=AF.Exp, scale=-1.0)
            ctx.fence(s, "scalar", s_dst)
            ctx.sems["C_w2r"].wge(s, qt)
            for h in range(NH):
                ctx.dsem["scalar"].inc(s.dma_start(out=W2T[0][:, :], in_=W2R[:, h * 256:h * 256 + 128], transpose=True), 16)
                ctx.dsem["scalar"].inc(s.dma_start(out=W2T[1][:, :], in_=W2R[:, h * 256 + 128:h * 256 + 256], transpose=True), 16)
                s.wait_ge(ctx.dsem["scalar"].h, ctx.dsem["scalar"].v)
                ctx.fence(s, "scalar", s_tp)
                ctx.sems["C_ctx2"].wge(s, (qt - 1) * NH + h + 1)

    @blk.gpsimd
    def _(g):
        for qt in range(1, NRC):
            if (qt - 1) % 4 == 0:
                tg = (qt - 1) // 4
                ctx.sems["C_dmm"].wge(g, tg * 8)
                ctx.dma(g, "gpsimd", DEC.rearrange("p (a b) -> p a b", a=8),
                        dram["decT"].rearrange("(a p) c -> p a c", p=128)[:, :, 0:512])
                g.wait_ge(ctx.dsem["gpsimd"].h, ctx.dsem["gpsimd"].v)
                ctx.fence(g, "gpsimd", s_dw)
            ctx.sems["C_wn"].wge(g, qt)
            ctx.dma(g, "gpsimd", ARW.rearrange("a (b c) -> a b c", b=8),
                    bass.AP(misc2.tensor, (qt - 1) * 128, [[1, 1], [2048, 8], [1, 128]]))
            g.wait_ge(ctx.dsem["gpsimd"].h, ctx.dsem["gpsimd"].v)
            ctx.fence(g, "gpsimd", s_wst)
            if qt % 4 == 0:
                tg = qt // 4 - 1
                ctx.sems["C_dmm"].wge(g, tg * 8 + 4)
                ctx.dma(g, "gpsimd", DEC.rearrange("p (a b) -> p a b", a=8),
                        dram["decT"].rearrange("(a p) c -> p a c", p=128)[:, :, 512:1024])
                g.wait_ge(ctx.dsem["gpsimd"].h, ctx.dsem["gpsimd"].v)
                ctx.fence(g, "gpsimd", s_dw)

    @blk.tensor
    def _(te):
        for qt in range(1, NRC):
            for h in range(NH):
                ctx.sems["C_tp"].wge(te, (qt - 1) * NH + h + 1)
                if (qt - 1) * NH + h >= 1:
                    ctx.sems["C_os"].wge(te, (qt - 1) * NH + h)
                s_abc.inc(nc.tensor.matmul(PS["B"][:, 0:128], onesb_row,
                                           ARW[:, h * 128:(h + 1) * 128], start=True, stop=True))
                nc.tensor.matmul(PS["A"][:, 0:128], V[:, ((qt - 1) * NH + h) * 128:((qt - 1) * NH + h) * 128 + 128],
                                 W2T[0][:, :], start=True, stop=False)
                nc.tensor.matmul(PS["A"][:, 0:128], V[:, (qt * NH + h) * 128:(qt * NH + h) * 128 + 128],
                                 W2T[1][:, :], start=False, stop=False)
                ctx.sems["C_qsc"].wge(te, (qt - 1) * NH + h + 1)
                s_ctx2.inc(nc.tensor.matmul(PS["A"][:, 0:128], HM[:, h * 128:(h + 1) * 128],
                                            QSC, start=False, stop=True))
            if qt % 4 == 0:
                tg = qt // 4 - 1
                ctx.sems["C_os"].wge(te, qt * NH)
                for p in range(2):
                    ctx.sems["C_dw"].wge(te, tg * 2 + p + 1)
                    if tg * 8 + p * 4 >= 4:
                        ctx.sems["C_dec"].wge(te, tg * 8 + p * 4 - 4 + 4)
                    for oc in range(4):
                        bank = [PS["C"][:, 0:512], PS["C"][:, 512:1024], PS["D"][:, 0:512], PS["D"][:, 512:1024]][oc]
                        for hc in range(NH):
                            last = nc.tensor.matmul(bank, DEC[:, hc * 512 + oc * 128:hc * 512 + oc * 128 + 128],
                                                    OS[:, hc * 512:(hc + 1) * 512],
                                                    start=(hc == 0), stop=(hc == 7))
                        s_dmm.inc(last)




_BUILD_CACHE = {}

def _get_program():
    key = (PHASES, tuple(_TAPS))
    if key not in _BUILD_CACHE:
        _BUILD_CACHE[key] = build_program()
    return _BUILD_CACHE[key]


def _shard_inputs(inputs):
    f32 = lambda a: np.ascontiguousarray(np.asarray(a, np.float32))
    x = f32(inputs["L0_attn_x"])
    eq = f32(inputs["L0_attn_encoder_query"])
    ev = f32(inputs["L0_attn_encoder_value"])
    com = {
        "eqT": np.ascontiguousarray(eq.transpose(1, 0, 2).reshape(D, NH * 128)),
        "evT": np.ascontiguousarray(ev.transpose(1, 0, 2).reshape(D, NH * 128)),
        "mn_w1": f32(inputs["L0_attn_mn_w1"]),
        "mn_b1": np.ascontiguousarray(f32(inputs["L0_attn_mn_b1"]).reshape(4, 128).T),
        "mn_w2": f32(inputs["L0_attn_mn_w2"]),
        "mn_b2": np.ascontiguousarray(f32(inputs["L0_attn_mn_b2"]).reshape(8, 128).T),
        "base_m": np.ascontiguousarray(f32(inputs["L0_attn_base_metric"]).T),
        "decT": f32(inputs["L0_attn_decoder"]),
        "hebUT": np.ascontiguousarray(f32(inputs["L0_attn_hebbian_U"]).transpose(0, 2, 1).reshape(NH * 32, 128)),
        "hebV": f32(inputs["L0_attn_hebbian_V"]).reshape(NH * 32, 128),
        "f0w1T": np.ascontiguousarray(f32(inputs["L0_ffn_W1"]).T),
        "f0w2T": np.ascontiguousarray(f32(inputs["L0_ffn_W2"]).T),
        "f0b1": np.ascontiguousarray(f32(inputs["L0_ffn_b1"]).reshape(32, 128).T),
        "f0b2": np.ascontiguousarray(f32(inputs["L0_ffn_b2"]).reshape(8, 128).T),
        "w_in": f32(inputs["L1_attn_w_in"]),
        "b_in": np.ascontiguousarray(f32(inputs["L1_attn_b_in"]).reshape(16, 128).T),
        "wgx": f32(inputs["L1_attn_w_gx"]).reshape(8 * 256, 128),
        "bgx": np.ascontiguousarray(f32(inputs["L1_attn_b_gx"]).reshape(8, 128).T),
        "wga": f32(inputs["L1_attn_w_ga"]).reshape(8 * 256, 128),
        "bga": np.ascontiguousarray(f32(inputs["L1_attn_b_ga"]).reshape(8, 128).T),
        "arp": np.ascontiguousarray(f32(inputs["L1_attn_a_real_param"]).reshape(8, 128).T),
        "aim": np.ascontiguousarray(f32(inputs["L1_attn_a_imag_param"]).reshape(8, 128).T),
        "w_out": f32(inputs["L1_attn_w_out"]),
        "b_out": np.ascontiguousarray(f32(inputs["L1_attn_b_out"]).reshape(8, 128).T),
        "gate_w": f32(inputs["L1_ffn_gate_w"]),
        "gate_bt": np.ascontiguousarray(np.repeat(f32(inputs["L1_ffn_gate_b"]).reshape(1, 16), 128, axis=0)),
        "sh_wg": f32(inputs["L1_ffn_shared_wg"]),
        "sh_wu": f32(inputs["L1_ffn_shared_wu"]),
        "sh_wd": f32(inputs["L1_ffn_shared_wd"]),
        "ex_wg": f32(inputs["L1_ffn_experts_wg"]).reshape(NEXP * D, DI),
        "ex_wu": f32(inputs["L1_ffn_experts_wu"]).reshape(NEXP * D, DI),
        "ex_wd": f32(inputs["L1_ffn_experts_wd"]).reshape(NEXP * DI, D),
    }
    maps = []
    for c in range(NCORE):
        b, s = divmod(c, 4)
        start = s * NTOK
        if s == 0:
            halo = np.repeat(x[b, 0:1, :], HALO, axis=0)
        else:
            halo = x[b, start - HALO:start, :]
        xc = np.concatenate([halo, x[b, start:start + NTOK, :]], axis=0)
        bm = np.zeros((128, 8), np.float32)
        bm[:, b * 4:(b + 1) * 4] = 1.0
        rm = np.zeros((128, 8), np.float32)
        rm[:, c] = 1.0
        m = dict(com)
        m["x_fm"] = np.ascontiguousarray(xc.T)
        m["bmt"] = bm
        m["rmt"] = rm
        maps.append(m)
    return maps


def kernel(**inputs):
    nc = _get_program()
    in_maps = _shard_inputs(inputs)
    res = run_bass_kernel_spmd(nc, in_maps, core_ids=list(range(NCORE)))
    out = np.zeros((B, T, D), np.float32)
    for c in range(NCORE):
        b, s = divmod(c, 4)
        out[b, s * NTOK:(s + 1) * NTOK, :] = res.results[c]["y_out"].T
    kernel.last_results = res
    return out


# version 08 (final)
# speedup vs baseline: 1.0388x; 1.0174x over previous
import contextlib
import numpy as np
import ml_dtypes
import concourse.bass as bass
import concourse.mybir as mybir
from concourse.bass_utils import run_bass_kernel_spmd
from concourse.alu_op_type import AluOpType as OP

F32 = mybir.dt.float32
BF16 = mybir.dt.bfloat16
AF = mybir.ActivationFunctionType
AX = mybir.AxisListType

B, T, D = 2, 8192, 1024
NCORE = 8
NTOK = 2048
HALO = 128
NROW = HALO + NTOK
NRC = NROW // 128
NH = 8
DFF = 4096
DHID, DINNER = 2048, 1024
NEXP, DI = 16, 512
CHUNK = 64
NCHUNK = NTOK // CHUNK
NQT = NTOK // 128
NT5 = (NROW + 511) // 512

_TAPS = ()
PHASES = 99

ENGS = ("tensor", "vector", "scalar", "gpsimd", "sync")
KB = 512


def tile_w(tt):
    return min(512, NROW - tt * 512)


class Phys:
    __slots__ = ("hh", "tot")
    def __init__(self, hh):
        self.hh, self.tot = hh, 0


class Sem:
    __slots__ = ("ph", "base", "v")
    def __init__(self, ph):
        self.ph, self.base, self.v = ph, ph.tot, 0
    @property
    def h(self):
        return self.ph.hh
    def inc(self, inst, n=1):
        inst.then_inc(self.ph.hh, n)
        self.v += n
        self.ph.tot += n
        return inst
    def wge(self, eng, val):
        eng.wait_ge(self.ph.hh, self.base + val)
    def wait(self, eng, v=None):
        eng.wait_ge(self.ph.hh, self.base + (self.v if v is None else v))


class Ctx:
    def __init__(self, nc, stack, arena):
        self.nc, self.stack, self.arena = nc, stack, arena
        self.sems = {}
        self.blk = None
        self.phase = 0
        self.taps = []
        self._avail = []
        self._scope = None
        self.bsem = {e: self.sem("bar_" + e) for e in ENGS}
        self.dsem = {e: self.sem("dma_" + e) for e in ENGS}

    def sem(self, name, pin=False):
        if name not in self.sems:
            if pin or self._scope is None or not self._avail:
                ph = Phys(self.stack.enter_context(self.nc.semaphore(name)))
            else:
                ph = self._avail.pop()
            s = Sem(ph)
            if self._scope is not None and not pin:
                self._scope.append(name)
            self.sems[name] = s
        return self.sems[name]

    def scope_begin(self):
        assert self._scope is None
        self._scope = []

    def scope_end(self):
        for name in self._scope:
            self._avail.append(self.sems.pop(name).ph)
        self._scope = None

    def at(self, kb_off, n_elem, dtype=F32):
        w = 2 if dtype == F32 else 1
        c0 = kb_off * 512
        ap = self.arena[:, c0:c0 + n_elem * w]
        return ap.bitcast(F32) if dtype == F32 else ap

    def dma(self, eng, ename, out, in_):
        return self.dsem[ename].inc(eng.dma_start(out=out, in_=in_), 16)

    def fence(self, eng, ename, sem, n=1):
        c0 = SM_KB * 512 + 1792 + 32 * ENGS.index(ename)
        d = self.arena[:, c0:c0 + 8]
        d2 = self.arena[:, c0 + 8:c0 + 16]
        if ename in ("vector", "gpsimd"):
            eng.drain()
            return sem.inc(eng.memset(d, 0.0), n)
        if ename == "scalar":
            if not getattr(self, "_sc_primed", False):
                eng.activation(out=d, in_=self.at(XF_KB, 8)[:, 0:8], func=AF.Relu)
                self._sc_primed = True
            eng.drain()
            return sem.inc(eng.activation(out=d2, in_=d, func=AF.Relu), n)
        return sem.inc(eng.nop(), n)

    def barrier(self):
        self.phase += 1
        ph = self.phase
        for e in ENGS:
            def mk(ename):
                def _(eng):
                    self.dsem[ename].wait(eng)
                    self.fence(eng, ename, self.bsem[ename])
                    for o in ENGS:
                        if o != ename:
                            eng.wait_ge(self.bsem[o].h, ph)
                return _
            getattr(self.blk, e)(mk(e))

    def tap(self, name, sbuf_ap):
        if name not in _TAPS:
            return
        t = self.nc.dram_tensor("tap_" + name, list(sbuf_ap.shape), sbuf_ap.dtype,
                                kind="ExternalOutput").ap()
        self.taps.append((t, sbuf_ap))

    def flush_taps(self):
        if not self.taps:
            return
        taps, self.taps = list(self.taps), []
        @self.blk.gpsimd
        def _(g):
            for t, s in taps:
                self.dma(g, "gpsimd", t, s)
            self.dsem["gpsimd"].wait(g)


XF_KB, SM_KB, R1_KB, R2_KB = 0, 68, 72, 107
ARENA_KB = 206


def xf_ap(ctx, dc, r0, w):
    base = XF_KB * 256
    return ctx.at(XF_KB, 8 * NROW)[:, dc * NROW + r0: dc * NROW + r0 + w]


def build_program():
    nc = bass.Bass("TRN2", target_bir_lowering=False, debug=False)
    stack = contextlib.ExitStack()
    arena = stack.enter_context(nc.sbuf_tensor("arena", [128, ARENA_KB * 512], BF16))
    ctx = Ctx(nc, stack, arena)
    for c, val in enumerate(CONSTS):
        nc.const_aps.register(F32, val, ctx.at(SM_KB, 1024)[:, 250 + c:251 + c])

    dram = {}
    def din(name, shape, dtype=F32):
        dram[name] = nc.dram_tensor(name, list(shape), dtype, kind="ExternalInput").ap()
    def dint(name, shape, dtype=F32, space="Local"):
        dram[name] = nc.dram_tensor(name, list(shape), dtype, kind="Internal", addr_space=space).ap()

    din("x_fm", [D, NROW])
    din("eqT", [D, NH * 128])
    din("evT", [D, NH * 128])
    din("mn_w1", [D, 512]); din("mn_b1", [128, 4])
    din("mn_w2", [512, D]); din("mn_b2", [128, 8])
    din("base_m", [128, 8])
    din("decT", [D, D])
    din("hebUT", [NH * 32, 128]); din("hebV", [NH * 32, 128])
    din("f0w1T", [D, DFF]); din("f0w2T", [DFF, D])
    din("f0b1", [128, 32]); din("f0b2", [128, 8])
    din("w_in", [D, DHID]); din("b_in", [128, 16])
    din("wgx", [NB8 := 8 * 256, 128]); din("bgx", [128, 8])
    din("wga", [NB8, 128]); din("bga", [128, 8])
    din("arp", [128, 8]); din("aim", [128, 8])
    din("w_out", [DHID, D]); din("b_out", [128, 8])
    din("gate_w", [D, 16]); din("gate_bt", [128, 16])
    din("sh_wg", [D, DI]); din("sh_wu", [D, DI]); din("sh_wd", [DI, D])
    din("ex_wg", [NEXP * D, DI]); din("ex_wu", [NEXP * D, DI]); din("ex_wd", [NEXP * DI, D])
    din("bmt", [128, 8])
    din("rmt", [128, 8])

    dram["y_out"] = nc.dram_tensor("y_out", [D, NTOK], F32, kind="ExternalOutput").ap()

    dint("g_hbm", [NH * (NQT + 1) * 128 * 257])
    dint("s_hbm", [NH, NROW + 192])
    dint("w2_hbm", [8 * 128 * 256], BF16)
    dint("wq1_hbm", [D * DFF], BF16)
    dint("wq2_hbm", [DFF * D], BF16)
    dint("cc_in1", [1, 2048]); dint("cc_out1", [8, 2048], space="Shared")
    dint("cc_in2", [1, 4096]); dint("cc_out2", [8, 4096], space="Shared")
    dint("misc_hbm", [16384])
    dint("spill_hbm", [4 * 32 * 128, 512], BF16)
    dint("misc2_hbm", [16384], BF16)
    dint("gp_hbm", [128, 2])
    dint("wmoe_hbm", [16, 2048], BF16)

    PS = dict(
        A=stack.enter_context(nc.psum_tensor("psA", [128, 1024], F32)),
        B=stack.enter_context(nc.psum_tensor("psB", [128, 1024], F32)),
        C=stack.enter_context(nc.psum_tensor("psC", [128, 1024], F32)),
        D=stack.enter_context(nc.psum_tensor("psD", [128, 1024], F32)),
    )

    with nc.Block() as blk:
        ctx.blk = blk
        emit(ctx, dram, PS)
    stack.close()
    return nc


NB = 8



def emit_ln(ctx, PS, st, pfx, XN):
    nc, blk = ctx.nc, ctx.blk
    onesb_col, onesb_row = st["onesb_col"], st["onesb_row"]
    XB = ctx.at(R1_KB, 8 * 512, BF16)
    SQ = ctx.at(R1_KB + 8, 8 * 512, BF16)
    SRO_ = ctx.at(R1_KB + 16, 6 * 512)[0:1, :]
    SRO = [SRO_[:, i * 512:(i + 1) * 512] for i in range(6)]
    ABH_ = ctx.at(R1_KB + 28, 4 * 512, BF16)[0:1, :]
    ABH = [ABH_[:, i * 512:(i + 1) * 512] for i in range(4)]
    TMP = ctx.at(R1_KB + 32, 512)
    NT = NTOK // 512

    s_pre = ctx.sem(pfx + "_pre")
    s_mm = ctx.sem(pfx + "_mm")
    s_ab = ctx.sem(pfx + "_ab")
    s_bc = ctx.sem(pfx + "_bc")
    s_ap = ctx.sem(pfx + "_ap")

    def bank_of(tt):
        return PS["A"] if tt % 2 == 0 else PS["B"]

    def pcd(tt):
        pa = PS["C"][:, 0:512] if tt % 2 == 0 else PS["C"][:, 512:1024]
        pb = PS["D"][:, 0:512] if tt % 2 == 0 else PS["D"][:, 512:1024]
        return pa, pb

    @blk.vector
    def _(v):
        for tt in range(NT + 1):
            if tt >= 1:
                j = tt - 1
                ctx.sems[pfx + "_mm"].wge(v, j + 1)
                bank = bank_of(j)
                v.drain()
                v.tensor_copy(SRO[0], bank[0:1, 0:512])
                v.drain()
                v.tensor_copy(SRO[1], bank[0:1, 512:1024])
                v.drain()
                v.tensor_scalar(out=SRO[0], in0=SRO[0], scalar1=1.0 / D, scalar2=None, op0=OP.mult)
                v.tensor_scalar(out=SRO[1], in0=SRO[1], scalar1=1.0 / D, scalar2=None, op0=OP.mult)
                v.drain()
                v.tensor_tensor(out=SRO[2], in0=SRO[0], in1=SRO[0], op=OP.mult)
                v.drain()
                v.tensor_tensor(out=SRO[1], in0=SRO[1], in1=SRO[2], op=OP.subtract)
                v.drain()
                v.tensor_scalar(out=SRO[1], in0=SRO[1], scalar1=1e-5, scalar2=None, op0=OP.add)
                ctx.fence(v, "vector", s_ab)
                ctx.sems[pfx + "_ab"].wge(v, 3 * j + 2)
                v.reciprocal(SRO[3], SRO[2])
                v.drain()
                v.tensor_tensor(out=SRO[4], in0=SRO[0], in1=SRO[3], op=OP.mult)
                v.drain()
                v.tensor_scalar(out=SRO[4], in0=SRO[4], scalar1=-1.0, scalar2=None, op0=OP.mult)
                v.tensor_copy(ABH[0], SRO[3])
                v.drain()
                v.tensor_copy(ABH[2], SRO[4])
                v.drain()
                v.tensor_copy(SRO[5], ABH[0])
                v.drain()
                v.tensor_tensor(out=SRO[5], in0=SRO[3], in1=SRO[5], op=OP.subtract)
                v.drain()
                v.tensor_copy(ABH[1], SRO[5])
                v.drain()
                v.tensor_copy(SRO[5], ABH[2])
                v.drain()
                v.tensor_tensor(out=SRO[5], in0=SRO[4], in1=SRO[5], op=OP.subtract)
                v.drain()
                v.tensor_copy(ABH[3], SRO[5])
                ctx.fence(v, "vector", s_ab)
                ctx.sems[pfx + "_bc"].wge(v, j + 1)
                pa, pb = pcd(j)
                o = j * 512
                for dc in range(8):
                    v.tensor_tensor(out=TMP, in0=xf_ap(ctx, dc, HALO + o, 512), in1=pa, op=OP.mult)
                    v.drain()
                    s_ap.inc(v.tensor_tensor(out=XN[:, dc * NTOK + o:dc * NTOK + o + 512],
                                             in0=TMP, in1=pb, op=OP.add))
                    v.drain()
            if tt < NT:
                o = tt * 512
                if tt >= 1:
                    ctx.sems[pfx + "_mm"].wge(v, tt)
                for dc in range(8):
                    v.tensor_copy(XB[:, dc * 512:dc * 512 + 512], xf_ap(ctx, dc, HALO + o, 512))
                ctx.fence(v, "vector", s_pre)

    @blk.scalar
    def _(s):
        act_preload(s, "sqrt_and_others")
        for tt in range(NT + 1):
            if tt >= 1:
                j = tt - 1
                ctx.sems[pfx + "_ab"].wge(s, 3 * j + 1)
                s.drain()
                s.activation(out=SRO[2], in_=SRO[1], func=AF.Sqrt)
                ctx.fence(s, "scalar", s_ab)
            if tt < NT:
                o = tt * 512
                if tt >= 1:
                    ctx.sems[pfx + "_mm"].wge(s, tt)
                for dc in range(8):
                    s.activation(out=SQ[:, dc * 512:dc * 512 + 512], in_=xf_ap(ctx, dc, HALO + o, 512), func=AF.Square)
                ctx.fence(s, "scalar", s_pre)

    @blk.tensor
    def _(te):
        for tt in range(NT + 1):
            if tt >= 1:
                j = tt - 1
                ctx.sems[pfx + "_ab"].wge(te, 3 * j + 3)
                if j >= 2:
                    ctx.sems[pfx + "_ap"].wge(te, 8 * (j - 1))
                pa, pb = pcd(j)
                nc.tensor.matmul(pa, onesb_row, ABH[0], start=True, stop=False)
                nc.tensor.matmul(pa, onesb_row, ABH[1], start=False, stop=True)
                nc.tensor.matmul(pb, onesb_row, ABH[2], start=True, stop=False)
                s_bc.inc(nc.tensor.matmul(pb, onesb_row, ABH[3], start=False, stop=True))
            if tt < NT:
                ctx.sems[pfx + "_pre"].wge(te, 2 * (tt + 1))
                if tt >= 2:
                    ctx.sems[pfx + "_ab"].wge(te, 3 * (tt - 2) + 1)
                bank = bank_of(tt)
                for dc in range(8):
                    nc.tensor.matmul(bank[0:1, 0:512], onesb_col, XB[:, dc * 512:dc * 512 + 512],
                                     start=(dc == 0), stop=(dc == 7))
                for dc in range(8):
                    last = nc.tensor.matmul(bank[0:1, 512:1024], onesb_col, SQ[:, dc * 512:dc * 512 + 512],
                                            start=(dc == 0), stop=(dc == 7))
                s_mm.inc(last)


def phaseC(ctx, dram, PS, st):
    nc, blk = ctx.nc, ctx.blk
    SM = st["SM"]
    onesb_col, onesb_row = st["onesb_col"], st["onesb_row"]

    WQ1 = ctx.at(R2_KB, 8 * 4096, BF16)
    XN2 = ctx.at(R2_KB + 64, 8 * NTOK, BF16)
    WSTR = [ctx.at(R1_KB + 8 * i, 2048) for i in range(2)]
    ABSD = [ctx.at(R1_KB + 16 + 8 * b, 2048) for b in range(2)]
    ABSF = ctx.at(R1_KB + 16, 2048)
    GP = ctx.at(SM_KB, 1024)[:, 344:392]
    CR = ctx.at(R2_KB + 96, 768)[0:1, :]
    SGNB = ctx.at(R1_KB + 24, 2048, BF16)
    MSKB = ctx.at(R1_KB + 28, 2048, BF16)
    QOUT = ctx.at(R1_KB + 32, 1024, BF16)
    HALL = ctx.at(R1_KB, 32 * 256, BF16)
    W2D = [ctx.at(R1_KB + 16 + 8 * i, 4096, BF16) for i in range(2)]
    GB = ctx.at(SM_KB + 2, 256, BF16)
    b1t = SM[:, 200:232]
    b2t = SM[:, 232:240]
    gsc = SM[:, 240:248]

    s_w = ctx.sem("F_w")
    s_gp = ctx.sem("F_gp")
    s_gam = ctx.sem("F_gam")
    s_qv = ctx.sem("F_qv")
    s_qs = ctx.sem("F_qs")
    s_qst = ctx.sem("F_qst", pin=True)
    s_h = ctx.sem("F_h")
    s_he = ctx.sem("F_he")
    s_o = ctx.sem("F_o")
    s_oe = ctx.sem("F_oe")
    s_w2a = ctx.sem("F_w2a")
    s_w2b = ctx.sem("F_w2b")

    w1t = dram["f0w1T"]
    w2t = dram["f0w2T"]
    wq2flat = dram["wq2_hbm"]

    chunks = []
    for dc in range(8):
        for half in range(2):
            chunks.append(("w1", dc, half, 2048,
                           bass.AP(w1t.tensor, dc * 128 * 4096 + half * 2048, [[4096, 128], [1, 2048]])))
    for i in range(32):
        chunks.append(("w2", i, 0, 1024,
                       bass.AP(w2t.tensor, i * 128 * 1024, [[1024, 128], [1, 1024]])))
    NCH = len(chunks)

    @blk.gpsimd
    def _(g):
        ctx.dma(g, "gpsimd", b1t, dram["f0b1"])
        ctx.dma(g, "gpsimd", b2t, dram["f0b2"])
        g.wait_ge(ctx.dsem["gpsimd"].h, ctx.dsem["gpsimd"].v)

    @blk.sync
    def _(e):
        for i, (kind, a, b, w, ap) in enumerate(chunks):
            if i >= 2:
                ctx.sems["F_qv"].wge(e, i - 1)
            ctx.dma(e, "sync", WSTR[i % 2][:, 0:w], ap)
            e.wait_ge(ctx.dsem["sync"].h, ctx.dsem["sync"].v)
            ctx.fence(e, "sync", s_w)

    @blk.scalar
    def _(s):
        for i, (kind, a, b, w, ap) in enumerate(chunks):
            ctx.sems["F_w"].wge(s, i + 1)
            if i >= 2:
                ctx.sems["F_gp"].wge(s, i - 1)
            s.drain()
            s_qv.inc(s.activation(out=ABSD[i % 2][:, 0:w], in_=WSTR[i % 2][:, 0:w], func=AF.Abs))

    @blk.vector
    def _(v):
        for i, (kind, a, b, w, ap) in enumerate(chunks):
            ctx.sems["F_qv"].wge(v, i + 1)
            v.drain()
            s_gp.inc(v.tensor_reduce(GP[:, i:i + 1], ABSD[i % 2][:, 0:w], axis=AX.X, op=OP.add))
        v.drain()
        v.tensor_reduce(gsc[:, 6:7], GP[:, 0:16], axis=AX.X, op=OP.add)
        v.drain()
        v.tensor_reduce(gsc[:, 7:8], GP[:, 16:NCH], axis=AX.X, op=OP.add)
        ctx.fence(v, "vector", s_gam)

    @blk.sync
    def _(e):
        ctx.sems["F_gam"].wge(e, 1)
        ctx.dma(e, "sync", dram["gp_hbm"], gsc[:, 6:8])
        e.wait_ge(ctx.dsem["sync"].h, ctx.dsem["sync"].v)
        ctx.dma(e, "sync", CR[:, 0:256], dram["gp_hbm"].rearrange("p q -> (p q)").rearrange("(a b) -> a b", a=1))
        e.wait_ge(ctx.dsem["sync"].h, ctx.dsem["sync"].v)
        ctx.fence(e, "sync", s_gam)

    @blk.vector
    def _(v):
        T, S1, S2, CC = CR[:, 0:256], CR[:, 256:384], CR[:, 384:512], CR[:, 512:640]
        ctx.sems["F_gam"].wge(v, 2)
        v.drain()
        v.tensor_copy(S1.rearrange("a (p q) -> a p q", q=1),
                      T.rearrange("a (p q) -> a p q", q=2)[:, :, 0:1])
        v.drain()
        v.tensor_copy(S2.rearrange("a (p q) -> a p q", q=1),
                      T.rearrange("a (p q) -> a p q", q=2)[:, :, 1:2])
        for t, S in ((0, S1), (1, S2)):
            v.drain()
            v.memset(CC, 0.0)
            m = 128
            while m > 1:
                h = m // 2
                a, b2 = S[:, 0:h], S[:, h:m]
                v.drain()
                v.tensor_tensor(out=T[:, 0:h], in0=a, in1=b2, op=OP.add)
                v.drain()
                v.tensor_tensor(out=T[:, h:2 * h], in0=T[:, 0:h], in1=a, op=OP.subtract)
                v.drain()
                v.tensor_tensor(out=T[:, 2 * h:3 * h], in0=T[:, 0:h], in1=T[:, h:2 * h], op=OP.subtract)
                v.drain()
                v.tensor_tensor(out=T[:, 2 * h:3 * h], in0=a, in1=T[:, 2 * h:3 * h], op=OP.subtract)
                v.drain()
                v.tensor_tensor(out=T[:, 3 * h:4 * h], in0=b2, in1=T[:, h:2 * h], op=OP.subtract)
                v.drain()
                v.tensor_tensor(out=T[:, 2 * h:3 * h], in0=T[:, 2 * h:3 * h], in1=T[:, 3 * h:4 * h], op=OP.add)
                v.drain()
                v.tensor_tensor(out=CC[:, 0:h], in0=CC[:, 0:h], in1=CC[:, h:m], op=OP.add)
                v.drain()
                v.tensor_tensor(out=CC[:, 0:h], in0=CC[:, 0:h], in1=T[:, 2 * h:3 * h], op=OP.add)
                v.drain()
                v.tensor_copy(S[:, 0:h], T[:, 0:h])
                m = h
            v.drain()
            v.tensor_tensor(out=S[:, 0:1], in0=S[:, 0:1], in1=CC[:, 0:1], op=OP.add)
            v.drain()
            v.tensor_scalar(out=S[:, 1:2], in0=S[:, 0:1], scalar1=2.0 ** -22, scalar2=1e-5,
                            op0=OP.mult, op1=OP.add)
            v.drain()
            v.tensor_scalar(out=gsc[0:1, 2 + t:3 + t], in0=S[:, 1:2], scalar1=0.5, scalar2=None, op0=OP.mult)
        v.drain()
        v.tensor_copy(GB[0:1, 144:146], gsc[0:1, 2:4])
        v.drain()
        v.tensor_copy(gsc[0:1, 6:8], GB[0:1, 144:146])
        v.drain()
        v.tensor_tensor(out=gsc[0:1, 6:8], in0=gsc[0:1, 2:4], in1=gsc[0:1, 6:8], op=OP.subtract)
        v.drain()
        v.tensor_copy(GB[0:1, 146:148], gsc[0:1, 6:8])
        v.drain()
        v.tensor_copy(CR[:, 640:642], GB[0:1, 146:148])
        v.drain()
        v.tensor_tensor(out=gsc[0:1, 6:8], in0=gsc[0:1, 6:8], in1=CR[:, 640:642], op=OP.subtract)
        v.drain()
        v.tensor_copy(GB[0:1, 148:150], gsc[0:1, 6:8])
        ctx.fence(v, "vector", s_gam)

    @blk.tensor
    def _(te):
        ctx.sems["F_gam"].wge(te, 3)
        for t in range(2):
            nc.tensor.matmul(PS["B"][:, t:t + 1], onesb_row, GB[0:1, 144 + t:145 + t], start=True, stop=False)
            nc.tensor.matmul(PS["B"][:, t:t + 1], onesb_row, GB[0:1, 146 + t:147 + t], start=False, stop=False)
            last = nc.tensor.matmul(PS["B"][:, t:t + 1], onesb_row, GB[0:1, 148 + t:149 + t], start=False, stop=True)
        s_gam.inc(last)

    @blk.vector
    def _(v):
        ctx.sems["F_gam"].wge(v, 4)
        v.tensor_copy(gsc[:, 4:6], PS["B"][:, 0:2])
        ctx.fence(v, "vector", s_gam)

    @blk.sync
    def _(e):
        def store(a):
            ctx.sems["F_qv"].wge(e, NCH + 16 + a + 1)
            ctx.sems["F_qst"].wge(e, 16 * a)
            dst = bass.AP(wq2flat.tensor, a * 128, [[4096, 128], [524288, 8], [1, 128]])
            s_qst.inc(e.dma_start(out=dst, in_=QOUT[:, 0:1024].rearrange("p (a b) -> p a b", a=8)), 16)
        for i in range(NCH):
            kind, a, b, w, ap = chunks[i]
            if i >= 1:
                ctx.sems["F_qv"].wge(e, NCH + i - 1)
            ctx.dma(e, "sync", WSTR[i % 2][:, 0:w], ap)
            e.wait_ge(ctx.dsem["sync"].h, ctx.dsem["sync"].v)
            ctx.fence(e, "sync", s_w)
            if i >= 17:
                store(i - 17)
        for a2 in range(NCH - 17, 32):
            store(a2)
        ctx.sems["F_qst"].wge(e, 16 * 32)

    @blk.scalar
    def _(s):
        ctx.sems["F_gam"].wge(s, 5)
        for i in range(NCH):
            kind, a, b, w, ap = chunks[i]
            ctx.sems["F_w"].wge(s, NCH + i + 1)
            if i >= 1:
                ctx.sems["F_qv"].wge(s, NCH + i)
            s.drain()
            s_qs.inc(s.activation(out=SGNB[:, 0:w], in_=WSTR[i % 2][:, 0:w], func=AF.Sign))

    @blk.vector
    def _(v):
        for i in range(NCH):
            kind, a, b, w, ap = chunks[i]
            ctx.sems["F_qs"].wge(v, i + 1)
            thr = gsc[:, 4:5] if kind == "w1" else gsc[:, 5:6]
            v.drain()
            v.tensor_scalar(out=ABSF[:, 0:w], in0=WSTR[i % 2][:, 0:w], scalar1=-1.0, scalar2=None, op0=OP.mult)
            v.drain()
            v.tensor_tensor(out=ABSF[:, 0:w], in0=ABSF[:, 0:w], in1=WSTR[i % 2][:, 0:w], op=OP.max)
            v.drain()
            v.tensor_scalar(out=MSKB[:, 0:w], in0=ABSF[:, 0:w], scalar1=thr, scalar2=None, op0=OP.is_ge)
            v.drain()
            if kind == "w1":
                dst = WQ1[:, a * 4096 + b * 2048: a * 4096 + b * 2048 + 2048]
                s_qv.inc(v.tensor_tensor(out=dst, in0=SGNB[:, 0:w], in1=MSKB[:, 0:w], op=OP.mult))
            else:
                ctx.sems["F_qst"].wge(v, 16 * a)
                s_qv.inc(v.tensor_tensor(out=QOUT[:, 0:w], in0=SGNB[:, 0:w], in1=MSKB[:, 0:w], op=OP.mult))
            v.drain()

    ctx.tap("wq1", WQ1)
    ctx.barrier()
    if PHASES == 35:
        return

    emit_ln(ctx, PS, st, "LN2", XN2)
    ctx.barrier()

    ctx.tap("xn2", XN2)
    NTT = NTOK // 256
    @blk.sync
    def _(e):
        for tt in range(NTT):
            for dout in range(8):
                k = tt * 8 + dout
                if k >= 2:
                    ctx.sems["F_o"].wge(e, k - 1)
                ctx.dma(e, "sync", W2D[k % 2], bass.AP(wq2flat.tensor, dout * 524288, [[4096, 128], [1, 4096]]))
                e.wait_ge(ctx.dsem["sync"].h, ctx.dsem["sync"].v)
                ctx.fence(e, "sync", s_w2a)

    @blk.tensor
    def _(te):
        for tt in range(NTT):
            o = tt * 256
            for f in range(32):
                k = tt * 32 + f
                hb = PS["C"][:, 0:256] if f % 2 == 0 else PS["C"][:, 512:768]
                if k >= 2:
                    ctx.sems["F_he"].wge(te, k - 1)
                for dc in range(8):
                    last = nc.tensor.matmul(hb, WQ1[:, dc * 4096 + f * 128:dc * 4096 + f * 128 + 128],
                                            XN2[:, dc * NTOK + o:dc * NTOK + o + 256],
                                            start=(dc == 0), stop=(dc == 7))
                s_h.inc(last)
            ctx.sems["F_he"].wge(te, (tt + 1) * 32)
            for dout in range(8):
                kd = tt * 8 + dout
                ctx.sems["F_w2a"].wge(te, kd + 1)
                if kd >= 2:
                    ctx.sems["F_oe"].wge(te, kd - 1)
                ob = PS["A"][:, 0:256] if dout % 2 == 0 else PS["A"][:, 512:768]
                for f in range(32):
                    last = nc.tensor.matmul(ob, W2D[kd % 2][:, f * 128:f * 128 + 128],
                                            HALL[:, f * 256:f * 256 + 256],
                                            start=(f == 0), stop=(f == 31))
                s_o.inc(last)

    @blk.scalar
    def _(s):
        for tt in range(NTT):
            if tt >= 1:
                ctx.sems["F_o"].wge(s, tt * 8)
            for f in range(32):
                k = tt * 32 + f
                ctx.sems["F_h"].wge(s, k + 1)
                hb = PS["C"][:, 0:256] if f % 2 == 0 else PS["C"][:, 512:768]
                s.drain()
                s_he.inc(s.activation(out=HALL[:, f * 256:f * 256 + 256], in_=hb,
                                      func=AF.Relu, bias=b1t[:, f:f + 1], scale=1.0))

    @blk.vector
    def _(v):
        for tt in range(NTT):
            o = tt * 256
            for dout in range(8):
                kd = tt * 8 + dout
                ctx.sems["F_o"].wge(v, kd + 1)
                ob = PS["A"][:, 0:256] if dout % 2 == 0 else PS["A"][:, 512:768]
                s_oe.inc(v.scalar_tensor_tensor(out=xf_ap(ctx, dout, HALO + o, 256), in0=ob,
                                                scalar=b2t[:, dout:dout + 1], in1=xf_ap(ctx, dout, HALO + o, 256),
                                                op0=OP.add, op1=OP.add))
                v.drain()




CONSTS = (1.5707963267948966, -44.0, 44.0)


def act_preload(s, setname):
    from concourse.hw_specs import get_activation_tables
    names = list(get_activation_tables(s.bass.m.arch).keys())
    return s.add_instruction(mybir.InstLoadActFuncSet(
        name=s.bass.get_next_instruction_name(),
        act_func_set_id=names.index(setname), ins=[], outs=[]))
EXPB = 44.0
EMB = 7.781132241133797e-20


def phaseD(ctx, dram, PS, st):
    nc, blk = ctx.nc, ctx.blk
    SM = st["SM"]

    XN3 = ctx.at(R2_KB, 8 * NTOK, BF16)
    XH = ctx.at(R2_KB + 32, 16 * 512, BF16)
    GX = ctx.at(R2_KB + 48, 8 * 512, BF16)
    GA = ctx.at(R2_KB + 56, 8 * 512)
    PL = [ctx.at(R2_KB + 72 + 2 * i, 512) for i in range(12)]
    HMB = ctx.at(R2_KB + 32, 16 * 512, BF16)
    RLB = [ctx.at(R2_KB + 48 + 4 * i, 2048, BF16) for i in range(2)]
    QALL = ctx.at(R1_KB, 1056)
    QSTr, QSTi = QALL[:, 0:264], QALL[:, 264:528]
    CLOr, CLOi = QALL[:, 528:792], QALL[:, 792:1056]
    MISC = ctx.at(R1_KB + 5, 416)
    T8 = MISC[:, 0:64]
    ECOS, ESIN = MISC[:, 64:72], MISC[:, 72:80]
    ECP, ESP = MISC[:, 80:88], MISC[:, 88:96]
    EZR, EZI = MISC[:, 96:104], MISC[:, 104:112]
    SUMT = MISC[:, 112:144]
    CIN = MISC[:, 144:160]
    TRT = [MISC[:, 160 + 32 * r:192 + 32 * r] for r in range(8)]
    OZ = ctx.at(R1_KB + 7, 66)
    ONES64, ZCOL = OZ[:, 0:64], OZ[:, 64:65]
    MSPT = ctx.at(R1_KB + 8, 512)
    AIMT = ctx.at(R1_KB + 10, 512)
    WGX = ctx.at(R1_KB + 12, 2048, BF16)
    WGA = ctx.at(R1_KB + 16, 2048, BF16)
    WINB = [ctx.at(R1_KB + 20 + 2 * i, 1024, BF16) for i in range(2)]
    SPB = [ctx.at(R1_KB + 24 + 4 * i, 2048, BF16) for i in range(2)]

    b_in = SM[:, 256:272]
    bgx = SM[:, 272:280]
    bga = SM[:, 280:288]
    msp = SM[:, 288:296]
    aim = SM[:, 296:304]
    b_out = SM[:, 304:312]
    smt = SM[:, 312:320]
    stmp = SM[:, 320:328]

    s_sml = ctx.sem("D_sml")
    s_dw = ctx.sem("D_dw")
    s_pre2 = ctx.sem("D_pre2")
    s_xh = ctx.sem("D_xh")
    s_xhe = ctx.sem("D_xhe")
    s_gmm = ctx.sem("D_gmm")
    s_ge = ctx.sem("D_ge")
    s_v1 = ctx.sem("D_v1")
    s_s1 = ctx.sem("D_s1")
    s_v2 = ctx.sem("D_v2")
    s_s2 = ctx.sem("D_s2")
    s_v3 = ctx.sem("D_v3")
    s_car = ctx.sem("D_car")
    s_sp = ctx.sem("D_sp", pin=True)
    s_sum = ctx.sem("D_sum")
    s_cc = ctx.sem("D_cc", pin=True)
    s_cin = ctx.sem("D_cin")
    s_rl = ctx.sem("D_rl")
    s_h = ctx.sem("D_h")
    s_wo = ctx.sem("D_wo")
    s_om = ctx.sem("D_om")
    s_oe = ctx.sem("D_oe")

    @blk.gpsimd
    def _(g):
        for dst, key in ((b_in, "b_in"), (bgx, "bgx"), (bga, "bga"), (msp, "arp"),
                         (aim, "aim"), (b_out, "b_out"), (smt, "rmt")):
            ctx.dma(g, "gpsimd", dst, dram[key])
        g.wait_ge(ctx.dsem["gpsimd"].h, ctx.dsem["gpsimd"].v)
        ctx.fence(g, "gpsimd", s_sml)

    @blk.scalar
    def _(s):
        s_sml.wait(s)
        s.drain()
        act_preload(s, "natural_log_exp_and_others")
        s.activation(out=stmp, in_=msp, func=AF.Exp)
        s.drain()
        s.activation(out=msp, in_=stmp, func=AF.Ln, bias=1.0, scale=1.0)
        ctx.fence(s, "scalar", s_sml)

    @blk.vector
    def _(v):
        ctx.sems["D_sml"].wge(v, 2)
        v.drain()
        v.tensor_scalar(out=msp, in0=msp, scalar1=-8.0, scalar2=None, op0=OP.mult)
        ctx.fence(v, "vector", s_sml)

    emit_ln(ctx, PS, st, "LN3", XN3)
    ctx.barrier()

    @blk.gpsimd
    def _(g):
        ctx.dma(g, "gpsimd", WGX.rearrange("p (a b) -> p a b", a=16),
                dram["wgx"].rearrange("(a p) c -> p a c", p=128))
        ctx.dma(g, "gpsimd", WGA.rearrange("p (a b) -> p a b", a=16),
                dram["wga"].rearrange("(a p) c -> p a c", p=128))
        g.wait_ge(ctx.dsem["gpsimd"].h, ctx.dsem["gpsimd"].v)
        ctx.fence(g, "gpsimd", s_pre2)

    @blk.vector
    def _(v):
        v.memset(ONES64, 1.0)
        v.memset(ZCOL, 0.0)
        v.memset(QSTr[:, 0:8], 1.0)
        v.memset(QSTi[:, 0:8], 0.0)
        v.memset(CLOr[:, 0:8], 0.0)
        v.memset(CLOi[:, 0:8], 0.0)
        v.drain()
        for g in range(8):
            v.tensor_scalar(out=MSPT[:, g * 64:(g + 1) * 64], in0=ONES64,
                            scalar1=msp[:, g:g + 1], scalar2=None, op0=OP.mult)
            v.tensor_scalar(out=AIMT[:, g * 64:(g + 1) * 64], in0=ONES64,
                            scalar1=aim[:, g:g + 1], scalar2=None, op0=OP.mult)
        ctx.fence(v, "vector", s_pre2)
    ctx.barrier()

    w_in_r = dram["w_in"].rearrange("(a p) c -> p a c", p=128)

    @blk.gpsimd
    def _(g):
        for stt in range(4):
            for cc in range(16):
                k = stt * 16 + cc
                if k >= 2:
                    ctx.sems["D_xh"].wge(g, k - 1)
                ctx.dma(g, "gpsimd", WINB[k % 2].rearrange("p (a b) -> p a b", a=8),
                        w_in_r[:, :, cc * 128:(cc + 1) * 128])
                g.wait_ge(ctx.dsem["gpsimd"].h, ctx.dsem["gpsimd"].v)
                ctx.fence(g, "gpsimd", s_dw)
            for k in range(8):
                gk = stt * 8 + k
                ctx.sems["D_v3"].wge(g, gk + 1)
                sb = SPB[gk % 2]
                for q in range(4):
                    s_sp.inc(g.dma_start(
                        out=dram["spill_hbm"][(q * 32 + gk) * 128:(q * 32 + gk + 1) * 128, :],
                        in_=sb[:, q * 512:(q + 1) * 512]), 16)
                g.drain()
                g.tensor_tensor(out=T8[:, 16:24], in0=T8[:, 48:56], in1=ECOS, op=OP.mult)
                g.drain()
                g.tensor_tensor(out=T8[:, 24:32], in0=T8[:, 48:56], in1=ESIN, op=OP.mult)
                g.drain()
                g.tensor_tensor(out=T8[:, 0:8], in0=ECP, in1=EZR, op=OP.mult)
                g.drain()
                g.tensor_tensor(out=T8[:, 8:16], in0=ESP, in1=EZI, op=OP.mult)
                g.drain()
                g.tensor_tensor(out=T8[:, 0:8], in0=T8[:, 0:8], in1=T8[:, 8:16], op=OP.subtract)
                g.drain()
                g.tensor_tensor(out=T8[:, 8:16], in0=ESP, in1=EZR, op=OP.mult)
                g.drain()
                g.tensor_tensor(out=T8[:, 32:40], in0=ECP, in1=EZI, op=OP.mult)
                g.drain()
                g.tensor_tensor(out=T8[:, 8:16], in0=T8[:, 8:16], in1=T8[:, 32:40], op=OP.add)
                g.drain()
                cr, ci = CLOr[:, gk * 8:gk * 8 + 8], CLOi[:, gk * 8:gk * 8 + 8]
                nr, ni = CLOr[:, gk * 8 + 8:gk * 8 + 16], CLOi[:, gk * 8 + 8:gk * 8 + 16]
                g.tensor_tensor(out=T8[:, 32:40], in0=T8[:, 16:24], in1=cr, op=OP.mult)
                g.drain()
                g.tensor_tensor(out=T8[:, 40:48], in0=T8[:, 24:32], in1=ci, op=OP.mult)
                g.drain()
                g.tensor_tensor(out=nr, in0=T8[:, 32:40], in1=T8[:, 40:48], op=OP.subtract)
                g.drain()
                g.tensor_tensor(out=nr, in0=nr, in1=T8[:, 0:8], op=OP.add)
                g.drain()
                g.tensor_tensor(out=T8[:, 32:40], in0=T8[:, 16:24], in1=ci, op=OP.mult)
                g.drain()
                g.tensor_tensor(out=T8[:, 40:48], in0=T8[:, 24:32], in1=cr, op=OP.mult)
                g.drain()
                g.tensor_tensor(out=ni, in0=T8[:, 32:40], in1=T8[:, 40:48], op=OP.add)
                g.drain()
                g.tensor_tensor(out=ni, in0=ni, in1=T8[:, 8:16], op=OP.add)
                g.drain()
                qr, qi = QSTr[:, gk * 8:gk * 8 + 8], QSTi[:, gk * 8:gk * 8 + 8]
                nqr, nqi = QSTr[:, gk * 8 + 8:gk * 8 + 16], QSTi[:, gk * 8 + 8:gk * 8 + 16]
                g.tensor_tensor(out=T8[:, 32:40], in0=qr, in1=T8[:, 16:24], op=OP.mult)
                g.drain()
                g.tensor_tensor(out=T8[:, 40:48], in0=qi, in1=T8[:, 24:32], op=OP.mult)
                g.drain()
                g.tensor_tensor(out=nqr, in0=T8[:, 32:40], in1=T8[:, 40:48], op=OP.subtract)
                g.drain()
                g.tensor_tensor(out=T8[:, 32:40], in0=qr, in1=T8[:, 24:32], op=OP.mult)
                g.drain()
                g.tensor_tensor(out=T8[:, 40:48], in0=qi, in1=T8[:, 16:24], op=OP.mult)
                g.drain()
                s_car.inc(g.tensor_tensor(out=nqi, in0=T8[:, 32:40], in1=T8[:, 40:48], op=OP.add))
                g.drain()
                ctx.sems["D_sp"].wge(g, 64 * (gk + 1))
        ctx.sems["D_sum"].wge(g, 1)
        ctx.dma(g, "gpsimd", dram["cc_in2"].rearrange("a (p f) -> (a p) f", p=128), SUMT)
        g.wait_ge(ctx.dsem["gpsimd"].h, ctx.dsem["gpsimd"].v)
        s_cc.inc(g.collective_compute("AllGather", OP.bypass, replica_groups=[[0, 1, 2, 3, 4, 5, 6, 7]],
                                      ins=[dram["cc_in2"][:]], outs=[dram["cc_out2"][:]]))
        ctx.sems["D_cc"].wge(g, 1)
        for r in range(8):
            ctx.dma(g, "gpsimd", TRT[r], dram["cc_out2"][r:r + 1, :].rearrange("a (p f) -> (a p) f", p=128))
        g.wait_ge(ctx.dsem["gpsimd"].h, ctx.dsem["gpsimd"].v)
        ctx.fence(g, "gpsimd", s_cc)

    @blk.tensor
    def _(te):
        for stt in range(4):
            for cc in range(16):
                k = stt * 16 + cc
                ctx.sems["D_dw"].wge(te, k + 1)
                if k >= 2:
                    ctx.sems["D_xhe"].wge(te, k - 1)
                bank = PS["A"][:, 0:512] if k % 2 == 0 else PS["A"][:, 512:1024]
                for dc in range(8):
                    last = nc.tensor.matmul(bank, WINB[k % 2][:, dc * 128:dc * 128 + 128],
                                            XN3[:, dc * NTOK + stt * 512:dc * NTOK + stt * 512 + 512],
                                            start=(dc == 0), stop=(dc == 7))
                s_xh.inc(last)
            ctx.sems["D_xhe"].wge(te, stt * 16 + 16)
            for j in range(8):
                kj = stt * 8 + j
                if kj >= 2:
                    ctx.sems["D_ge"].wge(te, 2 * kj - 2)
                bx = PS["C"][:, 0:512] if j % 2 == 0 else PS["C"][:, 512:1024]
                ba = PS["D"][:, 0:512] if j % 2 == 0 else PS["D"][:, 512:1024]
                for bc in range(2):
                    nc.tensor.matmul(bx, WGX[:, (j * 2 + bc) * 128:(j * 2 + bc) * 128 + 128],
                                     XH[:, (j * 2 + bc) * 512:(j * 2 + bc) * 512 + 512],
                                     start=(bc == 0), stop=(bc == 1))
                for bc in range(2):
                    last = nc.tensor.matmul(ba, WGA[:, (j * 2 + bc) * 128:(j * 2 + bc) * 128 + 128],
                                            XH[:, (j * 2 + bc) * 512:(j * 2 + bc) * 512 + 512],
                                            start=(bc == 0), stop=(bc == 1))
                s_gmm.inc(last)
            if stt < 3:
                ctx.sems["D_v3"].wge(te, 8 * (stt + 1))

    @blk.scalar
    def _(s):
        for stt in range(4):
            for cc in range(16):
                k = stt * 16 + cc
                ctx.sems["D_xh"].wge(s, k + 1)
                bank = PS["A"][:, 0:512] if k % 2 == 0 else PS["A"][:, 512:1024]
                s.drain()
                s_xhe.inc(s.activation(out=XH[:, cc * 512:(cc + 1) * 512], in_=bank,
                                       func=AF.Identity, bias=b_in[:, cc:cc + 1], scale=1.0))
            act_preload(s, "sigmoid_and_others")
            for j in range(8):
                kj = stt * 8 + j
                ctx.sems["D_gmm"].wge(s, kj + 1)
                bx = PS["C"][:, 0:512] if j % 2 == 0 else PS["C"][:, 512:1024]
                ba = PS["D"][:, 0:512] if j % 2 == 0 else PS["D"][:, 512:1024]
                s.drain()
                s_ge.inc(s.activation(out=GX[:, j * 512:(j + 1) * 512], in_=bx,
                                      func=AF.Sigmoid, bias=bgx[:, j:j + 1], scale=1.0))
                s.drain()
                s_ge.inc(s.activation(out=GA[:, j * 512:(j + 1) * 512], in_=ba,
                                      func=AF.Sigmoid, bias=bga[:, j:j + 1], scale=1.0))
            for k in range(8):
                gk = stt * 8 + k
                ctx.sems["D_v1"].wge(s, gk + 1)
                if gk >= 1:
                    ctx.sems["D_car"].wge(s, gk)
                s.drain()
                act_preload(s, "silu_and_others")
                s.activation(out=PL[7], in_=PL[4], func=AF.Sin)
                s.drain()
                s.activation(out=PL[8], in_=PL[5], func=AF.Sin, scale=-1.0, bias=1.5707963267948966)
                s.drain()
                act_preload(s, "natural_log_exp_and_others")
                s.activation(out=PL[6], in_=PL[0], func=AF.Exp, scale=2.0)
                s.drain()
                s.activation(out=PL[9], in_=PL[1], func=AF.Exp, scale=-1.0, bias=-EXPB)
                s.drain()
                s.activation(out=PL[10], in_=PL[1], func=AF.Exp, scale=1.0, bias=EXPB)
                s.drain()
                s.activation(out=T8[:, 48:56].rearrange("p (a b) -> p a b", b=1),
                             in_=PL[1].rearrange("p (a b) -> p a b", b=64)[:, :, 63:64],
                             func=AF.Exp)
                ctx.fence(s, "scalar", s_s1)
                ctx.sems["D_v2"].wge(s, gk + 1)
                s.drain()
                s.activation(out=PL[6], in_=PL[0], func=AF.Ln)
                s.drain()
                s.activation(out=PL[6], in_=PL[6], func=AF.Exp, scale=0.5)
                ctx.fence(s, "scalar", s_s2)

    def GAc(g, k):
        return GA[:, g * 512 + k * 64:g * 512 + k * 64 + 64]

    def GXc(g, k):
        return GX[:, g * 512 + k * 64:g * 512 + k * 64 + 64]

    def XHc(cc, k):
        return XH[:, cc * 512 + k * 64:cc * 512 + k * 64 + 64]

    def ends(p):
        return p.rearrange("p (a b) -> p a b", b=64)[:, :, 63:64]

    def t8(a, b):
        return T8[:, a:b].rearrange("p (a b) -> p a b", b=1)

    @blk.vector
    def _(v):
        for stt in range(4):
            for k in range(8):
                gk = stt * 8 + k
                ctx.sems["D_ge"].wge(v, stt * 16 + 16)
                if gk >= 2:
                    ctx.sems["D_sp"].wge(v, 64 * (gk - 1))
                v.drain()
                GA3 = GA.rearrange("p (g t) -> p g t", t=512)[:, :, k * 64:(k + 1) * 64]
                v.tensor_tensor(out=PL[0].rearrange("p (g t) -> p g t", t=64), in0=GA3,
                                in1=MSPT.rearrange("p (g t) -> p g t", t=64), op=OP.mult)
                v.tensor_tensor(out=PL[2].rearrange("p (g t) -> p g t", t=64), in0=GA3,
                                in1=AIMT.rearrange("p (g t) -> p g t", t=64), op=OP.mult)
                v.drain()
                for g in range(8):
                    v.tensor_tensor_scan(PL[1][:, g * 64:(g + 1) * 64], ONES64,
                                         PL[0][:, g * 64:(g + 1) * 64], ZCOL, OP.mult, OP.add)
                    v.tensor_tensor_scan(PL[3][:, g * 64:(g + 1) * 64], ONES64,
                                         PL[2][:, g * 64:(g + 1) * 64], ZCOL, OP.mult, OP.add)
                v.drain()
                v.tensor_scalar(out=PL[4], in0=PL[3], scalar1=0.15915494309189535, scalar2=None, op0=OP.mult)
                v.drain()
                v.tensor_scalar(out=PL[5], in0=PL[4], scalar1=0.5, scalar2=None, op0=OP.is_ge)
                v.drain()
                v.scalar_tensor_tensor(out=PL[5], in0=PL[4], scalar=1.5, in1=PL[5], op0=OP.is_ge, op1=OP.add)
                v.drain()
                v.scalar_tensor_tensor(out=PL[5], in0=PL[4], scalar=2.5, in1=PL[5], op0=OP.is_ge, op1=OP.add)
                v.drain()
                v.scalar_tensor_tensor(out=PL[5], in0=PL[4], scalar=3.5, in1=PL[5], op0=OP.is_ge, op1=OP.add)
                v.drain()
                v.scalar_tensor_tensor(out=PL[4], in0=PL[5], scalar=-6.283185307179586, in1=PL[3],
                                       op0=OP.mult, op1=OP.add)
                v.drain()
                v.tensor_scalar(out=PL[4], in0=PL[4], scalar1=3.1415925, scalar2=-3.1415925,
                                op0=OP.min, op1=OP.max)
                v.drain()
                v.tensor_scalar(out=PL[5], in0=PL[4], scalar1=-1.0, scalar2=None, op0=OP.mult)
                v.drain()
                v.tensor_tensor(out=PL[5], in0=PL[5], in1=PL[4], op=OP.max)
                ctx.fence(v, "vector", s_v1)
                ctx.sems["D_s1"].wge(v, gk + 1)
                v.drain()
                v.tensor_scalar(out=PL[0], in0=PL[6], scalar1=-1.0, scalar2=1.0, op0=OP.mult, op1=OP.add)
                v.drain()
                v.tensor_scalar(out=PL[0], in0=PL[0], scalar1=1e-6, scalar2=None, op0=OP.max)
                ctx.fence(v, "vector", s_v2)
                ctx.sems["D_s2"].wge(v, gk + 1)
                v.drain()
                GX3 = GX.rearrange("p (g t) -> p g t", t=512)[:, :, k * 64:(k + 1) * 64]
                XH3 = XH.rearrange("p (c t) -> p c t", t=512)[:, :, k * 64:(k + 1) * 64]
                v.tensor_tensor(out=PL[0].rearrange("p (g t) -> p g t", t=64), in0=GX3,
                                in1=PL[6].rearrange("p (g t) -> p g t", t=64), op=OP.mult)
                v.drain()
                v.tensor_tensor(out=PL[4].rearrange("p (g t) -> p g t", t=64),
                                in0=PL[0].rearrange("p (g t) -> p g t", t=64),
                                in1=XH3[:, 0:8, :], op=OP.mult)
                v.tensor_tensor(out=PL[5].rearrange("p (g t) -> p g t", t=64),
                                in0=PL[0].rearrange("p (g t) -> p g t", t=64),
                                in1=XH3[:, 8:16, :], op=OP.mult)
                v.drain()
                v.tensor_tensor(out=PL[11], in0=PL[9], in1=PL[8], op=OP.mult)
                v.drain()
                v.tensor_tensor(out=PL[6], in0=PL[9], in1=PL[7], op=OP.mult)
                v.drain()
                v.tensor_tensor(out=PL[0], in0=PL[11], in1=PL[4], op=OP.mult)
                v.drain()
                v.tensor_tensor(out=PL[9], in0=PL[6], in1=PL[5], op=OP.mult)
                v.drain()
                v.tensor_tensor(out=PL[0], in0=PL[0], in1=PL[9], op=OP.add)
                v.drain()
                v.tensor_tensor(out=PL[9], in0=PL[11], in1=PL[5], op=OP.mult)
                v.drain()
                v.tensor_tensor(out=PL[11], in0=PL[6], in1=PL[4], op=OP.mult)
                v.drain()
                v.tensor_tensor(out=PL[9], in0=PL[9], in1=PL[11], op=OP.subtract)
                v.drain()
                v.tensor_tensor(out=PL[4], in0=PL[10], in1=PL[8], op=OP.mult)
                v.drain()
                v.tensor_tensor(out=PL[5], in0=PL[10], in1=PL[7], op=OP.mult)
                v.drain()
                if gk >= 1:
                    ctx.sems["D_car"].wge(v, gk)
                v.tensor_copy(ECOS.rearrange("p (a b) -> p a b", b=1), ends(PL[8]))
                v.tensor_copy(ESIN.rearrange("p (a b) -> p a b", b=1), ends(PL[7]))
                v.drain()
                for g in range(8):
                    v.tensor_tensor_scan(PL[6][:, g * 64:(g + 1) * 64], ONES64,
                                         PL[0][:, g * 64:(g + 1) * 64], ZCOL, OP.mult, OP.add)
                    v.tensor_tensor_scan(PL[7][:, g * 64:(g + 1) * 64], ONES64,
                                         PL[9][:, g * 64:(g + 1) * 64], ZCOL, OP.mult, OP.add)
                v.drain()
                v.tensor_copy(ECP.rearrange("p (a b) -> p a b", b=1), ends(PL[4]))
                v.tensor_copy(ESP.rearrange("p (a b) -> p a b", b=1), ends(PL[5]))
                v.tensor_copy(EZR.rearrange("p (a b) -> p a b", b=1), ends(PL[6]))
                v.tensor_copy(EZI.rearrange("p (a b) -> p a b", b=1), ends(PL[7]))
                v.drain()
                sb = SPB[gk % 2]
                v.tensor_copy(sb[:, 0:512], PL[4])
                v.drain()
                v.tensor_copy(sb[:, 512:1024], PL[5])
                v.drain()
                v.tensor_copy(sb[:, 1024:1536], PL[6])
                v.drain()
                v.tensor_copy(sb[:, 1536:2048], PL[7])
                ctx.fence(v, "vector", s_v3)
        v.drain()
        ctx.sems["D_car"].wge(v, 32)
        v.tensor_copy(SUMT[:, 0:8], QSTr[:, 256:264])
        v.drain()
        v.tensor_copy(SUMT[:, 8:16], QSTi[:, 256:264])
        v.drain()
        v.tensor_copy(SUMT[:, 16:24], CLOr[:, 256:264])
        v.drain()
        v.tensor_copy(SUMT[:, 24:32], CLOi[:, 256:264])
        ctx.fence(v, "vector", s_sum)

    @blk.vector
    def _(v):
        ctx.sems["D_cc"].wge(v, 2)
        v.memset(CIN[:, 0:16], 0.0)
        for b in range(2):
            v.drain()
            v.memset(T8[:, 48:56], 0.0)
            v.memset(T8[:, 56:64], 0.0)
            for r in range(4):
                c = b * 4 + r
                if r > 0:
                    p = b * 4 + r - 1
                    v.drain()
                    v.tensor_tensor(out=T8[:, 0:8], in0=TRT[p][:, 0:8], in1=T8[:, 48:56], op=OP.mult)
                    v.drain()
                    v.tensor_tensor(out=T8[:, 8:16], in0=TRT[p][:, 8:16], in1=T8[:, 56:64], op=OP.mult)
                    v.drain()
                    v.tensor_tensor(out=T8[:, 16:24], in0=TRT[p][:, 0:8], in1=T8[:, 56:64], op=OP.mult)
                    v.drain()
                    v.tensor_tensor(out=T8[:, 24:32], in0=TRT[p][:, 8:16], in1=T8[:, 48:56], op=OP.mult)
                    v.drain()
                    v.tensor_tensor(out=T8[:, 48:56], in0=T8[:, 0:8], in1=T8[:, 8:16], op=OP.subtract)
                    v.drain()
                    v.tensor_tensor(out=T8[:, 56:64], in0=T8[:, 16:24], in1=T8[:, 24:32], op=OP.add)
                    v.drain()
                    v.tensor_tensor(out=T8[:, 48:56], in0=T8[:, 48:56], in1=TRT[p][:, 16:24], op=OP.add)
                    v.drain()
                    v.tensor_tensor(out=T8[:, 56:64], in0=T8[:, 56:64], in1=TRT[p][:, 24:32], op=OP.add)
                v.drain()
                v.tensor_scalar(out=T8[:, 0:8], in0=T8[:, 48:56], scalar1=smt[:, c:c + 1], scalar2=None, op0=OP.mult)
                v.drain()
                v.tensor_tensor(out=CIN[:, 0:8], in0=CIN[:, 0:8], in1=T8[:, 0:8], op=OP.add)
                v.drain()
                v.tensor_scalar(out=T8[:, 0:8], in0=T8[:, 56:64], scalar1=smt[:, c:c + 1], scalar2=None, op0=OP.mult)
                v.drain()
                v.tensor_tensor(out=CIN[:, 8:16], in0=CIN[:, 8:16], in1=T8[:, 0:8], op=OP.add)
        ctx.fence(v, "vector", s_cin)

    ctx.barrier()

    def bank8(oc):
        ps = PS[["A", "B", "C", "D"][oc // 2]]
        return ps[:, 0:512] if oc % 2 == 0 else ps[:, 512:1024]

    @blk.sync
    def _(e):
        for gk in range(32):
            if gk >= 2:
                ctx.sems["D_h"].wge(e, gk - 1)
            rb = RLB[gk % 2]
            for q in range(4):
                ctx.dma(e, "sync", rb[:, q * 512:(q + 1) * 512],
                        dram["spill_hbm"][(q * 32 + gk) * 128:(q * 32 + gk + 1) * 128, :])
            e.wait_ge(ctx.dsem["sync"].h, ctx.dsem["sync"].v)
            ctx.fence(e, "sync", s_rl)

    @blk.gpsimd
    def _(g):
        for stt in range(4):
            for hc in range(16):
                kk = stt * 16 + hc
                if kk >= 2:
                    ctx.sems["D_om"].wge(g, kk - 1)
                ctx.dma(g, "gpsimd", WINB[kk % 2], dram["w_out"][hc * 128:(hc + 1) * 128, :])
                g.wait_ge(ctx.dsem["gpsimd"].h, ctx.dsem["gpsimd"].v)
                ctx.fence(g, "gpsimd", s_wo)

    @blk.tensor
    def _(te):
        for stt in range(4):
            ctx.sems["D_h"].wge(te, 8 * (stt + 1))
            if stt >= 1:
                ctx.sems["D_oe"].wge(te, 8 * stt)
            for hc in range(16):
                kk = stt * 16 + hc
                ctx.sems["D_wo"].wge(te, kk + 1)
                for oc in range(8):
                    last = nc.tensor.matmul(bank8(oc), WINB[kk % 2][:, oc * 128:oc * 128 + 128],
                                            HMB[:, hc * 512:(hc + 1) * 512],
                                            start=(hc == 0), stop=(hc == 15))
                s_om.inc(last)

    @blk.vector
    def _(v):
        def evac(stt):
            for oc in range(8):
                ctx.sems["D_om"].wge(v, 16 * (stt + 1))
                v.drain()
                s_oe.inc(v.scalar_tensor_tensor(out=xf_ap(ctx, oc, HALO + stt * 512, 512),
                                                in0=bank8(oc), scalar=b_out[:, oc:oc + 1],
                                                in1=xf_ap(ctx, oc, HALO + stt * 512, 512),
                                                op0=OP.add, op1=OP.add))
                v.drain()
        for stt in range(4):
            if stt >= 1:
                evac(stt - 1)
            for k in range(8):
                gk = stt * 8 + k
                ctx.sems["D_rl"].wge(v, gk + 1)
                if k == 0 and stt >= 1:
                    ctx.sems["D_om"].wge(v, 16 * stt)
                rb = RLB[gk % 2]
                v.drain()
                v.tensor_tensor(out=T8[:, 0:8], in0=QSTr[:, gk * 8:gk * 8 + 8], in1=CIN[:, 0:8], op=OP.mult)
                v.drain()
                v.tensor_tensor(out=T8[:, 8:16], in0=QSTi[:, gk * 8:gk * 8 + 8], in1=CIN[:, 8:16], op=OP.mult)
                v.drain()
                v.tensor_tensor(out=T8[:, 0:8], in0=T8[:, 0:8], in1=T8[:, 8:16], op=OP.subtract)
                v.drain()
                v.tensor_tensor(out=T8[:, 0:8], in0=T8[:, 0:8], in1=CLOr[:, gk * 8:gk * 8 + 8], op=OP.add)
                v.drain()
                v.tensor_tensor(out=T8[:, 8:16], in0=QSTr[:, gk * 8:gk * 8 + 8], in1=CIN[:, 8:16], op=OP.mult)
                v.drain()
                v.tensor_tensor(out=T8[:, 16:24], in0=QSTi[:, gk * 8:gk * 8 + 8], in1=CIN[:, 0:8], op=OP.mult)
                v.drain()
                v.tensor_tensor(out=T8[:, 8:16], in0=T8[:, 8:16], in1=T8[:, 16:24], op=OP.add)
                v.drain()
                v.tensor_tensor(out=T8[:, 8:16], in0=T8[:, 8:16], in1=CLOi[:, gk * 8:gk * 8 + 8], op=OP.add)
                v.drain()
                v.tensor_scalar(out=T8[:, 0:16], in0=T8[:, 0:16], scalar1=EMB, scalar2=None, op0=OP.mult)
                v.drain()
                for g in range(8):
                    v.tensor_scalar(out=PL[0][:, g * 64:(g + 1) * 64],
                                    in0=rb[:, 1024 + g * 64:1024 + g * 64 + 64],
                                    scalar1=T8[:, g:g + 1], scalar2=None, op0=OP.add)
                    v.tensor_scalar(out=PL[1][:, g * 64:(g + 1) * 64],
                                    in0=rb[:, 1536 + g * 64:1536 + g * 64 + 64],
                                    scalar1=T8[:, 8 + g:9 + g], scalar2=None, op0=OP.add)
                v.drain()
                v.tensor_tensor(out=PL[2], in0=rb[:, 0:512], in1=PL[0], op=OP.mult)
                v.drain()
                v.tensor_tensor(out=PL[3], in0=rb[:, 512:1024], in1=PL[1], op=OP.mult)
                v.drain()
                v.tensor_tensor(out=PL[2], in0=PL[2], in1=PL[3], op=OP.subtract)
                v.drain()
                v.tensor_tensor(out=PL[3], in0=rb[:, 512:1024], in1=PL[0], op=OP.mult)
                v.drain()
                v.tensor_tensor(out=PL[4], in0=rb[:, 0:512], in1=PL[1], op=OP.mult)
                v.drain()
                v.tensor_tensor(out=PL[3], in0=PL[3], in1=PL[4], op=OP.add)
                v.drain()
                H3 = HMB.rearrange("p (a b) -> p a b", b=512)
                v.tensor_copy(H3[:, 0:8, k * 64:k * 64 + 64], PL[2].rearrange("p (a b) -> p a b", a=8))
                v.drain()
                s_h.inc(v.tensor_copy(H3[:, 8:16, k * 64:k * 64 + 64], PL[3].rearrange("p (a b) -> p a b", a=8)))
                v.drain()
        evac(3)

    ctx.tap("qall", QALL)
    ctx.tap("misc", MISC)
    ctx.tap("hmb", HMB)




NE = 17


def phaseE(ctx, dram, PS, st):
    nc, blk = ctx.nc, ctx.blk
    SM = st["SM"]
    onesb_row = st["onesb_row"]

    XN4 = ctx.at(R2_KB, 8 * NTOK, BF16)
    WEX = []
    for b in range(2):
        base = R2_KB + 32 + 24 * b
        WEX.append(dict(
            WG=ctx.at(base, 4096, BF16),
            WU=ctx.at(base + 8, 4096, BF16),
            WD=ctx.at(base + 16, 4096, BF16),
        ))
    GW = ctx.at(R2_KB + 80, 16 * 512, BF16)
    RTT = ctx.at(R2_KB + 96, 256)
    SCR = ctx.at(R2_KB + 97, 256)
    WTROW = ctx.at(R1_KB, 2048, BF16)[0:1, :]
    WTONES = ctx.at(R1_KB + 4, 2048, BF16)[0:1, :]
    GRP = ctx.at(R1_KB + 8, 512, BF16)
    WB16 = GRP[:, 0:16]
    GWR = GRP[:, 128:256]
    SILB = [ctx.at(R1_KB + 9 + b, 512, BF16) for b in range(2)]
    HUW = [ctx.at(R1_KB + 12 + 2 * b, 512) for b in range(2)]
    WBTF = ctx.at(R1_KB + 16, 2048, BF16)
    GBT = SM[:, 328:344]
    WMT = ctx.stack.enter_context(
        nc.sbuf_tensor("wmtE", [128, 128], BF16, is_transpose_dest=True)).ap()

    s_rg = ctx.sem("E_rg")
    s_rmm = ctx.sem("E_rmm")
    s_ra = ctx.sem("E_ra")
    s_rs = ctx.sem("E_rs")
    s_rw = ctx.sem("E_rw")
    s_rt = ctx.sem("E_rt")
    s_wts = ctx.sem("E_wts")
    s_x4a = ctx.sem("E_x4a")
    s_wl = ctx.sem("E_wl")
    s_wtr = ctx.sem("E_wtr")
    s_wb = ctx.sem("E_wb")
    s_wbe = ctx.sem("E_wbe")
    s_p1 = ctx.sem("E_p1")
    s_sil = ctx.sem("E_sil")
    s_huw = ctx.sem("E_huw")
    s_gw = ctx.sem("E_gw")
    s_p2 = ctx.sem("E_p2")
    s_ev = ctx.sem("E_ev")

    emit_ln(ctx, PS, st, "LN4", XN4)
    ctx.barrier()

    B4 = [PS["A"][:, 0:512], PS["A"][:, 512:1024], PS["B"][:, 0:512], PS["B"][:, 512:1024]]
    B4H = [PS["C"][:, 0:512], PS["C"][:, 512:1024], PS["D"][:, 0:512], PS["D"][:, 512:1024]]
    B8 = B4 + B4H

    def wsrc(e):
        if e < 16:
            return (dram["ex_wg"][e * 1024:(e + 1) * 1024, :],
                    dram["ex_wu"][e * 1024:(e + 1) * 1024, :],
                    dram["ex_wd"][e * 512:(e + 1) * 512, :])
        return (dram["sh_wg"], dram["sh_wu"], dram["sh_wd"])

    def ldw(g, e):
        sg, su, sd = wsrc(e)
        w = WEX[e % 2]
        ctx.dma(g, "gpsimd", w["WG"].rearrange("p (a b) -> p a b", a=8),
                sg.rearrange("(a p) c -> p a c", p=128))
        ctx.dma(g, "gpsimd", w["WU"].rearrange("p (a b) -> p a b", a=8),
                su.rearrange("(a p) c -> p a c", p=128))
        ctx.dma(g, "gpsimd", w["WD"].rearrange("p (a b) -> p a b", a=4),
                sd.rearrange("(a p) c -> p a c", p=128))

    @blk.gpsimd
    def _(g):
        ctx.dma(g, "gpsimd", GWR.rearrange("p (a b) -> p a b", a=8),
                dram["gate_w"].rearrange("(a p) c -> p a c", p=128))
        ctx.dma(g, "gpsimd", GBT, dram["gate_bt"])
        g.wait_ge(ctx.dsem["gpsimd"].h, ctx.dsem["gpsimd"].v)
        ctx.fence(g, "gpsimd", s_rg)
        ldw(g, 0)
        g.wait_ge(ctx.dsem["gpsimd"].h, ctx.dsem["gpsimd"].v)
        ctx.fence(g, "gpsimd", s_wl)
        for e in range(NE):
            if e + 1 < NE:
                ctx.sems["E_p2"].wge(g, 4 * e)
                ldw(g, e + 1)
            for ic in range(4):
                for t in range(4):
                    gi = 16 * e + 4 * ic + t
                    ctx.sems["E_sil"].wge(g, gi + 1)
                    ctx.sems["E_huw"].wge(g, gi + 1)
                    if e >= 1 and ic == 0 and t == 0:
                        ctx.sems["E_p2"].wge(g, 4 * e)
                    s_gw.inc(g.tensor_tensor(out=GW[:, ic * 2048 + t * 512:ic * 2048 + t * 512 + 512],
                                             in0=SILB[t % 2], in1=HUW[t % 2], op=OP.mult))
            g.wait_ge(ctx.dsem["gpsimd"].h, ctx.dsem["gpsimd"].v)
            ctx.fence(g, "gpsimd", s_wl)

    @blk.sync
    def _(e_):
        for e in range(16):
            if e == 0:
                ctx.sems["E_rt"].wge(e_, 1)
            ctx.sems["E_wb"].wge(e_, 4 * e)
            ctx.dma(e_, "sync", WTROW, dram["wmoe_hbm"][e:e + 1, :])
            e_.wait_ge(ctx.dsem["sync"].h, ctx.dsem["sync"].v)
            ctx.fence(e_, "sync", s_wtr)

    @blk.tensor
    def _(te):
        ctx.sems["E_rg"].wge(te, 1)
        for tile in range(16):
            if tile >= 2:
                ctx.sems["E_rs"].wge(te, tile - 1)
            bank = PS["A"][:, 0:16] if tile % 2 == 0 else PS["A"][:, 512:528]
            for dc in range(8):
                last = nc.tensor.matmul(bank, XN4[:, dc * NTOK + tile * 128:dc * NTOK + tile * 128 + 128],
                                        GWR[:, dc * 16:(dc + 1) * 16], start=(dc == 0), stop=(dc == 7))
            s_rmm.inc(last)
        for e in range(NE):
            ctx.sems["E_wl"].wge(te, e + 1)
            if e < 16:
                ctx.sems["E_wtr"].wge(te, e + 1)
            else:
                ctx.sems["E_x4a"].wge(te, 1)
            if e >= 1:
                ctx.sems["E_ev"].wge(te, 4 * e)
            row = WTONES if e == 16 else WTROW
            for t in range(4):
                s_wb.inc(nc.tensor.matmul(B4[t], onesb_row,
                                          row[0:1, t * 512:(t + 1) * 512], start=True, stop=True))
            for ic in range(4):
                if ic == 0:
                    ctx.sems["E_wbe"].wge(te, 4 * e + 4)
                else:
                    ctx.sems["E_sil"].wge(te, 16 * e + 4 * ic)
                    ctx.sems["E_huw"].wge(te, 16 * e + 4 * ic)
                for dc in range(8):
                    for t in range(4):
                        last = nc.tensor.matmul(B4[t], WEX[e % 2]["WG"][:, dc * 512 + ic * 128:dc * 512 + ic * 128 + 128],
                                                XN4[:, dc * NTOK + t * 512:dc * NTOK + t * 512 + 512],
                                                start=(dc == 0), stop=(dc == 7))
                s_p1.inc(last)
                for dc in range(8):
                    for t in range(4):
                        last = nc.tensor.matmul(B4H[t], WEX[e % 2]["WU"][:, dc * 512 + ic * 128:dc * 512 + ic * 128 + 128],
                                                XN4[:, dc * NTOK + t * 512:dc * NTOK + t * 512 + 512],
                                                start=(dc == 0), stop=(dc == 7))
                s_p1.inc(last)
            ctx.sems["E_gw"].wge(te, 16 * (e + 1))
            for p4 in range(4):
                if p4 >= 1:
                    ctx.sems["E_ev"].wge(te, 4 * e + p4)
                for ic in range(4):
                    for o1 in range(2):
                        oc = 2 * p4 + o1
                        for t in range(4):
                            last = nc.tensor.matmul(B8[o1 * 4 + t],
                                                    WEX[e % 2]["WD"][:, ic * 1024 + oc * 128:ic * 1024 + oc * 128 + 128],
                                                    GW[:, ic * 2048 + t * 512:ic * 2048 + t * 512 + 512],
                                                    start=(ic == 0), stop=(ic == 3))
                s_p2.inc(last)

    @blk.scalar
    def _(s):
        act_preload(s, "sigmoid_and_others")
        for tile in range(16):
            ctx.sems["E_ra"].wge(s, tile + 1)
            s.drain()
            s.activation(out=RTT[:, tile * 16:(tile + 1) * 16], in_=SCR[:, 0:16], func=AF.Sigmoid)
            ctx.fence(s, "scalar", s_rs)
            ctx.sems["E_rw"].wge(s, tile + 1)
            ctx.dsem["scalar"].inc(s.dma_start(out=WMT[:, :], in_=GRP[:, 0:128], transpose=True), 16)
            s.wait_ge(ctx.dsem["scalar"].h, ctx.dsem["scalar"].v)
            ctx.dma(s, "scalar", dram["wmoe_hbm"][0:16, tile * 128:(tile + 1) * 128], WMT[0:16, 0:128])
            s.wait_ge(ctx.dsem["scalar"].h, ctx.dsem["scalar"].v)
            ctx.fence(s, "scalar", s_wts)
        ctx.fence(s, "scalar", s_rt)
        act_preload(s, "silu_and_others")
        for e in range(NE):
            for ic in range(4):
                ctx.sems["E_p1"].wge(s, 8 * e + 2 * ic + 1)
                for t in range(4):
                    gi = 16 * e + 4 * ic + t
                    if gi >= 2:
                        ctx.sems["E_gw"].wge(s, gi - 1)
                    s.drain()
                    s_sil.inc(s.activation(out=SILB[t % 2], in_=B4[t], func=AF.Silu))

    @blk.vector
    def _(v):
        def top2(t0):
            R = RTT[:, t0 * 16:(t0 + 1) * 16]
            m1, c1, f1, m2 = SCR[:, 16:17], SCR[:, 17:18], SCR[:, 18:19], SCR[:, 19:20]
            den, dv = SCR[:, 20:21], SCR[:, 21:22]
            msk1, inv, R2v = SCR[:, 32:48], SCR[:, 48:64], SCR[:, 64:80]
            mskB, SEL, WS, WF = SCR[:, 80:96], SCR[:, 96:112], SCR[:, 112:128], SCR[:, 128:144]
            ctx.sems["E_rs"].wge(v, t0 + 1)
            v.drain()
            v.tensor_reduce(m1, R, axis=AX.X, op=OP.max)
            v.drain()
            v.tensor_scalar(out=msk1, in0=R, scalar1=m1, scalar2=None, op0=OP.is_ge)
            v.drain()
            v.tensor_reduce(c1, msk1, axis=AX.X, op=OP.add)
            v.drain()
            v.tensor_scalar(out=f1, in0=c1, scalar1=1.5, scalar2=None, op0=OP.is_le)
            v.tensor_scalar(out=inv, in0=msk1, scalar1=-1.0, scalar2=1.0, op0=OP.mult, op1=OP.add)
            v.drain()
            v.tensor_tensor(out=R2v, in0=R, in1=inv, op=OP.mult)
            v.drain()
            v.tensor_reduce(m2, R2v, axis=AX.X, op=OP.max)
            v.drain()
            v.tensor_scalar(out=mskB, in0=R2v, scalar1=m2, scalar2=None, op0=OP.is_ge)
            v.drain()
            v.tensor_scalar(out=mskB, in0=mskB, scalar1=f1, scalar2=None, op0=OP.mult)
            v.drain()
            v.tensor_tensor(out=SEL, in0=msk1, in1=mskB, op=OP.max)
            v.drain()
            v.tensor_tensor(out=WS, in0=R, in1=SEL, op=OP.mult)
            v.drain()
            v.tensor_reduce(den, WS, axis=AX.X, op=OP.add)
            v.drain()
            v.reciprocal(dv, den)
            v.drain()
            v.tensor_scalar(out=WF, in0=WS, scalar1=dv, scalar2=None, op0=OP.mult)
            v.drain()
            if t0 >= 1:
                ctx.sems["E_wts"].wge(v, t0)
            v.tensor_copy(WB16, WF)
            ctx.fence(v, "vector", s_rw)
        for tile in range(17):
            if tile < 16:
                ctx.sems["E_rmm"].wge(v, tile + 1)
                if tile >= 1:
                    ctx.sems["E_rs"].wge(v, tile)
                bank = PS["A"][:, 0:16] if tile % 2 == 0 else PS["A"][:, 512:528]
                v.drain()
                v.tensor_tensor(out=SCR[:, 0:16], in0=bank, in1=GBT, op=OP.add)
                ctx.fence(v, "vector", s_ra)
            if tile >= 1:
                top2(tile - 1)
        v.memset(WTONES, 1.0)
        v.drain()
        for dc in range(8):
            v.tensor_tensor(out=xf_ap(ctx, dc, HALO, NTOK), in0=xf_ap(ctx, dc, HALO, NTOK),
                            in1=XN4[:, dc * NTOK:(dc + 1) * NTOK], op=OP.add)
        ctx.fence(v, "vector", s_x4a)
        for e in range(NE):
            for t in range(4):
                ctx.sems["E_wb"].wge(v, 4 * e + t + 1)
                s_wbe.inc(v.tensor_copy(WBTF[:, t * 512:(t + 1) * 512], B4[t]))
            v.drain()
            for ic in range(4):
                ctx.sems["E_p1"].wge(v, 8 * e + 2 * ic + 2)
                for t in range(4):
                    gi = 16 * e + 4 * ic + t
                    if gi >= 2:
                        ctx.sems["E_gw"].wge(v, gi - 1)
                    s_huw.inc(v.tensor_tensor(out=HUW[t % 2], in0=B4H[t],
                                              in1=WBTF[:, t * 512:(t + 1) * 512], op=OP.mult))
            for p4 in range(4):
                ctx.sems["E_p2"].wge(v, 4 * e + p4 + 1)
                for o1 in range(2):
                    oc = 2 * p4 + o1
                    for t in range(4):
                        last = v.tensor_tensor(out=xf_ap(ctx, oc, HALO + t * 512, 512),
                                               in0=B8[o1 * 4 + t],
                                               in1=xf_ap(ctx, oc, HALO + t * 512, 512), op=OP.add)
                s_ev.inc(last)


def emit(ctx, dram, PS):
    st = {}
    def run(fn):
        ctx.scope_begin()
        fn(ctx, dram, PS, st)
        ctx.barrier()
        ctx.scope_end()
    run(phaseA)
    if PHASES >= 2 and PHASES != 30:
        run(phaseB1)
    if PHASES >= 3 and PHASES != 30:
        run(phaseB2)
    if PHASES >= 4 or PHASES in (30, 35):
        run(phaseC)
    if PHASES >= 5:
        run(phaseD)
    if PHASES >= 6:
        run(phaseE)
    ctx.flush_taps()
    emit_store(ctx, dram)


def emit_store(ctx, dram):
    blk = ctx.blk
    y = dram["y_out"]
    @blk.gpsimd
    def _(g):
        for dc in range(8):
            ctx.dma(g, "gpsimd", y[dc * 128:(dc + 1) * 128, :], xf_ap(ctx, dc, HALO, NTOK))
        ctx.dsem["gpsimd"].wait(g)


def phaseA(ctx, dram, PS, st):
    nc, blk = ctx.nc, ctx.blk
    SM = ctx.at(SM_KB, 1024)
    ones_col = SM[:, 0:1]
    onesb_row = ctx.at(SM_KB + 2, 256, BF16)[0:1, 0:128]
    onesb_col = ctx.at(SM_KB + 2, 256, BF16)[:, 140:141]
    st["SM"] = SM
    st["ones_col"] = ones_col
    st["onesb_col"] = onesb_col
    st["onesb_row"] = onesb_row
    XN1 = ctx.at(R1_KB, 8 * NROW, BF16)
    st["XN1"] = XN1
    XB = ctx.at(R2_KB, 8 * 512, BF16)
    SQ = ctx.at(R2_KB + 8, 8 * 512, BF16)
    SROW_ = ctx.at(R2_KB + 16, 6 * 2304)[0:1, :]
    SROW = [SROW_[:, i * 2304:(i + 1) * 2304] for i in range(6)]
    TMP = ctx.at(R2_KB + 70, 512)
    ABH_ = ctx.at(R2_KB + 72, 4 * 2304, BF16)[0:1, :]
    ABH = [ABH_[:, i * 2304:(i + 1) * 2304] for i in range(4)]

    s_ld = ctx.sem("A_ld")
    s_pre = ctx.sem("A_pre")
    s_mm = ctx.sem("A_mm")
    s_ev = ctx.sem("A_ev")
    s_ab = ctx.sem("A_ab")
    s_bc = ctx.sem("A_bc")
    s_ap = ctx.sem("A_ap")

    @blk.sync
    def _(e):
        for dc in range(8):
            ctx.dma(e, "sync", xf_ap(ctx, dc, 0, NROW), dram["x_fm"][dc * 128:(dc + 1) * 128, :])

    def bank_of(tt):
        return PS["A"] if tt % 2 == 0 else PS["B"]

    @blk.vector
    def _(v):
        v.memset(ones_col, 1.0)
        v.memset(onesb_row, 1.0)
        v.memset(onesb_col, 1.0)
        v.memset(SM[:, 1:2], 1e-8)
        for c, val in enumerate(CONSTS):
            v.memset(SM[:, 250 + c:251 + c], val)
        ctx.dsem["sync"].wait(v)
        for tt in range(NT5 + 1):
            if tt >= 1:
                ctx.sems["A_mm"].wge(v, tt)
                o1, w1 = (tt - 1) * 512, tile_w(tt - 1)
                bank = bank_of(tt - 1)
                v.tensor_copy(SROW[0][:, o1:o1 + w1], bank[0:1, 0:w1])
                s_ev.inc(v.tensor_copy(SROW[1][:, o1:o1 + w1], bank[0:1, 512:512 + w1]))
            if tt < NT5:
                o, w = tt * 512, tile_w(tt)
                for dc in range(8):
                    v.tensor_copy(XB[:, dc * 512:dc * 512 + w], xf_ap(ctx, dc, o, w))
                ctx.fence(v, "vector", s_pre)

    @blk.scalar
    def _(s):
        act_preload(s, "sqrt_and_others")
        ctx.dsem["sync"].wait(s)
        for tt in range(NT5):
            o, w = tt * 512, tile_w(tt)
            if tt >= 1:
                ctx.sems["A_mm"].wge(s, tt)
            for dc in range(8):
                s.activation(out=SQ[:, dc * 512:dc * 512 + w], in_=xf_ap(ctx, dc, o, w), func=AF.Square)
            ctx.fence(s, "scalar", s_pre)

    @blk.tensor
    def _(te):
        for tt in range(NT5):
            o, w = tt * 512, tile_w(tt)
            ctx.sems["A_pre"].wge(te, 2 * (tt + 1))
            if tt >= 2:
                ctx.sems["A_ev"].wge(te, tt - 1)
            bank = bank_of(tt)
            for dc in range(8):
                nc.tensor.matmul(bank[0:1, 0:w], onesb_col, XB[:, dc * 512:dc * 512 + w],
                                 start=(dc == 0), stop=(dc == 7))
            for dc in range(8):
                last = nc.tensor.matmul(bank[0:1, 512:512 + w], onesb_col, SQ[:, dc * 512:dc * 512 + w],
                                        start=(dc == 0), stop=(dc == 7))
            s_mm.inc(last)

    @blk.vector
    def _(v):
        ctx.sems["A_ev"].wge(v, NT5)
        v.drain()
        v.tensor_scalar(out=SROW[0][:, :NROW], in0=SROW[0][:, :NROW], scalar1=1.0 / D, scalar2=None, op0=OP.mult)
        v.tensor_scalar(out=SROW[1][:, :NROW], in0=SROW[1][:, :NROW], scalar1=1.0 / D, scalar2=None, op0=OP.mult)
        v.drain()
        v.tensor_tensor(out=SROW[2][:, :NROW], in0=SROW[0][:, :NROW], in1=SROW[0][:, :NROW], op=OP.mult)
        v.drain()
        v.tensor_tensor(out=SROW[1][:, :NROW], in0=SROW[1][:, :NROW], in1=SROW[2][:, :NROW], op=OP.subtract)
        v.drain()
        v.tensor_scalar(out=SROW[1][:, :NROW], in0=SROW[1][:, :NROW], scalar1=1e-5, scalar2=None, op0=OP.add)
        ctx.fence(v, "vector", s_ab)

    @blk.scalar
    def _(s):
        s_ab.wait(s)
        s.activation(out=SROW[2][:, :NROW], in_=SROW[1][:, :NROW], func=AF.Sqrt)
        ctx.fence(s, "scalar", s_ab)

    @blk.vector
    def _(v):
        ctx.sems["A_ab"].wge(v, 2)
        v.reciprocal(SROW[3][:, :NROW], SROW[2][:, :NROW])
        v.drain()
        v.tensor_tensor(out=SROW[4][:, :NROW], in0=SROW[0][:, :NROW], in1=SROW[3][:, :NROW], op=OP.mult)
        v.drain()
        v.tensor_scalar(out=SROW[4][:, :NROW], in0=SROW[4][:, :NROW], scalar1=-1.0, scalar2=None, op0=OP.mult)
        v.tensor_copy(ABH[0][:, :NROW], SROW[3][:, :NROW])
        v.drain()
        v.tensor_copy(ABH[2][:, :NROW], SROW[4][:, :NROW])
        v.drain()
        v.tensor_copy(SROW[5][:, :NROW], ABH[0][:, :NROW])
        v.drain()
        v.tensor_tensor(out=SROW[5][:, :NROW], in0=SROW[3][:, :NROW], in1=SROW[5][:, :NROW], op=OP.subtract)
        v.drain()
        v.tensor_copy(ABH[1][:, :NROW], SROW[5][:, :NROW])
        v.drain()
        v.tensor_copy(SROW[5][:, :NROW], ABH[2][:, :NROW])
        v.drain()
        v.tensor_tensor(out=SROW[5][:, :NROW], in0=SROW[4][:, :NROW], in1=SROW[5][:, :NROW], op=OP.subtract)
        v.drain()
        v.tensor_copy(ABH[3][:, :NROW], SROW[5][:, :NROW])
        ctx.fence(v, "vector", s_ab)

    @blk.tensor
    def _(te):
        ctx.sems["A_ab"].wge(te, 3)
        for tt in range(NT5):
            o, w = tt * 512, tile_w(tt)
            pa = PS["C"][:, 0:w] if tt % 2 == 0 else PS["C"][:, 512:512 + w]
            pb = PS["D"][:, 0:w] if tt % 2 == 0 else PS["D"][:, 512:512 + w]
            nc.tensor.matmul(pa, onesb_row, ABH[0][:, o:o + w], start=True, stop=False)
            nc.tensor.matmul(pa, onesb_row, ABH[1][:, o:o + w], start=False, stop=True)
            nc.tensor.matmul(pb, onesb_row, ABH[2][:, o:o + w], start=True, stop=False)
            s_bc.inc(nc.tensor.matmul(pb, onesb_row, ABH[3][:, o:o + w], start=False, stop=True))
            ctx.sems["A_ap"].wge(te, tt * 8 + 8)

    @blk.vector
    def _(v):
        for tt in range(NT5):
            o, w = tt * 512, tile_w(tt)
            ctx.sems["A_bc"].wge(v, tt + 1)
            pa = PS["C"][:, 0:w] if tt % 2 == 0 else PS["C"][:, 512:512 + w]
            pb = PS["D"][:, 0:w] if tt % 2 == 0 else PS["D"][:, 512:512 + w]
            for dc in range(8):
                v.tensor_tensor(out=TMP[:, 0:w], in0=xf_ap(ctx, dc, o, w), in1=pa, op=OP.mult)
                v.drain()
                s_ap.inc(v.tensor_tensor(out=XN1[:, dc * NROW + o:dc * NROW + o + w],
                                         in0=TMP[:, 0:w], in1=pb, op=OP.add))
                v.drain()

    ctx.tap("xn1", XN1)


def phaseB1(ctx, dram, PS, st):
    nc, blk = ctx.nc, ctx.blk
    SM, ones_col = st["SM"], st["ones_col"]
    XN1 = st["XN1"]
    Q = ctx.at(R2_KB, NH * NROW, BF16)
    V = ctx.at(R2_KB + 35, NRC * NH * 128, BF16)
    EW = ctx.at(R2_KB + 70, 8 * NH * 128, BF16)
    MNW1 = ctx.at(R2_KB + 86, 4096, BF16)
    MNW2 = ctx.at(R2_KB + 90, 4096, BF16)
    GATH = ctx.at(SM_KB + 3, 128)
    st["Q"], st["V"] = Q, V
    sigb = ctx.at(SM_KB + 2, 256, BF16)[:, 128:136]
    mn_b1 = SM[:, 64:68]; mn_b2 = SM[:, 72:80]; base_m = SM[:, 80:88]
    sig = SM[:, 8:16]; metric = SM[:, 16:24]
    st["sig"], st["metric"] = sig, metric
    partials = SM[:, 40:56]
    bmt = SM[:, 96:104]
    hidb = ctx.at(SM_KB + 2, 256, BF16)[:, 136:140]

    s_w = ctx.sem("B_w")
    s_qmm = ctx.sem("B_qmm")
    s_qe = ctx.sem("B_qe")
    s_vmm = ctx.sem("B_vmm")
    s_ve = ctx.sem("B_ve")
    s_part = ctx.sem("B_part")
    s_cc = ctx.sem("B_cc", pin=True)
    s_met = ctx.sem("B_met")

    @blk.gpsimd
    def _(g):
        ctx.dma(g, "gpsimd", EW.rearrange("p (a b) -> p a b", a=8),
                dram["eqT"].rearrange("(a p) c -> p a c", p=128))
        ctx.dma(g, "gpsimd", MNW1.rearrange("p (a b) -> p a b", a=8), dram["mn_w1"].rearrange("(a p) c -> p a c", p=128))
        g.wait_ge(ctx.dsem["gpsimd"].h, ctx.dsem["gpsimd"].v)
        ctx.fence(g, "gpsimd", s_w)

    @blk.sync
    def _(e):
        ctx.dma(e, "sync", mn_b1, dram["mn_b1"])
        ctx.dma(e, "sync", mn_b2, dram["mn_b2"])
        ctx.dma(e, "sync", base_m, dram["base_m"])
        ctx.dma(e, "sync", bmt, dram["bmt"])

    banks4 = [PS["A"][:, 0:512], PS["A"][:, 512:1024], PS["B"][:, 0:512], PS["B"][:, 512:1024]]
    @blk.tensor
    def _(te):
        s_w.wait(te)
        k = 0
        for h in range(NH):
            for tt in range(NT5):
                o, w = tt * 512, tile_w(tt)
                if k >= 4:
                    ctx.sems["B_qe"].wge(te, k - 3)
                bank = banks4[k % 4]
                for dc in range(8):
                    last = nc.tensor.matmul(bank[:, 0:w], EW[:, (dc * NH + h) * 128:(dc * NH + h) * 128 + 128],
                                            XN1[:, dc * NROW + o:dc * NROW + o + w],
                                            start=(dc == 0), stop=(dc == 7))
                s_qmm.inc(last)
                k += 1

    @blk.scalar
    def _(s):
        k = 0
        for h in range(NH):
            for tt in range(NT5):
                o, w = tt * 512, tile_w(tt)
                ctx.sems["B_qmm"].wge(s, k + 1)
                s_qe.inc(s.activation(out=Q[:, h * NROW + o:h * NROW + o + w], in_=banks4[k % 4][:, 0:w], func=AF.Relu))
                k += 1

    @blk.gpsimd
    def _(g):
        ctx.sems["B_qmm"].wge(g, NH * NT5)
        ctx.dma(g, "gpsimd", EW.rearrange("p (a b) -> p a b", a=8),
                dram["evT"].rearrange("(a p) c -> p a c", p=128))
        g.wait_ge(ctx.dsem["gpsimd"].h, ctx.dsem["gpsimd"].v)
        ctx.fence(g, "gpsimd", s_w)

    @blk.tensor
    def _(te):
        ctx.sems["B_w"].wge(te, 2)
        ctx.sems["B_qe"].wge(te, NH * NT5)
        for rc in range(NRC):
            o = rc * 128
            if rc >= 2:
                ctx.sems["B_ve"].wge(te, rc - 1)
            bank = PS["C"] if rc % 2 == 0 else PS["D"]
            for half in range(2):
                for dc in range(8):
                    last = nc.tensor.matmul(bank[:, half * 512:half * 512 + 512],
                                            XN1[:, dc * NROW + o:dc * NROW + o + 128],
                                            EW[:, dc * 1024 + half * 512:dc * 1024 + half * 512 + 512],
                                            start=(dc == 0), stop=(dc == 7))
            s_vmm.inc(last)

    @blk.scalar
    def _(s):
        for rc in range(NRC):
            ctx.sems["B_vmm"].wge(s, rc + 1)
            bank = PS["C"] if rc % 2 == 0 else PS["D"]
            s_ve.inc(s.activation(out=V[:, rc * 1024:(rc + 1) * 1024], in_=bank[:, 0:1024], func=AF.Relu))

    @blk.vector
    def _(v):
        ctx.sems["B_qe"].wge(v, NH * NT5)
        for h in range(NH):
            v.tensor_reduce(partials[:, h:h + 1], Q[:, h * NROW + HALO:h * NROW + NROW], axis=AX.X, op=OP.add)
        for h in range(NH):
            qs = Q[:, h * NROW + HALO:h * NROW + NROW].rearrange("p (a b) -> p a b", b=8)[:, :, 0:1]
            v.tensor_reduce(partials[:, 8 + h:9 + h], qs, axis=AX.XY, op=OP.add)
        ctx.fence(v, "vector", s_part)

    @blk.gpsimd
    def _(g):
        s_part.wait(g)
        dst = dram["cc_in1"].rearrange("a (p f) -> (a p) f", p=128)
        ctx.dma(g, "gpsimd", dst, partials)
        g.wait_ge(ctx.dsem["gpsimd"].h, ctx.dsem["gpsimd"].v)
        s_cc.inc(g.collective_compute("AllGather", OP.bypass, replica_groups=[[0, 1, 2, 3, 4, 5, 6, 7]],
                                      ins=[dram["cc_in1"][:]], outs=[dram["cc_out1"][:]]))
        ctx.sems["B_cc"].wge(g, 1)
        for r in range(8):
            src = dram["cc_out1"][r:r + 1, :].rearrange("a (p f) -> (a p) f", p=128)
            ctx.dma(g, "gpsimd", GATH[:, r * 16:(r + 1) * 16], src)
        g.wait_ge(ctx.dsem["gpsimd"].h, ctx.dsem["gpsimd"].v)
        ctx.fence(g, "gpsimd", s_cc)

    @blk.vector
    def _(v):
        ctx.sems["B_cc"].wge(v, 2)
        ctx.dsem["sync"].wait(v)
        for r in range(8):
            v.tensor_scalar(out=GATH[:, r * 16:(r + 1) * 16], in0=GATH[:, r * 16:(r + 1) * 16],
                            scalar1=bmt[:, r:r + 1], scalar2=None, op0=OP.mult)
        for r in range(1, 8):
            v.drain()
            v.tensor_tensor(out=GATH[:, 0:16], in0=GATH[:, 0:16], in1=GATH[:, r * 16:(r + 1) * 16], op=OP.add)
        v.drain()
        v.tensor_scalar(out=GATH[:, 0:8], in0=GATH[:, 0:8], scalar1=1.0 / T, scalar2=None, op0=OP.mult)
        v.tensor_scalar(out=GATH[:, 8:16], in0=GATH[:, 8:16], scalar1=0.5 / 1024.0, scalar2=None, op0=OP.mult)
        v.drain()
        v.tensor_tensor(out=sig, in0=GATH[:, 0:8], in1=GATH[:, 8:16], op=OP.add)
        v.drain()
        v.tensor_copy(sigb, sig)
        ctx.fence(v, "vector", s_cc)

    @blk.tensor
    def _(te):
        ctx.sems["B_cc"].wge(te, 3)
        for hc in range(4):
            for dc in range(8):
                last = nc.tensor.matmul(PS["A"][:, hc:hc + 1],
                                        MNW1[:, dc * 512 + hc * 128:dc * 512 + hc * 128 + 128],
                                        sigb[:, dc:dc + 1], start=(dc == 0), stop=(dc == 7))
        s_met.inc(last)

    @blk.scalar
    def _(s):
        s_met.wait(s)
        ctx.dsem["sync"].wait(s)
        for hc in range(4):
            if hc == 0:
                act_preload(s, "gelu_and_others")
            s.activation(out=hidb[:, hc:hc + 1], in_=PS["A"][:, hc:hc + 1], func=AF.Gelu,
                         bias=mn_b1[:, hc:hc + 1], scale=1.0)
        ctx.fence(s, "scalar", s_met)

    @blk.gpsimd
    def _(g):
        s_met.wait(g)
        ctx.dma(g, "gpsimd", MNW2.rearrange("p (a b) -> p a b", a=4), dram["mn_w2"].rearrange("(a p) c -> p a c", p=128))
        g.wait_ge(ctx.dsem["gpsimd"].h, ctx.dsem["gpsimd"].v)
        ctx.fence(g, "gpsimd", s_w)

    @blk.tensor
    def _(te):
        ctx.sems["B_met"].wge(te, 2)
        ctx.sems["B_w"].wge(te, 3)
        for oc in range(8):
            for hc in range(4):
                last = nc.tensor.matmul(PS["B"][:, oc:oc + 1],
                                        MNW2[:, hc * 1024 + oc * 128:hc * 1024 + oc * 128 + 128],
                                        hidb[:, hc:hc + 1], start=(hc == 0), stop=(hc == 3))
        s_met.inc(last)

    @blk.vector
    def _(v):
        ctx.sems["B_met"].wge(v, 3)
        MW = SM[:, 160:168]
        for oc in range(8):
            v.tensor_scalar(out=MW[:, oc:oc + 1], in0=PS["B"][:, oc:oc + 1],
                            scalar1=mn_b2[:, oc:oc + 1], scalar2=0.1, op0=OP.add, op1=OP.mult)
        v.drain()
        v.tensor_tensor(out=MW, in0=MW, in1=base_m, op=OP.add)
        ctx.fence(v, "vector", s_met)

    @blk.scalar
    def _(s):
        ctx.sems["B_met"].wge(s, 4)
        act_preload(s, "natural_log_exp_and_others")
        s.activation(out=SM[:, 168:176], in_=SM[:, 160:168], func=AF.Exp)
        s.drain()
        s.activation(out=metric, in_=SM[:, 168:176], func=AF.Ln, bias=1.0, scale=1.0)
        ctx.fence(s, "scalar", s_met)

    ctx.tap("q", Q)
    ctx.tap("v", V)
    ctx.tap("sig", sig)
    ctx.tap("metric", metric)


def phaseB2(ctx, dram, PS, st):
    nc, blk = ctx.nc, ctx.blk
    SM, metric = st["SM"], st["metric"]
    onesb_row, onesb_col = st["onesb_row"], st["onesb_col"]
    Q, V = st["Q"], st["V"]
    gflat = dram["g_hbm"]
    s_hbm = dram["s_hbm"]
    w2flat = dram["w2_hbm"]
    misc = dram["misc_hbm"]
    misc2 = dram["misc2_hbm"]
    GREG = 32896
    SREG = NROW + 192

    OS = ctx.at(R2_KB + 70, NH * 512, BF16)
    DEC = ctx.at(R2_KB + 78, 8 * 512, BF16)
    ALB = ctx.at(R2_KB + 86, NTOK, BF16)
    HM = ctx.at(R2_KB + 90, NH * 128, BF16)
    ZB = ctx.at(R2_KB + 92, 256, BF16)
    HBU = ctx.at(R2_KB + 93, 8 * 128, BF16)
    HBV = ctx.at(R2_KB + 95, 8 * 128, BF16)
    GSB = [ctx.at(R2_KB + 72 + i, 256) for i in range(4)]
    SRW = ctx.at(R2_KB + 70, 512)[0:1, :]

    s_hm = ctx.sem("C_hm")
    s_hme = ctx.sem("C_hme")
    s_qmm2 = ctx.sem("C_qmm2")
    s_qn = ctx.sem("C_qn")
    s_al = ctx.sem("C_al")
    s_kt = ctx.sem("C_kt")
    s_pr = ctx.sem("C_pr")
    s_smm = ctx.sem("C_smm")
    s_srow = ctx.sem("C_srow")
    s_gmm = ctx.sem("C_gmm")
    s_gev = ctx.sem("C_gev")
    s_mw = ctx.sem("C_mw", pin=True)
    s_sw = ctx.sem("C_sw")
    s_gst = ctx.sem("C_gst")

    KT = ctx.at(R1_KB, NH * NROW, BF16)
    SQB = ctx.at(R2_KB + 72, 1024, BF16)
    QNS = ctx.at(R2_KB + 74, NTOK)
    QW = ctx.at(R2_KB + 82, 1024)
    PRD = ctx.at(R2_KB + 78, NROW, BF16)

    @blk.gpsimd
    def _(g):
        ctx.dma(g, "gpsimd", HBU[0:32, :].rearrange("p (a b) -> p a b", a=8),
                dram["hebUT"].rearrange("(a p) c -> p a c", p=32))
        ctx.dma(g, "gpsimd", HBV[0:32, :].rearrange("p (a b) -> p a b", a=8),
                dram["hebV"].rearrange("(a p) c -> p a c", p=32))
        g.wait_ge(ctx.dsem["gpsimd"].h, ctx.dsem["gpsimd"].v)
        ctx.fence(g, "gpsimd", s_hm)

    @blk.vector
    def _(v):
        v.memset(ZB, 0.0)
        for h in range(NH):
            v.tensor_scalar(out=KT[:, h * NROW:(h + 1) * NROW], in0=Q[:, h * NROW:(h + 1) * NROW],
                            scalar1=metric[:, h:h + 1], scalar2=None, op0=OP.mult)
        ctx.fence(v, "vector", s_kt)

    @blk.gpsimd
    def _(g):
        s_kt.wait(g)
        for i in range(8):
            ctx.dma(g, "gpsimd", bass.AP(w2flat.tensor, i * 32768, [[256, 128], [1, 256]]), ZB)
        ctx.fence(g, "gpsimd", s_kt)

    @blk.tensor
    def _(te):
        s_hm.wait(te)
        for h in range(NH):
            if h >= 2:
                ctx.sems["C_hme"].wge(te, h - 1)
            bank = PS["A"][:, 0:128] if h % 2 == 0 else PS["A"][:, 512:640]
            s_hm.inc(nc.tensor.matmul(bank, HBU[0:32, h * 128:(h + 1) * 128],
                                      HBV[0:32, h * 128:(h + 1) * 128], start=True, stop=True))

    @blk.vector
    def _(v):
        for h in range(NH):
            ctx.sems["C_hm"].wge(v, 1 + h + 1)
            bank = PS["A"][:, 0:128] if h % 2 == 0 else PS["A"][:, 512:640]
            s_hme.inc(v.tensor_copy(HM[:, h * 128:(h + 1) * 128], bank))

    @blk.scalar
    def _(s):
        for h in range(NH):
            for half in range(2):
                ctx.sems["C_qmm2"].wge(s, 2 * (2 * h + half))
                s.drain()
                s_qn.inc(s.activation(out=SQB, in_=Q[:, h * NROW + HALO + half * 1024:
                                                     h * NROW + HALO + half * 1024 + 1024], func=AF.Square))

    @blk.tensor
    def _(te):
        ctx.sems["C_hm"].wge(te, 1 + NH)
        for k in range(2 * NH):
            ctx.sems["C_qn"].wge(te, k + 1)
            ctx.sems["C_srow"].wge(te, 2 * k)
            for j in range(2):
                bank = PS["B"] if j == 0 else PS["C"]
                s_qmm2.inc(nc.tensor.matmul(bank[0:1, 0:512],
                                            onesb_col, SQB[:, j * 512:(j + 1) * 512],
                                            start=True, stop=True))

    @blk.vector
    def _(v):
        for k in range(2 * NH):
            ctx.sems["C_qmm2"].wge(v, 2 * (k + 1))
            for j in range(2):
                bank = PS["B"] if j == 0 else PS["C"]
                ctx.sems["C_mw"].wge(v, 16 * (2 * k + j))
                s_srow.inc(v.tensor_copy(SRW, bank[0:1, 0:512]))

    @blk.gpsimd
    def _(g):
        for k in range(2 * NH):
            for j in range(2):
                ctx.sems["C_srow"].wge(g, 2 * k + j + 1)
                ctx.sems["C_mw"].wge(g, 16 * (2 * k + j))
                s_mw.inc(g.dma_start(out=bass.AP(misc.tensor, k * 1024 + j * 512, [[512, 1], [1, 512]]),
                                     in_=SRW), 16)
        ctx.sems["C_mw"].wge(g, 16 * 32)
        ctx.dma(g, "gpsimd", QNS[0:8, :], misc.rearrange("(a b) -> a b", b=2048)[0:8, :])
        g.wait_ge(ctx.dsem["gpsimd"].h, ctx.dsem["gpsimd"].v)
        ctx.fence(g, "gpsimd", s_qn)

    @blk.scalar
    def _(s):
        for half in range(2):
            ctx.sems["C_qn"].wge(s, 2 * NH + 1 + half)
            if half == 1:
                ctx.sems["C_al"].wge(s, 1)
            s.drain()
            if half == 0:
                act_preload(s, "sqrt_and_others")
            s_qn.inc(s.activation(out=QW[0:8, :], in_=QNS[0:8, half * 1024:(half + 1) * 1024], func=AF.Sqrt))

    @blk.vector
    def _(v):
        for half in range(2):
            ctx.sems["C_qn"].wge(v, 18 + half)
            sl = slice(half * 1024, (half + 1) * 1024)
            v.drain()
            v.tensor_scalar(out=QW[0:8, :], in0=QW[0:8, :], scalar1=1e-12, scalar2=None, op0=OP.max)
            v.drain()
            v.reciprocal(QW[0:8, :], QW[0:8, :])
            v.drain()
            v.tensor_scalar(out=QW[0:8, :], in0=QW[0:8, :], scalar1=0.1, scalar2=None, op0=OP.mult)
            v.drain()
            v.tensor_scalar(out=QNS[0:8, sl], in0=QNS[0:8, sl], scalar1=0.04, scalar2=None, op0=OP.is_gt)
            v.drain()
            v.tensor_tensor(out=QNS[0:8, sl], in0=QNS[0:8, sl], in1=QW[0:8, :], op=OP.mult)
            v.drain()
            v.tensor_copy(ALB[0:8, sl], QNS[0:8, sl])
            ctx.fence(v, "vector", s_al)
    

    @blk.gpsimd
    def _(g):
        ctx.sems["C_al"].wge(g, 2)
        ctx.dma(g, "gpsimd", misc2.rearrange("(a b) -> a b", b=2048)[0:8, :], ALB[0:8, :])
        g.wait_ge(ctx.dsem["gpsimd"].h, ctx.dsem["gpsimd"].v)
        ctx.fence(g, "gpsimd", s_al)


    @blk.tensor
    def _(te):
        ctx.sems["C_qmm2"].wge(te, 2 * 2 * NH)
        ctx.sems["C_al"].wge(te, 2)
        for h in range(NH):
            ctx.sems["C_pr"].wge(te, h + 1)
            ctx.sems["C_srow"].wge(te, 32 + 5 * h)
            for j in range(5):
                w = min(512, NROW - j * 512)
                bank = [PS["B"], PS["B"], PS["C"], PS["C"], PS["D"]][j]
                col = (j % 2) * 512 if j < 4 else 0
                s_smm.inc(nc.tensor.matmul(bank[0:1, col:col + w], onesb_col,
                                           PRD[:, j * 512:j * 512 + w], start=True, stop=True))

    @blk.vector
    def _(v):
        ctx.sems["C_mw"].wge(v, 16 * 32)
        for h in range(NH + 1):
            ctx.sems["C_smm"].wge(v, 5 * h)
            if h >= 1:
                for j in range(5):
                    w = min(512, NROW - j * 512)
                    bank = [PS["B"], PS["B"], PS["C"], PS["C"], PS["D"]][j]
                    col = (j % 2) * 512 if j < 4 else 0
                    ctx.sems["C_sw"].wge(v, 16 * (5 * (h - 1) + j))
                    s_srow.inc(v.tensor_copy(SRW[:, 0:w], bank[0:1, col:col + w]))
            if h < NH:
                v.drain()
                v.tensor_tensor(out=PRD, in0=Q[:, h * NROW:(h + 1) * NROW],
                                in1=KT[:, h * NROW:(h + 1) * NROW], op=OP.mult)
                ctx.fence(v, "vector", s_pr)

    @blk.sync
    def _(e):
        for h in range(NH):
            for j in range(5):
                w = min(512, NROW - j * 512)
                ctx.sems["C_srow"].wge(e, 32 + 5 * h + j + 1)
                ctx.sems["C_sw"].wge(e, 16 * (5 * h + j))
                s_sw.inc(e.dma_start(out=s_hbm[h:h + 1, j * 512:j * 512 + w], in_=SRW[:, 0:w]), 16)
        ctx.sems["C_sw"].wge(e, 16 * 40)

    ctx.barrier()

    banks4 = [PS["A"][:, 0:256], PS["A"][:, 512:768], PS["B"][:, 0:256], PS["B"][:, 512:768]]
    @blk.tensor
    def _(te):
        k = 0
        for h in range(NH):
            for qt in range(1, NRC):
                if k >= 4:
                    ctx.sems["C_gev"].wge(te, k - 3)
                key0 = (qt - 1) * 128
                s_gmm.inc(nc.tensor.matmul(banks4[k % 4], Q[:, h * NROW + qt * 128:h * NROW + qt * 128 + 128],
                                           KT[:, h * NROW + key0:h * NROW + key0 + 256],
                                           start=True, stop=True))
                k += 1

    @blk.vector
    def _(v):
        for k in range(NH * NQT):
            ctx.sems["C_gmm"].wge(v, k + 1)
            if k % 4 == 0 and k >= 4:
                ctx.sems["C_gst"].wge(v, 64 * (k // 4))
            s_gev.inc(v.tensor_copy(GSB[k % 4], banks4[k % 4]))

    @blk.sync
    def _(e):
        for grp in range(NH * NQT // 4):
            ctx.sems["C_gev"].wge(e, 4 * grp + 4)
            for i in range(4):
                k = 4 * grp + i
                idx = k
                s_gst.inc(e.dma_start(out=bass.AP(gflat.tensor, idx * GREG, [[256, 128], [1, 256]]),
                                      in_=GSB[k % 4]), 16)
            ctx.sems["C_gst"].wge(e, 64 * (grp + 1))

    ctx.barrier()

    WC = ctx.at(R1_KB, 512)
    SU = ctx.at(R1_KB + 2, 512)
    STC = ctx.at(R1_KB + 4, 16)
    NEG = ctx.at(R1_KB + 6, 512)
    SELA = ctx.at(R1_KB + 8, 512)
    SELB = ctx.at(R1_KB + 10, 512)
    MX = ctx.at(R1_KB + 12, 8)
    MASK = ctx.at(R1_KB + 13, 512)
    NEGC = ctx.at(R1_KB + 15, 512)
    DSTT = ctx.at(R1_KB + 17, 512)
    WEXP = ctx.at(R1_KB + 19, 512)
    WW = ctx.at(R1_KB + 21, 512)
    RS = ctx.at(R1_KB + 23, 24)
    W2R = ctx.at(R1_KB + 24, 2048, BF16)
    ARW = ctx.at(R1_KB + 28, 1024, BF16)[0:1, :]
    QSC = ctx.at(R1_KB + 30, 128, BF16)
    WWB = ctx.at(R1_KB + 31, 512, BF16)

    if "W2T" not in st:
        st["W2T"] = [ctx.stack.enter_context(
            nc.sbuf_tensor(f"w2t{i}", [128, 128], BF16, is_transpose_dest=True)) for i in range(2)]
    W2T = st["W2T"]

    s_wld = ctx.sem("C_wld")
    s_sel = ctx.sem("C_sel")
    s_dst = ctx.sem("C_dst")
    s_wn = ctx.sem("C_wn")
    s_wst = ctx.sem("C_wst")
    s_w2r = ctx.sem("C_w2r")
    s_tp = ctx.sem("C_tp")
    s_abc = ctx.sem("C_abc")
    s_qsc = ctx.sem("C_qsc")
    s_ctx2 = ctx.sem("C_ctx2")
    s_os = ctx.sem("C_os")
    s_dmm = ctx.sem("C_dmm")
    s_dec = ctx.sem("C_dec")
    s_dw = ctx.sem("C_dw")

    NQ = NRC - 1

    @blk.sync
    def _(e):
        for qt in range(1, NRC):
            if qt >= 2:
                ctx.sems["C_os"].wge(e, (qt - 1) * NH)
            for h in range(NH):
                idx = h * NQT + (qt - 1)
                ctx.dma(e, "sync", WC[:, h * 64:(h + 1) * 64],
                        bass.AP(gflat.tensor, idx * GREG + 65, [[257, 128], [1, 64]]))
                ctx.dma(e, "sync", SU[:, h * 64:(h + 1) * 64],
                        bass.AP(s_hbm.tensor, h * SREG + qt * 128 - 63, [[1, 128], [1, 64]]))
                ctx.dma(e, "sync", STC[:, h:h + 1],
                        bass.AP(s_hbm.tensor, h * SREG + qt * 128, [[1, 128], [1, 1]]))
            e.wait_ge(ctx.dsem["sync"].h, ctx.dsem["sync"].v)
            ctx.fence(e, "sync", s_wld)
            ctx.sems["C_wn"].wge(e, qt)
            for h in range(NH):
                ctx.dma(e, "sync", bass.AP(w2flat.tensor, h * 32768 + 65, [[257, 128], [1, 64]]),
                        WWB[:, h * 64:(h + 1) * 64])
            e.wait_ge(ctx.dsem["sync"].h, ctx.dsem["sync"].v)
            ctx.sems["C_wst"].wge(e, qt)
            ctx.sems["C_tp"].wge(e, (qt - 1) * NH)
            for h in range(NH):
                ctx.dma(e, "sync", W2R[:, h * 256:(h + 1) * 256],
                        bass.AP(w2flat.tensor, h * 32768, [[256, 128], [1, 256]]))
            e.wait_ge(ctx.dsem["sync"].h, ctx.dsem["sync"].v)
            ctx.fence(e, "sync", s_w2r)

    @blk.vector
    def _(v):
        for qt in range(1, NRC):
            ctx.sems["C_wld"].wge(v, qt)
            v.scalar_tensor_tensor(out=NEG, in0=WC, scalar=2.0, in1=SU,
                                   op0=OP.mult, op1=OP.subtract)
            v.drain()
            for h in range(NH):
                v.tensor_scalar(out=NEG[:, h * 64:(h + 1) * 64], in0=NEG[:, h * 64:(h + 1) * 64],
                                scalar1=STC[:, h:h + 1], scalar2=None, op0=OP.subtract)
            v.drain()
            for h in range(NH):
                cur = NEG[:, h * 64:(h + 1) * 64]
                for r in range(4):
                    nxt = (SELA if r % 2 == 0 else SELB)[:, h * 64:(h + 1) * 64]
                    v.max(out=MX, in_=cur)
                    v.tensor_copy(nxt, cur)
                    v.drain()
                    v.match_replace(out=nxt, in_to_replace=MX, in_values=nxt, imm_value=-2.0e38)
                    v.drain()
                    cur = nxt
            v.tensor_scalar(out=MASK, in0=SELB, scalar1=-1.0e38, scalar2=None, op0=OP.is_le)
            v.tensor_scalar(out=NEGC, in0=NEG, scalar1=0.0, scalar2=None, op0=OP.min)
            ctx.fence(v, "vector", s_sel)
            ctx.sems["C_dst"].wge(v, qt)
            v.tensor_tensor(out=WW, in0=WEXP, in1=MASK, op=OP.mult)
            v.drain()
            v.tensor_reduce(RS[:, 0:8], WW.rearrange("p (a b) -> p a b", b=64), axis=AX.X, op=OP.add)
            v.drain()
            v.tensor_scalar(out=RS[:, 8:16], in0=RS[:, 0:8], scalar1=1e-8, scalar2=None, op0=OP.add)
            v.drain()
            v.reciprocal(RS[:, 16:24], RS[:, 8:16])
            v.drain()
            for h in range(NH):
                v.tensor_scalar(out=WW[:, h * 64:(h + 1) * 64], in0=WW[:, h * 64:(h + 1) * 64],
                                scalar1=RS[:, 16 + h:17 + h], scalar2=None, op0=OP.mult)
            v.drain()
            v.tensor_copy(WWB, WW)
            ctx.fence(v, "vector", s_wn)
            for h in range(NH):
                ctx.sems["C_abc"].wge(v, (qt - 1) * NH + h + 1)
                v.tensor_tensor(out=QSC, in0=Q[:, h * NROW + qt * 128:h * NROW + qt * 128 + 128],
                                in1=PS["B"][:, 0:128], op=OP.mult)
                ctx.fence(v, "vector", s_qsc)
                ctx.sems["C_ctx2"].wge(v, (qt - 1) * NH + h + 1)
                s_os.inc(v.tensor_tensor(out=OS[:, h * 512 + ((qt - 1) % 4) * 128:h * 512 + ((qt - 1) % 4) * 128 + 128],
                                         in0=Q[:, h * NROW + qt * 128:h * NROW + qt * 128 + 128],
                                         in1=PS["A"][:, 0:128], op=OP.mult))
            if qt % 4 == 0:
                tg = qt // 4 - 1
                for p in range(2):
                    ctx.sems["C_dmm"].wge(v, tg * 8 + p * 4 + 4)
                    for oc in range(4):
                        bank = [PS["C"][:, 0:512], PS["C"][:, 512:1024], PS["D"][:, 0:512], PS["D"][:, 512:1024]][oc]
                        ocg = p * 4 + oc
                        s_dec.inc(v.tensor_tensor(out=xf_ap(ctx, ocg, HALO + tg * 512, 512),
                                                  in0=xf_ap(ctx, ocg, HALO + tg * 512, 512),
                                                  in1=bank, op=OP.add))
                v.drain()

    @blk.scalar
    def _(s):
        for qt in range(1, NRC):
            ctx.sems["C_sel"].wge(s, qt)
            s.drain()
            act_preload(s, "sqrt_and_others")
            s.activation(out=DSTT, in_=NEGC, func=AF.Sqrt, bias=SM[:, 1:2], scale=-1.0)
            s.drain()
            act_preload(s, "natural_log_exp_and_others")
            s.activation(out=WEXP, in_=DSTT, func=AF.Exp, scale=-1.0)
            ctx.fence(s, "scalar", s_dst)
            ctx.sems["C_w2r"].wge(s, qt)
            for h in range(NH):
                ctx.dsem["scalar"].inc(s.dma_start(out=W2T[0][:, :], in_=W2R[:, h * 256:h * 256 + 128], transpose=True), 16)
                ctx.dsem["scalar"].inc(s.dma_start(out=W2T[1][:, :], in_=W2R[:, h * 256 + 128:h * 256 + 256], transpose=True), 16)
                s.wait_ge(ctx.dsem["scalar"].h, ctx.dsem["scalar"].v)
                ctx.fence(s, "scalar", s_tp)
                ctx.sems["C_ctx2"].wge(s, (qt - 1) * NH + h + 1)

    @blk.gpsimd
    def _(g):
        for qt in range(1, NRC):
            if (qt - 1) % 4 == 0:
                tg = (qt - 1) // 4
                ctx.sems["C_dmm"].wge(g, tg * 8)
                ctx.dma(g, "gpsimd", DEC.rearrange("p (a b) -> p a b", a=8),
                        dram["decT"].rearrange("(a p) c -> p a c", p=128)[:, :, 0:512])
                g.wait_ge(ctx.dsem["gpsimd"].h, ctx.dsem["gpsimd"].v)
                ctx.fence(g, "gpsimd", s_dw)
            ctx.sems["C_wn"].wge(g, qt)
            ctx.dma(g, "gpsimd", ARW.rearrange("a (b c) -> a b c", b=8),
                    bass.AP(misc2.tensor, (qt - 1) * 128, [[1, 1], [2048, 8], [1, 128]]))
            g.wait_ge(ctx.dsem["gpsimd"].h, ctx.dsem["gpsimd"].v)
            ctx.fence(g, "gpsimd", s_wst)
            if qt % 4 == 0:
                tg = qt // 4 - 1
                ctx.sems["C_dmm"].wge(g, tg * 8 + 4)
                ctx.dma(g, "gpsimd", DEC.rearrange("p (a b) -> p a b", a=8),
                        dram["decT"].rearrange("(a p) c -> p a c", p=128)[:, :, 512:1024])
                g.wait_ge(ctx.dsem["gpsimd"].h, ctx.dsem["gpsimd"].v)
                ctx.fence(g, "gpsimd", s_dw)

    @blk.tensor
    def _(te):
        for qt in range(1, NRC):
            for h in range(NH):
                ctx.sems["C_tp"].wge(te, (qt - 1) * NH + h + 1)
                if (qt - 1) * NH + h >= 1:
                    ctx.sems["C_os"].wge(te, (qt - 1) * NH + h)
                s_abc.inc(nc.tensor.matmul(PS["B"][:, 0:128], onesb_row,
                                           ARW[:, h * 128:(h + 1) * 128], start=True, stop=True))
                nc.tensor.matmul(PS["A"][:, 0:128], V[:, ((qt - 1) * NH + h) * 128:((qt - 1) * NH + h) * 128 + 128],
                                 W2T[0][:, :], start=True, stop=False)
                nc.tensor.matmul(PS["A"][:, 0:128], V[:, (qt * NH + h) * 128:(qt * NH + h) * 128 + 128],
                                 W2T[1][:, :], start=False, stop=False)
                ctx.sems["C_qsc"].wge(te, (qt - 1) * NH + h + 1)
                s_ctx2.inc(nc.tensor.matmul(PS["A"][:, 0:128], HM[:, h * 128:(h + 1) * 128],
                                            QSC, start=False, stop=True))
            if qt % 4 == 0:
                tg = qt // 4 - 1
                ctx.sems["C_os"].wge(te, qt * NH)
                for p in range(2):
                    ctx.sems["C_dw"].wge(te, tg * 2 + p + 1)
                    if tg * 8 + p * 4 >= 4:
                        ctx.sems["C_dec"].wge(te, tg * 8 + p * 4 - 4 + 4)
                    for oc in range(4):
                        bank = [PS["C"][:, 0:512], PS["C"][:, 512:1024], PS["D"][:, 0:512], PS["D"][:, 512:1024]][oc]
                        for hc in range(NH):
                            last = nc.tensor.matmul(bank, DEC[:, hc * 512 + oc * 128:hc * 512 + oc * 128 + 128],
                                                    OS[:, hc * 512:(hc + 1) * 512],
                                                    start=(hc == 0), stop=(hc == 7))
                        s_dmm.inc(last)




_BUILD_CACHE = {}

def _get_program():
    key = (PHASES, tuple(_TAPS))
    if key not in _BUILD_CACHE:
        _BUILD_CACHE[key] = build_program()
    return _BUILD_CACHE[key]


def _shard_inputs(inputs):
    f32 = lambda a: np.ascontiguousarray(np.asarray(a, np.float32))
    x = f32(inputs["L0_attn_x"])
    eq = f32(inputs["L0_attn_encoder_query"])
    ev = f32(inputs["L0_attn_encoder_value"])
    com = {
        "eqT": np.ascontiguousarray(eq.transpose(1, 0, 2).reshape(D, NH * 128)),
        "evT": np.ascontiguousarray(ev.transpose(1, 0, 2).reshape(D, NH * 128)),
        "mn_w1": f32(inputs["L0_attn_mn_w1"]),
        "mn_b1": np.ascontiguousarray(f32(inputs["L0_attn_mn_b1"]).reshape(4, 128).T),
        "mn_w2": f32(inputs["L0_attn_mn_w2"]),
        "mn_b2": np.ascontiguousarray(f32(inputs["L0_attn_mn_b2"]).reshape(8, 128).T),
        "base_m": np.ascontiguousarray(f32(inputs["L0_attn_base_metric"]).T),
        "decT": f32(inputs["L0_attn_decoder"]),
        "hebUT": np.ascontiguousarray(f32(inputs["L0_attn_hebbian_U"]).transpose(0, 2, 1).reshape(NH * 32, 128)),
        "hebV": f32(inputs["L0_attn_hebbian_V"]).reshape(NH * 32, 128),
        "f0w1T": np.ascontiguousarray(f32(inputs["L0_ffn_W1"]).T),
        "f0w2T": np.ascontiguousarray(f32(inputs["L0_ffn_W2"]).T),
        "f0b1": np.ascontiguousarray(f32(inputs["L0_ffn_b1"]).reshape(32, 128).T),
        "f0b2": np.ascontiguousarray(f32(inputs["L0_ffn_b2"]).reshape(8, 128).T),
        "w_in": f32(inputs["L1_attn_w_in"]),
        "b_in": np.ascontiguousarray(f32(inputs["L1_attn_b_in"]).reshape(16, 128).T),
        "wgx": f32(inputs["L1_attn_w_gx"]).reshape(8 * 256, 128),
        "bgx": np.ascontiguousarray(f32(inputs["L1_attn_b_gx"]).reshape(8, 128).T),
        "wga": f32(inputs["L1_attn_w_ga"]).reshape(8 * 256, 128),
        "bga": np.ascontiguousarray(f32(inputs["L1_attn_b_ga"]).reshape(8, 128).T),
        "arp": np.ascontiguousarray(f32(inputs["L1_attn_a_real_param"]).reshape(8, 128).T),
        "aim": np.ascontiguousarray(f32(inputs["L1_attn_a_imag_param"]).reshape(8, 128).T),
        "w_out": f32(inputs["L1_attn_w_out"]),
        "b_out": np.ascontiguousarray(f32(inputs["L1_attn_b_out"]).reshape(8, 128).T),
        "gate_w": f32(inputs["L1_ffn_gate_w"]),
        "gate_bt": np.ascontiguousarray(np.repeat(f32(inputs["L1_ffn_gate_b"]).reshape(1, 16), 128, axis=0)),
        "sh_wg": f32(inputs["L1_ffn_shared_wg"]),
        "sh_wu": f32(inputs["L1_ffn_shared_wu"]),
        "sh_wd": f32(inputs["L1_ffn_shared_wd"]),
        "ex_wg": f32(inputs["L1_ffn_experts_wg"]).reshape(NEXP * D, DI),
        "ex_wu": f32(inputs["L1_ffn_experts_wu"]).reshape(NEXP * D, DI),
        "ex_wd": f32(inputs["L1_ffn_experts_wd"]).reshape(NEXP * DI, D),
    }
    maps = []
    for c in range(NCORE):
        b, s = divmod(c, 4)
        start = s * NTOK
        if s == 0:
            halo = np.repeat(x[b, 0:1, :], HALO, axis=0)
        else:
            halo = x[b, start - HALO:start, :]
        xc = np.concatenate([halo, x[b, start:start + NTOK, :]], axis=0)
        bm = np.zeros((128, 8), np.float32)
        bm[:, b * 4:(b + 1) * 4] = 1.0
        rm = np.zeros((128, 8), np.float32)
        rm[:, c] = 1.0
        m = dict(com)
        m["x_fm"] = np.ascontiguousarray(xc.T)
        m["bmt"] = bm
        m["rmt"] = rm
        maps.append(m)
    return maps


def kernel(**inputs):
    nc = _get_program()
    in_maps = _shard_inputs(inputs)
    res = run_bass_kernel_spmd(nc, in_maps, core_ids=list(range(NCORE)))
    out = np.zeros((B, T, D), np.float32)
    for c in range(NCORE):
        b, s = divmod(c, 4)
        out[b, s * NTOK:(s + 1) * NTOK, :] = res.results[c]["y_out"].T
    kernel.last_results = res
    return out
